# version 03
# baseline (speedup 1.0000x reference)
import contextlib
import numpy as np
import ml_dtypes
import concourse.bass as bass
import concourse.mybir as mybir
from concourse.bass_utils import run_bass_kernel_spmd

F32 = mybir.dt.float32
BF16 = mybir.dt.bfloat16
AF = mybir.ActivationFunctionType
OP = mybir.AluOpType
AX = mybir.AxisListType

NCORE = 8
T = 2048
TH = 1024
THP = TH + 3
D = 4096
DC = 32
H = 128
G = 8
HPG = 16
NZ = 8
EPS = 1e-5

ROW_GATE = 0
ROW_HS = 8192
ROW_B = 16384
ROW_C = 17408
ROW_DT = 18432
NCHUNK = 145

BF = ml_dtypes.bfloat16


def _chunk_rows(ci):
    if ci == 0:
        return ROW_DT
    g = (ci - 1) // 18
    r = (ci - 1) % 18
    if r == 0:
        return ROW_B + 128 * g
    if r == 1:
        return ROW_C + 128 * g
    if r < 10:
        return ROW_HS + 1024 * g + 128 * (r - 2)
    return ROW_GATE + 1024 * g + 128 * (r - 10)


def CH_B(g):
    return 1 + 18 * g


def CH_C(g):
    return 2 + 18 * g


def CH_HS(g, q):
    return 3 + 18 * g + q


def CH_GATE(g, q):
    return 11 + 18 * g + q


def CONV_B(g):
    return 10 * g


def CONV_C(g):
    return 10 * g + 1


def CONV_HS(g, q):
    return 10 * g + 2 + q


def _conv_ch_of_chunk(j):
    g = j // 10
    r = j % 10
    if r == 0:
        rows = ROW_B + 128 * g
    elif r == 1:
        rows = ROW_C + 128 * g
    else:
        rows = ROW_HS + 1024 * g + 128 * (r - 2)
    if ROW_HS <= rows < ROW_B:
        cbase = rows - ROW_HS
    elif ROW_B <= rows < ROW_C:
        cbase = 8192 + (rows - ROW_B)
    else:
        cbase = 9216 + (rows - ROW_C)
    return np.arange(cbase, cbase + 128)


class Sem:
    def __init__(self, h):
        self.h = h
        self.n = 0


def build_nc(taps=()):
    nc = bass.Bass("TRN2", target_bir_lowering=False)
    taps = set(taps)

    def din(name, shape, dt=F32):
        return nc.dram_tensor(name, shape, dt, kind="ExternalInput").ap()

    x_tm = din("x_tm", [T + 3, D])
    x_fm = din("x_fm", [D, T])
    w_inT = din("w_inT", [NCHUNK, 128, DC, 128])
    convdiag = din("convdiag", [80, 128, 4, 128])
    convb = din("convb", [128, 80])
    w_outT = din("w_outT", [32, 128, 64, 128])
    kern = din("kern", [32, 128, 32, 128])
    loraA = din("loraA", [16, 128, 1024])
    loraAs = din("loraAs", [128, 1024])
    loraB = din("loraB", [16, 32, 4096])
    loraBs = din("loraBs", [32, 4096])
    gaterow = din("gaterow", [1, 16])
    drow = din("drow", [1, 128])
    alog = din("alog", [128, 1])
    dtb_d = din("dtb", [128, 1])
    normw_d = din("normw", [128, 64])
    ffnb_d = din("ffnb", [128, 32])
    selb_d = din("selb", [128, 8])
    c_ibf = din("c_ibf", [128, 128], BF16)
    c_tril = din("c_tril", [128, 128], BF16)
    c_ones = din("c_ones", [128, 128], BF16)
    c_negex8 = din("c_negex8", [8, 1024], BF16)
    c_selz8 = din("c_selz8", [8, 1024], BF16)
    cvals = din("cvals", [128, 8])

    out_fm = nc.dram_tensor("out_fm", [D, T], F32, kind="ExternalOutput").ap()

    def dint(name, shape, dt=BF16):
        return nc.dram_tensor(name, shape, dt, kind="Internal").ap()

    X_dram = dint("X_dram", [G, 16, 128, 1024])
    B_dram = dint("B_dram", [G, 128, T])
    C_dram = dint("C_dram", [G, 128, T])
    dt_dram = dint("dt_dram", [2, 5, 128, TH])
    gn_dram = dint("gn_dram", [64, 128, 16, 128])
    ffn_dram = dint("ffn_dram", [32, 128, 4, 512])
    ex_dram = dint("ex_dram", [128, 8192])
    gath_dram = dint("gath_dram", [NCORE * 128, 8192])
    woutbf = dint("woutbf", [32, 128, 8192])

    tap_aps = {}
    for tname, tshape, tdt in [
        ("tap_xn", [128, DC * THP], BF16),
        ("tap_acum", [128, TH], F32),
        ("tap_prevall", [128, 8192], BF16),
        ("tap_sin", [128, 8192], BF16),
        ("tap_g", [128, 1024], BF16),
        ("tap_ms", [128, 2048], BF16),
        ("tap_gn", [128, 1024], BF16),
        ("tap_ln", [128, 16], F32),
        ("tap_B", [128, 2048], BF16),
        ("tap_X", [128, 1024], BF16),
        ("tap_pz", [128, 2048], F32),
    ]:
        if tname in taps:
            tap_aps[tname] = nc.dram_tensor(tname, tshape, tdt, kind="ExternalOutput").ap()

    PROG = []

    def E(ek, fn):
        PROG.append((ek, fn))

    def EI(ek, fn, sem, amt=1):
        h = sem.h
        PROG.append((ek, lambda e, fn=fn, h=h, amt=amt: fn(e).then_inc(h, amt)))
        sem.n += amt
        return (sem, sem.n)

    def EW(ek, tok):
        if tok is None:
            return
        sem, val = tok
        h = sem.h
        PROG.append((ek, lambda e, h=h, v=val: e.wait_ge(h, v)))

    st = contextlib.ExitStack()
    DQ = Sem(st.enter_context(nc.semaphore("DQ")))
    SQ = Sem(st.enter_context(nc.semaphore("SQ")))
    PE = Sem(st.enter_context(nc.semaphore("PEs")))
    DV = Sem(st.enter_context(nc.semaphore("DVs")))
    AC = Sem(st.enter_context(nc.semaphore("ACs")))
    CC = Sem(st.enter_context(nc.semaphore("CCs")))

    def now(sem):
        return (sem, sem.n)

    def sb(name, shape, dt=BF16):
        return st.enter_context(nc.sbuf_tensor(name, shape, dt))

    IBF = sb("IBF", [128, 128])
    TRIL = sb("TRIL", [128, 128])
    ONES = sb("ONES", [128, 128])
    NEGEX8 = sb("NEGEX8", [8, 1024])
    SELZ8 = sb("SELZ8", [8, 1024])
    CONVB = sb("CONVB", [128, 80], F32)
    NORMW = sb("NORMW", [128, 64], F32)
    FFNB = sb("FFNB", [128, 32], F32)
    DTB = sb("DTB", [128, 1], F32)
    ACOL = sb("ACOL", [128, 1], F32)
    SELB = sb("SELB", [128, 8], F32)
    DROWB = sb("DROWB", [1, 128])
    GROWB = sb("GROWB", [1, 16])
    DBC = sb("DBC", [128, 128], F32)
    GBC = sb("GBC", [128, 16], F32)
    CVALS = sb("CVALS", [128, 8], F32)
    for _i, _v in enumerate([EPS, 1.0, -1.0, -0.5, 1.0 / 1024]):
        nc.const_aps.register(F32, _v, CVALS[:, _i:_i + 1])
    WBUF = sb("WBUF", [128, 2 * 4096])
    ARB = sb("ARB", [128, DC * THP])
    PREVALL = sb("PREVALL", [128, 8192])
    ZZ = sb("ZZ", [128, 128], F32)
    AEFF = [None]
    BST = [None]
    SINALL = [None]

    p512a = st.enter_context(nc.psum_tensor("p512a", [128, 512], F32))
    p512b = st.enter_context(nc.psum_tensor("p512b", [128, 512], F32))
    pBig1 = st.enter_context(nc.psum_tensor("pBig1", [128, 1024], F32))
    pBig2 = st.enter_context(nc.psum_tensor("pBig2", [128, 1024], F32))
    pT = st.enter_context(nc.psum_tensor("pT", [128, 1024], BF16))
    pSmall = st.enter_context(nc.psum_tensor("pSmall", [128, 131], F32))
    pSc = pSmall[:, 0:128]
    pMini = pSmall[:, 128:131]

    wslot_free = [None, None]
    p512_free = [None, None]
    pT_free = [None]
    pSc_free = [None]

    ln_state = {}

    def load_wchunk(ci, slot):
        EW('g', wslot_free[slot])
        src = w_inT[ci, :, :, :].rearrange("d k c -> d (k c)")
        dst = WBUF[:, slot * 4096:(slot + 1) * 4096]
        return EI('g', lambda g, s=src, d=dst: g.dma_start(d, s), DQ, 16)

    def p0():
        for dst, src in [(IBF, c_ibf), (TRIL, c_tril), (ONES, c_ones),
                         (NEGEX8, c_negex8), (SELZ8, c_selz8),
                         (CONVB, convb), (NORMW, normw_d), (FFNB, ffnb_d),
                         (DTB, dtb_d), (SELB, selb_d), (CVALS, cvals)]:
            EI('q', lambda q, d=dst, s=src: q.dma_start(d[:, :], s[:, :]), SQ, 16)
        EW('s', now(SQ))
        EW('v', now(SQ))
        t1 = EI('g', lambda g: g.dma_start(DROWB[:, :], drow[:, :]), DQ, 16)
        t2 = EI('g', lambda g: g.dma_start(GROWB[:, :], gaterow[:, :]), DQ, 16)
        talog = EI('q', lambda q: q.dma_start(ACOL[:, :], alog[:, :]), SQ, 16)
        EW('s', talog)
        ta = EI('s', lambda s: s.activation(ACOL[:, :], ACOL[:, :], AF.Exp), AC)
        EW('v', ta)
        EI('v', lambda v: v.tensor_scalar_mul(ACOL[:, :], ACOL[:, :], -1.0), DV)
        EI('v', lambda v: v.memset(ZZ[:, :], 0.0), DV)
        EW('t', now(SQ))
        EW('t', t1)
        EW('t', t2)
        tm1 = EI('t', lambda t: t.matmul(pSc[:, :], ONES[0:1, :], DROWB[:, :],
                                         start=True, stop=True), PE)
        EW('v', tm1)
        td = EI('v', lambda v: v.tensor_copy(DBC[:, :], pSc[:, :]), DV)
        EW('t', td)
        tm2 = EI('t', lambda t: t.matmul(pSc[:, 0:16], ONES[0:1, :], GROWB[:, :],
                                         start=True, stop=True), PE)
        EW('v', tm2)
        tg_ = EI('v', lambda v: v.tensor_copy(GBC[:, :], pSc[:, 0:16]), DV)
        pSc_free[0] = tg_

    def p0_lora():
        av = AEFF[0].rearrange("p (dc r2) -> p dc r2", r2=64)
        for n in range(16):
            slot = n % 2
            EW('g', wslot_free[slot])
            tl = EI('g', lambda g, n=n, sl=slot:
                    g.dma_start(WBUF[:, sl * 4096: sl * 4096 + 1024], loraA[n, :, :]), DQ, 16)
            EW('v', tl)
            src = WBUF[:, slot * 4096: slot * 4096 + 1024].rearrange("p (dc r) -> p dc r", r=32)
            if n == 0:
                tv = EI('v', lambda v, s=src:
                        v.tensor_scalar(av[:, :, 0:32], s, GBC[:, 0:1], None, OP.mult), DV)
            else:
                tv = EI('v', lambda v, s=src, n=n:
                        v.scalar_tensor_tensor(av[:, :, 0:32], s, GBC[:, n:n + 1],
                                               av[:, :, 0:32], OP.mult, OP.add), DV)
            wslot_free[slot] = tv
        EW('g', wslot_free[0])
        tl = EI('g', lambda g: g.dma_start(WBUF[:, 0:1024], loraAs[:, :]), DQ, 16)
        EW('v', tl)
        srcs = WBUF[:, 0:1024].rearrange("p (dc r) -> p dc r", r=32)
        tv = EI('v', lambda v, s=srcs: v.tensor_copy(av[:, :, 32:64], s), DV)
        wslot_free[0] = tv
        for n in range(16):
            slot = n % 2
            EW('g', wslot_free[slot])
            tl = EI('g', lambda g, n=n, sl=slot:
                    g.dma_start(WBUF[0:32, sl * 4096: sl * 4096 + 4096], loraB[n, :, :]), DQ, 16)
            EW('v', tl)
            if n == 0:
                tv = EI('v', lambda v, sl=slot:
                        v.tensor_scalar(BST[0][0:32, :], WBUF[0:32, sl * 4096: sl * 4096 + 4096],
                                        GBC[0:32, 0:1], None, OP.mult), DV)
            else:
                tv = EI('v', lambda v, sl=slot, n=n:
                        v.scalar_tensor_tensor(BST[0][0:32, :], WBUF[0:32, sl * 4096: sl * 4096 + 4096],
                                               GBC[0:32, n:n + 1], BST[0][0:32, :], OP.mult, OP.add), DV)
            wslot_free[slot] = tv
        tl = EI('g', lambda g: g.dma_start(BST[0][32:64, :], loraBs[:, :]), DQ, 16)
        wslot_free[0] = None
        wslot_free[1] = None
        return tl

    def ln_half(p, bufs, arb_free):
        xt, xnb, sqd, mcol, sqcol, sq2col, vcol, rstd, nmcol = bufs
        xt_free = ln_state.get(id(bufs[0]))
        xnb_free = ln_state.get(('xnb', id(bufs[0])))
        last = None
        for i in range(9):
            rows = 3 if i == 8 else 128
            r0 = TH * p + (128 * i if i < 8 else 1024)
            col0 = 128 * i if i < 8 else 1024
            EW('q', xt_free)
            tl = EI('q', lambda q, r0=r0, rows=rows:
                    q.dma_start(xt[0:rows, :], x_tm[r0:r0 + rows, 0:2048]), SQ, 16)
            EW('v', tl)
            EW('v', xt_free)
            t1_ = EI('v', lambda v, rows=rows:
                     v.reduce_sum(mcol[0:rows, :], xt[0:rows, :], axis=AX.X), DV)
            EW('s', t1_)
            tsqa = EI('s', lambda s, rows=rows:
                      s.activation(sqd[0:rows, :], xt[0:rows, :], AF.Square,
                                   accum_out=sqcol[0:rows, :]), AC)
            EW('q', tsqa)
            EW('q', t1_)
            tl = EI('q', lambda q, r0=r0, rows=rows:
                    q.dma_start(xt[0:rows, :], x_tm[r0:r0 + rows, 2048:4096]), SQ, 16)
            EW('v', tl)
            EI('v', lambda v, rows=rows:
               v.reduce_sum(vcol[0:rows, :], xt[0:rows, :], axis=AX.X), DV)
            EW('v', now(DV))
            tsum = EI('v', lambda v, rows=rows:
                      v.tensor_add(mcol[0:rows, :], mcol[0:rows, :], vcol[0:rows, :]), DV)
            EW('v', now(DV))
            EI('v', lambda v, rows=rows:
               v.tensor_scalar_mul(mcol[0:rows, :], mcol[0:rows, :], 1.0 / D), DV)
            EW('s', tsum)
            tsq = EI('s', lambda s, rows=rows:
                     s.activation(sqd[0:rows, :], xt[0:rows, :], AF.Square,
                                  accum_out=sq2col[0:rows, :]), AC)
            EW('v', tsq)
            EI('v', lambda v, rows=rows: v.tensor_add(sqcol[0:rows, :], sqcol[0:rows, :], sq2col[0:rows, :]), DV)
            EW('v', now(DV))
            EI('v', lambda v, rows=rows: v.tensor_scalar_mul(sqcol[0:rows, :], sqcol[0:rows, :], 1.0 / D), DV)
            EW('v', now(DV))
            EI('v', lambda v, rows=rows: v.scalar_tensor_tensor(vcol[0:rows, :], mcol[0:rows, :],
                                                                1.0, mcol[0:rows, :], OP.mult, OP.mult), DV)
            EW('v', now(DV))
            tvar = EI('v', lambda v, rows=rows: v.tensor_sub(vcol[0:rows, :], sqcol[0:rows, :], vcol[0:rows, :]), DV)
            EW('v', now(DV))
            EW('s', tvar)
            EI('s', lambda s, rows=rows: s.activation(rstd[0:rows, :], vcol[0:rows, :], AF.Ln, bias=EPS), AC)
            trs = EI('s', lambda s, rows=rows: s.activation(rstd[0:rows, :], rstd[0:rows, :], AF.Exp, scale=-0.5), AC)
            EW('v', trs)
            tnm = EI('v', lambda v, rows=rows: v.scalar_tensor_tensor(nmcol[0:rows, :], mcol[0:rows, :],
                                                                     -1.0, rstd[0:rows, :], OP.mult, OP.mult), DV)
            EW('v', now(DV))
            EW('s', tnm)
            EW('s', xnb_free)
            if "tap_ln" in tap_aps and p == 0 and i == 1 and arb_free is not None and getattr(ln_half, "_tapped", False) is False:
                ln_half._tapped = True
                EW('q', tnm)
                for _j, _c in enumerate([mcol, sqcol, sq2col, vcol, rstd, nmcol]):
                    EI('q', lambda q, j=_j, c=_c: q.dma_start(tap_aps["tap_ln"][:, 2 * j: 2 * j + 1], c[:, :], allow_slow_non_contiguous=True), SQ, 16)
                EW('v', now(SQ))
                EW('s', now(SQ))
            tx1 = EI('s', lambda s, rows=rows: s.activation(xnb[0:rows, 2048:4096], xt[0:rows, :], AF.Identity,
                                                            bias=nmcol[0:rows, :], scale=rstd[0:rows, :]), AC)
            EW('q', tx1)
            EW('q', tvar)
            tl = EI('q', lambda q, r0=r0, rows=rows:
                    q.dma_start(xt[0:rows, :], x_tm[r0:r0 + rows, 0:2048]), SQ, 16)
            EW('s', tl)
            txn = EI('s', lambda s, rows=rows: s.activation(xnb[0:rows, 0:2048], xt[0:rows, :], AF.Identity,
                                                            bias=nmcol[0:rows, :], scale=rstd[0:rows, :]), AC)
            xt_free = txn
            ln_state[id(bufs[0])] = txn
            for bb in range(4):
                EW('t', txn)
                EW('t', pT_free[0])
                ttr = None
                for dd in range(8):
                    dc = bb * 8 + dd
                    ttr = EI('t', lambda t, dc=dc, dd=dd, rows=rows:
                             t.transpose(pT[:, dd * 128: dd * 128 + rows],
                                         xnb[0:rows, dc * 128:(dc + 1) * 128],
                                         IBF[0:rows, 0:rows]), PE)
                EW('v', ttr)
                EW('v', arb_free[0])
                arb_free[0] = None
                dst = ARB.rearrange("x (dc c) -> x dc c", c=THP)[:, bb * 8:(bb + 1) * 8, col0:col0 + rows]
                srcv = pT.rearrange("x (dd c) -> x dd c", c=128)[:, :, 0:rows]
                tev = EI('v', lambda v, d=dst, s=srcv: v.tensor_copy(d, s), DV)
                pT_free[0] = tev
                last = tev
            xnb_free = now(PE)
            ln_state[('xnb', id(bufs[0]))] = xnb_free
        return last

    def conv_chunk(ci, cj, dest, convin, cdslot, sc, dest_wait):
        slot = sc['wslot']
        sc['wslot'] = 1 - slot
        tload = load_wchunk(ci, slot)
        EW('g', sc['cd_free'])
        tcd = EI('g', lambda g, cj=cj: g.dma_start(
            cdslot[:, :], convdiag[cj, :, :, :].rearrange("i k c -> i (k c)")), DQ, 16)
        EW('t', tload)
        for j in range(2):
            pp = [p512a, p512b][j]
            EW('t', p512_free[j])
            for k in range(DC):
                lhsT = WBUF[:, slot * 4096 + k * 128: slot * 4096 + (k + 1) * 128]
                rhs = ARB[:, k * THP + 3 + 512 * j: k * THP + 3 + 512 * (j + 1)]
                EI('t', lambda t, o=pp[:, :], l_=lhsT, r=rhs, s=(k == 0), e=(k == DC - 1):
                   t.matmul(o, l_, r, start=s, stop=e), PE)
        EW('t', sc['mini_free'])
        EW('t', pSc_free[0])
        tmini = None
        for k in range(DC):
            lhsT = WBUF[:, slot * 4096 + k * 128: slot * 4096 + (k + 1) * 128]
            rhs = ARB[:, k * THP: k * THP + 3]
            tmini = EI('t', lambda t, l_=lhsT, r=rhs, s=(k == 0), e=(k == DC - 1):
                       t.matmul(pMini[:, :], l_, r, start=s, stop=e), PE)
        wslot_free[slot] = tmini
        EW('v', sc['convin_free'])
        EW('v', tmini)
        EI('v', lambda v: v.tensor_copy(convin[:, 0:3], pMini[:, :]), DV)
        sc['mini_free'] = now(DV)
        pSc_free[0] = now(DV)
        EI('v', lambda v: v.tensor_copy(convin[:, 3:515], p512a[:, :]), DV)
        e1 = EI('v', lambda v: v.tensor_copy(convin[:, 515:1027], p512b[:, :]), DV)
        EW('t', tcd)
        EW('t', e1)
        for j in range(2):
            pp = [p512a, p512b][j]
            for k in range(4):
                lhsT = cdslot[:, k * 128:(k + 1) * 128]
                rhs = convin[:, 512 * j + k: 512 * j + k + 512]
                EI('t', lambda t, o=pp[:, :], l_=lhsT, r=rhs, s=(k == 0), e=(k == 3):
                   t.matmul(o, l_, r, start=s, stop=e), PE)
        tconv = now(PE)
        sc['cd_free'] = tconv
        sc['convin_free'] = tconv
        EW('s', tconv)
        if dest_wait is not None:
            for tok in (dest_wait if isinstance(dest_wait, list) else [dest_wait]):
                EW('s', tok)
        EI('s', lambda s, cj=cj: s.activation(dest[:, 0:512], p512a[:, :], AF.Silu,
                                              bias=CONVB[:, cj:cj + 1]), AC)
        ts = EI('s', lambda s, cj=cj: s.activation(dest[:, 512:1024], p512b[:, :], AF.Silu,
                                                   bias=CONVB[:, cj:cj + 1]), AC)
        p512_free[0] = ts
        p512_free[1] = ts
        return ts

    def s1_dt(p, b):
        (dtf, scr, acum, dt_tm, dtd_tm, hi_tm, lo_tm, alt8, lam_all,
         hi_fm, lo_fm, dt_fm, dd_fm) = b
        EW('v', s1dt_dump[0])
        EW('v', now(AC))
        EW('s', now(DV))
        slot = 0
        tload = load_wchunk(0, slot)
        EW('t', tload)
        for j in range(2):
            pp = [p512a, p512b][j]
            EW('t', p512_free[j])
            for k in range(DC):
                lhsT = WBUF[:, slot * 4096 + k * 128: slot * 4096 + (k + 1) * 128]
                rhs = ARB[:, k * THP + 3 + 512 * j: k * THP + 3 + 512 * (j + 1)]
                EI('t', lambda t, o=pp[:, :], l_=lhsT, r=rhs, s=(k == 0), e=(k == DC - 1):
                   t.matmul(o, l_, r, start=s, stop=e), PE)
        wslot_free[slot] = now(PE)
        EW('v', now(PE))
        EI('v', lambda v: v.tensor_copy(dtf[:, 0:512], p512a[:, :]), DV)
        te = EI('v', lambda v: v.tensor_copy(dtf[:, 512:1024], p512b[:, :]), DV)
        p512_free[0] = te
        p512_free[1] = te
        EW('v', now(DV))
        tb = EI('v', lambda v: v.tensor_scalar(dtf[:, :], dtf[:, :], DTB[:, :], None, OP.add), DV)
        EW('s', tb)
        EI('s', lambda s: s.activation(scr[:, :], dtf[:, :], AF.Abs), AC)
        EI('s', lambda s: s.activation(scr[:, :], scr[:, :], AF.Exp, scale=-1.0), AC)
        ta = EI('s', lambda s: s.activation(scr[:, :], scr[:, :], AF.Ln, bias=1.0), AC)
        EW('v', ta)
        EI('v', lambda v: v.tensor_scalar_max(dtf[:, :], dtf[:, :], 0.0), DV)
        EW('v', now(DV))
        EI('v', lambda v: v.tensor_add(dtf[:, :], dtf[:, :], scr[:, :]), DV)
        EW('v', now(DV))
        EI('v', lambda v: v.tensor_scalar_min(dtf[:, :], dtf[:, :], 100.0), DV)
        EW('v', now(DV))
        EI('v', lambda v: v.tensor_scalar(scr[:, :], dtf[:, :], ACOL[:, :], None, OP.mult), DV)
        EW('v', now(DV))
        for z in range(NZ):
            EI('v', lambda v, z=z: v.tensor_tensor_scan(
                acum[:, z * 128:(z + 1) * 128], scr[:, z * 128:(z + 1) * 128],
                ZZ[:, :], 0.0, OP.add, OP.add), DV)
        EW('v', now(DV))
        EI('v', lambda v: v.tensor_copy(hi_fm[:, :], acum[:, :]), DV)
        EW('v', now(DV))
        EI('v', lambda v: v.tensor_tensor(lo_fm[:, :], acum[:, :], hi_fm[:, :], OP.subtract), DV)
        EI('v', lambda v: v.tensor_copy(dt_fm[:, :], dtf[:, :]), DV)
        acv = acum.rearrange("h (z l) -> h z l", l=128)
        alb = acv[:, :, 127:128].broadcast_to([128, NZ, 128])
        tda = EI('v', lambda v, a=acv, b_=alb: v.tensor_tensor(
            scr.rearrange("h (z l) -> h z l", l=128), b_, a, OP.subtract), DV)
        EW('s', tda)
        tdd = EI('s', lambda s: s.activation(dd_fm[:, :], scr[:, :], AF.Exp), AC)
        EW('v', tdd)
        EW('v', now(DV))
        EI('v', lambda v: v.tensor_mul(dd_fm[:, :], dd_fm[:, :], dt_fm[:, :]), DV)
        EW('v', now(DV))
        tal = EI('v', lambda v, a=acv: v.tensor_copy(dt_tm[:, 0:NZ], a[:, :, 127]), DV)
        EW('t', tal)
        EW('t', pT_free[0])
        ttr = EI('t', lambda t: t.transpose(pT[0:NZ, 0:128], dt_tm[:, 0:NZ], IBF[:, :]), PE)
        EW('v', ttr)
        tv = EI('v', lambda v: v.tensor_copy(alt8[:, :], pT[0:NZ, 0:128]), DV)
        pT_free[0] = tv
        for (src, dst) in [(hi_fm, hi_tm), (lo_fm, lo_tm), (dt_fm, dt_tm), (dd_fm, dtd_tm)]:
            EW('t', now(DV))
            EW('t', pT_free[0])
            ttr = None
            for z in range(NZ):
                ttr = EI('t', lambda t, s_=src, z=z:
                         t.transpose(pT[:, z * 128:(z + 1) * 128],
                                     s_[:, z * 128:(z + 1) * 128], IBF[:, :]), PE)
            EW('v', ttr)
            dstv = dst.rearrange("l (z h) -> l z h", h=128)
            srcv = pT.rearrange("l (z h) -> l z h", h=128)
            tv = EI('v', lambda v, d=dstv, s_=srcv: v.tensor_copy(d, s_), DV)
            pT_free[0] = tv
        EW('t', now(DV))
        for z in range(NZ):
            EW('t', pSc_free[0])
            tm = EI('t', lambda t, z=z: t.matmul(pSc[:, :], SELZ8[:, z * 128:(z + 1) * 128],
                                                 alt8[:, :], start=True, stop=True), PE)
            EW('s', tm)
            tse = EI('s', lambda s, z=z: s.activation(lam_all[:, z * 128:(z + 1) * 128],
                                                      pSc[:, :], AF.Exp), AC)
            pSc_free[0] = tse
        EW('q', now(DV))
        EW('q', now(AC))
        for idx, src in [(0, dt_tm), (1, dtd_tm), (2, hi_tm), (3, lo_tm)]:
            EI('q', lambda q, i=idx, s_=src, p=p:
               q.dma_start(dt_dram[p, i, :, :], s_[:, :]), SQ, 16)
        tdump = EI('q', lambda q, p=p: q.dma_start(dt_dram[p, 4, 0:8, 0:128], alt8[:, :]), SQ, 16)
        s1dt_dump[0] = tdump
        if "tap_acum" in tap_aps and p == 0:
            EI('q', lambda q: q.dma_start(tap_aps["tap_acum"][:, :], acum[:, :]), SQ, 16)
        return tdump

    def s1_group(p, g, b, sc):
        (convin, cdslot, Bfm, Cfm, Xtmg, btm, xdtd, prevf,
         dt_tm, dtd_tm, lam_all) = b
        tb = conv_chunk(CH_B(g), CONV_B(g), Bfm, convin, cdslot, sc, sc['bfm_free'])
        EW('q', tb)
        tbd = EI('q', lambda q, g=g, p=p: q.dma_start(B_dram[g, :, TH * p: TH * (p + 1)], Bfm[:, :]), SQ, 16)
        tc = conv_chunk(CH_C(g), CONV_C(g), Cfm, convin, cdslot, sc, sc['cfm_free'])
        EW('q', tc)
        tcd_ = EI('q', lambda q, g=g, p=p: q.dma_start(C_dram[g, :, TH * p: TH * (p + 1)], Cfm[:, :]), SQ, 16)
        cfm_wait = [tcd_]
        for q_ in range(8):
            tx = conv_chunk(CH_HS(g, q_), CONV_HS(g, q_), Cfm, convin, cdslot, sc, cfm_wait)
            EW('t', tx)
            EW('t', pT_free[0])
            ttr = None
            for z in range(NZ):
                ttr = EI('t', lambda t, z=z: t.transpose(pT[:, z * 128:(z + 1) * 128],
                                                         Cfm[:, z * 128:(z + 1) * 128], IBF[:, :]), PE)
            cfm_wait = [ttr]
            EW('v', ttr)
            EW('v', sc['xtm_free'])
            sc['xtm_free'] = None
            EW('v', sc['xdump_free'])
            sc['xdump_free'] = None
            dstv = Xtmg.rearrange("l (z c) -> l z c", c=1024)[:, :, q_ * 128:(q_ + 1) * 128]
            srcv = pT.rearrange("l (z c) -> l z c", c=128)
            tev = EI('v', lambda v, d=dstv, s_=srcv: v.tensor_copy(d, s_), DV)
            pT_free[0] = tev
            EW('q', tev)
            txd_ = EI('q', lambda q, g=g, p=p, q_=q_, dv_=dstv: q.dma_start(
                X_dram[g, NZ * p: NZ * (p + 1), :, q_ * 128:(q_ + 1) * 128].rearrange("z l c -> l z c"),
                dv_), SQ, 16)
            sc['last_xdump'] = txd_
        if p == 0:
            tpi = EI('v', lambda v: v.memset(prevf[:, :], 0.0), DV)
        else:
            EW('v', sc['prevall_ok'])
            tpi = EI('v', lambda v, g=g: v.tensor_copy(prevf[:, :], PREVALL[:, g * 1024:(g + 1) * 1024]), DV)
        tst = None
        for z in range(NZ):
            EW('v', now(DV))
            if "tap_pz" in tap_aps and p == 0 and g == 0 and z == 1:
                EW('q', tpi)
                EI('q', lambda q: q.dma_start(tap_aps["tap_pz"][:, 0:1024], prevf[:, :]), SQ, 16)
                EI('g', lambda g_: g_.dma_start(tap_aps["tap_pz"][:, 1024:1040], lam_all[:, 0:16]), DQ, 16)
                EI('g', lambda g_: g_.dma_start(tap_aps["tap_pz"][:, 1040:1056], dtd_tm[:, 0:16]), DQ, 16)
                EI('g', lambda g_: g_.dma_start(tap_aps["tap_pz"][:, 1056:1072], dt_tm[:, 0:16]), DQ, 16)
                EI('g', lambda g_: g_.dma_start(tap_aps["tap_pz"][:, 1072:2048], xdtd[:, 0:976]), DQ, 16)
                EW('v', now(DQ))
                EW('v', now(SQ))
            EW('v', tst)
            EW('t', pT_free[0])
            ttr = EI('t', lambda t, z=z: t.transpose(pT[:, 0:128], Bfm[:, z * 128:(z + 1) * 128],
                                                     IBF[:, :]), PE)
            EW('v', ttr)
            tbm = EI('v', lambda v: v.tensor_copy(btm[:, :], pT[:, 0:128]), DV)
            pT_free[0] = tbm
            xv = Xtmg[:, z * 1024:(z + 1) * 1024].rearrange("l (hh pp) -> l hh pp", pp=64)
            dd = dtd_tm[:, z * 128 + HPG * g: z * 128 + HPG * (g + 1)].unsqueeze(2).broadcast_to([128, HPG, 64])
            txd = EI('v', lambda v, xv=xv, dd=dd: v.tensor_tensor(
                xdtd.rearrange("l (hh pp) -> l hh pp", pp=64), xv, dd, OP.mult), DV)
            EW('t', txd)
            for j in range(2):
                pp = [p512a, p512b][j]
                EW('t', p512_free[j])
                EI('t', lambda t, o=pp[:, :], j=j: t.matmul(
                    o, btm[:, :], xdtd[:, 512 * j: 512 * (j + 1)], start=True, stop=True), PE)
            tst = now(PE)
            lamv = lam_all[:, z * 128 + HPG * g: z * 128 + HPG * (g + 1)].unsqueeze(2).broadcast_to([128, HPG, 64])
            EW('v', now(DV))
            EI('v', lambda v, lamv=lamv: v.tensor_tensor(
                prevf.rearrange("n (hh pp) -> n hh pp", pp=64),
                prevf.rearrange("n (hh pp) -> n hh pp", pp=64), lamv, OP.mult), DV)
            EW('v', now(DV))
            EW('v', tst)
            EI('v', lambda v: v.tensor_add(prevf[:, 0:512], prevf[:, 0:512], p512a[:, :]), DV)
            tpu = EI('v', lambda v: v.tensor_add(prevf[:, 512:1024], prevf[:, 512:1024], p512b[:, :]), DV)
            p512_free[0] = tpu
            p512_free[1] = tpu
        EW('v', now(DV))
        tsv = EI('v', lambda v, g=g: v.tensor_copy(PREVALL[:, g * 1024:(g + 1) * 1024], prevf[:, :]), DV)
        sc['xtm_free'] = tst
        sc['xdump_free'] = sc['last_xdump']
        sc['bfm_free'] = [now(PE), tbd]
        sc['cfm_free'] = cfm_wait + [sc['last_xdump']]
        return tsv

    def s2_group(p, g, b, sc):
        (Bfm, Cfm, sgtm, sgf, E_, scm, btm, xz, xdt, xdtd,
         prevf, prevbf, hiT8, loT8, rhs1, ea, t1b, gzb,
         varcol, rstd, dt_tm, dtd_tm, hi_tm, lo_tm, lam_all) = b
        EW('q', sc['bc_free'])
        tlb = EI('q', lambda q, g=g, p=p: q.dma_start(Bfm[:, :], B_dram[g, :, TH * p: TH * (p + 1)]), SQ, 16)
        tlc = EI('q', lambda q, g=g, p=p: q.dma_start(Cfm[:, :], C_dram[g, :, TH * p: TH * (p + 1)]), SQ, 16)
        sgf_free = None
        for q_ in range(8):
            slot = sc['wslot']
            sc['wslot'] = 1 - slot
            tload = load_wchunk(CH_GATE(g, q_), slot)
            EW('t', tload)
            for j in range(2):
                pp = [p512a, p512b][j]
                EW('t', p512_free[j])
                for k in range(DC):
                    lhsT = WBUF[:, slot * 4096 + k * 128: slot * 4096 + (k + 1) * 128]
                    rhs = ARB[:, k * THP + 3 + 512 * j: k * THP + 3 + 512 * (j + 1)]
                    EI('t', lambda t, o=pp[:, :], l_=lhsT, r=rhs, s=(k == 0), e=(k == DC - 1):
                       t.matmul(o, l_, r, start=s, stop=e), PE)
            wslot_free[slot] = now(PE)
            EW('s', now(PE))
            EW('s', sgf_free)
            EI('s', lambda s: s.activation(sgf[:, 0:512], p512a[:, :], AF.Silu), AC)
            tsg = EI('s', lambda s: s.activation(sgf[:, 512:1024], p512b[:, :], AF.Silu), AC)
            p512_free[0] = tsg
            p512_free[1] = tsg
            EW('t', tsg)
            EW('t', pT_free[0])
            ttr = None
            for z in range(NZ):
                ttr = EI('t', lambda t, z=z: t.transpose(pT[:, z * 128:(z + 1) * 128],
                                                         sgf[:, z * 128:(z + 1) * 128], IBF[:, :]), PE)
            sgf_free = ttr
            EW('v', ttr)
            EW('v', sc['sgtm_free'])
            sc['sgtm_free'] = None
            dstv = sgtm.rearrange("l (z c) -> l z c", c=1024)[:, :, q_ * 128:(q_ + 1) * 128]
            srcv = pT.rearrange("l (z c) -> l z c", c=128)
            tev = EI('v', lambda v, d=dstv, s_=srcv: v.tensor_copy(d, s_), DV)
            pT_free[0] = tev
        if p == 0:
            tpi = EI('v', lambda v, g=g: v.tensor_copy(prevf[:, :], SINALL[0][:, g * 1024:(g + 1) * 1024]), DV)
        else:
            tpi = EI('v', lambda v, g=g: v.tensor_copy(prevf[:, :], PREVALL[:, g * 1024:(g + 1) * 1024]), DV)
        EW('t', tlb)
        EW('t', tlc)
        xz_free = [None, None]
        z_pe = None
        for z in range(NZ):
            gz = NZ * p + z
            xsl = z % 2
            EW('q', xz_free[xsl])
            tlx = EI('q', lambda q, g=g, gz=gz, xsl=xsl: q.dma_start(
                xz[:, xsl * 1024:(xsl + 1) * 1024], X_dram[g, gz, :, :]), SQ, 16)
            xzv = xz[:, xsl * 1024:(xsl + 1) * 1024]
            EW('v', z_pe)
            EW('v', now(AC))
            EW('s', z_pe)
            EW('s', now(DV))
            EW('v', sc['gn_free'])
            EW('t', pT_free[0])
            ttr = EI('t', lambda t, z=z: t.transpose(pT[:, 0:128], Bfm[:, z * 128:(z + 1) * 128],
                                                     IBF[:, :]), PE)
            EW('v', ttr)
            tbm = EI('v', lambda v: v.tensor_copy(btm[:, :], pT[:, 0:128]), DV)
            pT_free[0] = tbm
            EW('t', pSc_free[0])
            tsc = EI('t', lambda t, z=z: t.matmul(pSc[:, :], Bfm[:, z * 128:(z + 1) * 128],
                                                  Cfm[:, z * 128:(z + 1) * 128], start=True, stop=True), PE)
            EW('v', tsc)
            tscm = EI('v', lambda v: v.tensor_tensor(scm[:, :], pSc[:, :], TRIL[:, :], OP.mult), DV)
            pSc_free[0] = tscm
            EW('v', tlx)
            dtv = dt_tm[:, z * 128 + HPG * g: z * 128 + HPG * (g + 1)].unsqueeze(2).broadcast_to([128, HPG, 64])
            ddv = dtd_tm[:, z * 128 + HPG * g: z * 128 + HPG * (g + 1)].unsqueeze(2).broadcast_to([128, HPG, 64])
            xvv = xzv.rearrange("l (hh pp) -> l hh pp", pp=64)
            txdt = EI('v', lambda v, xvv=xvv, dtv=dtv: v.tensor_tensor(
                xdt.rearrange("l (hh pp) -> l hh pp", pp=64), xvv, dtv, OP.mult), DV)
            txdd = EI('v', lambda v, xvv=xvv, ddv=ddv: v.tensor_tensor(
                xdtd.rearrange("l (hh pp) -> l hh pp", pp=64), xvv, ddv, OP.mult), DV)
            dbv = DBC[:, HPG * g: HPG * (g + 1)].unsqueeze(2).broadcast_to([128, HPG, 64])
            txD = EI('v', lambda v, xvv=xvv, dbv=dbv: v.tensor_tensor(
                gzb.rearrange("l (hh pp) -> l hh pp", pp=64), xvv, dbv, OP.mult), DV)
            first_y = True
            for q2 in range(2):
                h0 = z * 128 + HPG * g + 8 * q2
                EW('v', sc['ms_free'])
                sc['ms_free'] = None
                EW('t', pT_free[0])
                EI('t', lambda t, h0=h0: t.transpose(pT[0:8, 0:128], hi_tm[:, h0:h0 + 8], IBF[:, :]), PE)
                t2_ = EI('t', lambda t, h0=h0: t.transpose(pT[0:8, 128:256], lo_tm[:, h0:h0 + 8], IBF[:, :]), PE)
                EW('v', t2_)
                EI('v', lambda v: v.tensor_copy(hiT8[:, :], pT[0:8, 0:128]), DV)
                tcl = EI('v', lambda v: v.tensor_copy(loT8[:, :], pT[0:8, 128:256]), DV)
                pT_free[0] = tcl
                ex = IBF[:, :].unsqueeze(1).broadcast_to([128, 8, 128])
                hb = hi_tm[:, h0:h0 + 8].unsqueeze(2).broadcast_to([128, 8, 128])
                lb = lo_tm[:, h0:h0 + 8].unsqueeze(2).broadcast_to([128, 8, 128])
                tr1 = EI('v', lambda v, ex=ex, hb=hb: v.tensor_tensor(
                    rhs1.rearrange("l (hh c) -> l hh c", c=128), ex, hb, OP.mult), DV)
                EW('t', sc['pBig2_free'])
                sc['pBig2_free'] = None
                EW('t', tcl)
                EW('t', tr1)
                for j in range(2):
                    EI('t', lambda t, j=j: t.matmul(pBig2[:, 512 * j:512 * (j + 1)], hiT8[:, :],
                                                    NEGEX8[:, 512 * j: 512 * (j + 1)], start=True, stop=False), PE)
                    EI('t', lambda t, j=j: t.matmul(pBig2[:, 512 * j:512 * (j + 1)], loT8[:, :],
                                                    NEGEX8[:, 512 * j: 512 * (j + 1)], start=False, stop=False), PE)
                    EI('t', lambda t, j=j: t.matmul(pBig2[:, 512 * j:512 * (j + 1)], ONES[:, :],
                                                    rhs1[:, 512 * j: 512 * (j + 1)], start=False, stop=False), PE)
                th1 = now(PE)
                EW('v', th1)
                tr2 = EI('v', lambda v, ex=ex, lb=lb: v.tensor_tensor(
                    rhs1.rearrange("l (hh c) -> l hh c", c=128), ex, lb, OP.mult), DV)
                EW('t', tr2)
                tmd = None
                for j in range(2):
                    tmd = EI('t', lambda t, j=j: t.matmul(pBig2[:, 512 * j:512 * (j + 1)], ONES[:, :],
                                                          rhs1[:, 512 * j: 512 * (j + 1)], start=False, stop=True), PE)
                EW('v', tmd)
                EW('v', sc['sq_free'])
                sc['sq_free'] = None
                tmn = EI('v', lambda v: v.tensor_scalar_min(E_[:, :], pBig2[:, :], 0.0), DV)
                sc['pBig2_free'] = tmn
                EW('s', tmn)
                tex = EI('s', lambda s: s.activation(E_[:, :], E_[:, :], AF.Exp), AC)
                EW('v', tex)
                EW('v', tscm)
                scb = scm[:, :].unsqueeze(1).broadcast_to([128, 8, 128])
                tms = EI('v', lambda v, scb=scb: v.tensor_tensor(
                    E_.rearrange("s (hh c) -> s hh c", c=128),
                    E_.rearrange("s (hh c) -> s hh c", c=128), scb, OP.mult), DV)
                if first_y:
                    EW('t', sc['pBig1_free'])
                    sc['pBig1_free'] = None
                    first_y = False
                EW('t', tms)
                EW('t', txdt)
                EW('t', txD)
                for hh8 in range(8):
                    hh = 8 * q2 + hh8
                    EI('t', lambda t, hh=hh, hh8=hh8: t.matmul(pBig1[:, hh * 64:(hh + 1) * 64],
                                                               E_[:, hh8 * 128:(hh8 + 1) * 128],
                                                               xdt[:, hh * 64:(hh + 1) * 64], start=True, stop=False), PE)
                    EI('t', lambda t, hh=hh: t.matmul(pBig1[:, hh * 64:(hh + 1) * 64],
                                                      IBF[:, :],
                                                      gzb[:, hh * 64:(hh + 1) * 64], start=False, stop=True), PE)
                sc['ms_free'] = now(PE)
            tyd = now(PE)
            EW('v', tpi)
            tpb = EI('v', lambda v: v.tensor_copy(prevbf[:, :], prevf[:, :]), DV)
            EW('t', sc['pBig2_free'])
            sc['pBig2_free'] = None
            EW('t', tpb)
            for j in range(2):
                EI('t', lambda t, z=z, j=j: t.matmul(pBig2[:, 512 * j:512 * (j + 1)],
                                                     Cfm[:, z * 128:(z + 1) * 128],
                                                     prevbf[:, 512 * j:512 * (j + 1)], start=True, stop=True), PE)
            tya = now(PE)
            EW('t', txdd)
            for j in range(2):
                pp = [p512a, p512b][j]
                EW('t', p512_free[j])
                EI('t', lambda t, o=pp[:, :], j=j: t.matmul(
                    o, btm[:, :], xdtd[:, 512 * j: 512 * (j + 1)], start=True, stop=True), PE)
            tst = now(PE)
            lamv = lam_all[:, z * 128 + HPG * g: z * 128 + HPG * (g + 1)].unsqueeze(2).broadcast_to([128, HPG, 64])
            EW('v', tya)
            EW('v', now(DV))
            EI('v', lambda v, lamv=lamv: v.tensor_tensor(
                prevf.rearrange("n (hh pp) -> n hh pp", pp=64),
                prevf.rearrange("n (hh pp) -> n hh pp", pp=64), lamv, OP.mult), DV)
            EW('v', now(DV))
            EW('v', tst)
            EI('v', lambda v: v.tensor_add(prevf[:, 0:512], prevf[:, 0:512], p512a[:, :]), DV)
            tpu = EI('v', lambda v: v.tensor_add(prevf[:, 512:1024], prevf[:, 512:1024], p512b[:, :]), DV)
            p512_free[0] = tpu
            p512_free[1] = tpu
            tpi = tpu
            tea = EI('s', lambda s, z=z, g=g: s.activation(
                ea[:, :], hi_tm[:, z * 128 + HPG * g: z * 128 + HPG * (g + 1)], AF.Exp), AC)
            EW('v', tea)
            eav = ea[:, :].unsqueeze(2).broadcast_to([128, HPG, 64])
            tt1 = EI('v', lambda v, eav=eav: v.tensor_tensor(
                t1b.rearrange("l (hh pp) -> l hh pp", pp=64),
                pBig2.rearrange("l (hh pp) -> l hh pp", pp=64), eav, OP.mult), DV)
            sc['pBig2_free'] = tt1
            EW('v', tyd)
            EW('v', now(DV))
            tt2 = EI('v', lambda v: v.tensor_add(gzb[:, :], pBig1[:, :], t1b[:, :]), DV)
            sc['pBig1_free'] = tt2
            EW('v', now(DV))
            tgt = EI('v', lambda v, z=z: v.tensor_mul(gzb[:, :], gzb[:, :],
                                                      sgtm[:, z * 1024:(z + 1) * 1024]), DV)
            if "tap_g" in tap_aps and p == 0 and g == 0 and z == 0:
                EW('q', tgt)
                EI('q', lambda q: q.dma_start(tap_aps["tap_g"][:, :], gzb[:, :]), SQ, 16)
            EW('s', tgt)
            tsqg = EI('s', lambda s: s.activation(E_[:, :], gzb[:, :], AF.Square,
                                                  accum_out=varcol[:, :]), AC)
            sc['sq_free'] = tsqg
            EI('s', lambda s: s.activation(rstd[:, :], varcol[:, :], AF.Ln,
                                           bias=EPS, scale=1.0 / 1024), AC)
            tsr = EI('s', lambda s: s.activation(rstd[:, :], rstd[:, :], AF.Exp, scale=-0.5), AC)
            EW('v', tsr)
            tgn = EI('v', lambda v: v.tensor_scalar(gzb[:, :], gzb[:, :], rstd[:, :], None, OP.mult), DV)
            EW('t', tgn)
            EW('t', pT_free[0])
            ttr = None
            for cc in range(8):
                ttr = EI('t', lambda t, cc=cc: t.transpose(pT[:, cc * 128:(cc + 1) * 128],
                                                           gzb[:, cc * 128:(cc + 1) * 128], IBF[:, :]), PE)
            EW('v', ttr)
            tev = None
            for cc in range(8):
                tev = EI('v', lambda v, cc=cc, g=g: v.tensor_scalar(
                    t1b[:, cc * 128:(cc + 1) * 128], pT[:, cc * 128:(cc + 1) * 128],
                    NORMW[:, 8 * g + cc: 8 * g + cc + 1], None, OP.mult), DV)
            pT_free[0] = tev
            EW('q', tev)
            tdq = None
            for cc in range(8):
                tdq = EI('q', lambda q, cc=cc, g=g, gz=gz: q.dma_start(
                    gn_dram[8 * g + cc, :, gz, :], t1b[:, cc * 128:(cc + 1) * 128]), SQ, 16)
            sc['gn_free'] = tdq
            if "tap_gn" in tap_aps and p == 0 and g == 0 and z == 0:
                EI('q', lambda q: q.dma_start(tap_aps["tap_gn"][:, :], t1b[:, :]), SQ, 16)
            xz_free[xsl] = txD
            z_pe = now(PE)
        EW('v', now(DV))
        tsv = EI('v', lambda v, g=g: v.tensor_copy(PREVALL[:, g * 1024:(g + 1) * 1024], prevf[:, :]), DV)
        sc['bc_free'] = z_pe
        sc['ddiag_free'] = z_pe
        return tsv

    def s2_ffn(p, axb, ft):
        for j in range(2):
            EW('t', p512_free[0])
            tmm = None
            for dc in range(DC):
                lhsT = AEFF[0][:, dc * 64:(dc + 1) * 64]
                rhs = ARB[:, dc * THP + 3 + 512 * j: dc * THP + 3 + 512 * (j + 1)]
                tmm = EI('t', lambda t, l_=lhsT, r=rhs, s=(dc == 0), e=(dc == DC - 1):
                         t.matmul(p512a[0:64, :], l_, r, start=s, stop=e), PE)
            EW('v', tmm)
            EW('v', p == 1 and ffn_axfree[0] or None)
            tev = EI('v', lambda v, j=j: v.tensor_copy(axb[0:64, 512 * j: 512 * (j + 1)], p512a[0:64, :]), DV)
            p512_free[0] = tev
        axtok = now(DV)
        ft_free = None
        for oc in range(32):
            slot = oc % 2
            EW('g', wslot_free[slot])
            tload = EI('g', lambda g, oc=oc, sl=slot: g.dma_start(
                WBUF[:, sl * 4096:(sl + 1) * 4096],
                kern[oc, :, :, :].rearrange("d k c -> d (k c)")), DQ, 16)
            EW('t', tload)
            EW('t', axtok)
            for j in range(2):
                pp = [p512a, p512b][j]
                EW('t', p512_free[j])
                for k in range(DC):
                    lhsT = WBUF[:, slot * 4096 + k * 128: slot * 4096 + (k + 1) * 128]
                    rhs = ARB[:, k * THP + 3 + 512 * j: k * THP + 3 + 512 * (j + 1)]
                    EI('t', lambda t, o=pp[:, :], l_=lhsT, r=rhs, s=(k == 0):
                       t.matmul(o, l_, r, start=s, stop=False), PE)
                EI('t', lambda t, o=pp[:, :], oc=oc, j=j: t.matmul(
                    o, BST[0][:, oc * 128:(oc + 1) * 128], axb[0:64, 512 * j: 512 * (j + 1)],
                    start=False, stop=True), PE)
            tmm = now(PE)
            wslot_free[slot] = tmm
            EW('s', tmm)
            EW('s', ft_free)
            tev = None
            for j in range(2):
                pp = [p512a, p512b][j]
                tev = EI('s', lambda s, pp=pp, oc=oc, j=j: s.activation(
                    ft[:, 512 * j: 512 * (j + 1)], pp[:, :], AF.Identity, bias=FFNB[:, oc:oc + 1]), AC)
            p512_free[0] = tev
            p512_free[1] = tev
            EW('q', tev)
            tq = EI('q', lambda q, oc=oc, p=p: q.dma_start(
                ffn_dram[oc, :, 2 * p: 2 * p + 2, :].rearrange("c tg t -> c (tg t)"), ft[:, :]), SQ, 16)
            ft_free = tq
        ffn_axfree[0] = now(PE)
        return ft_free

    ffn_axfree = [None]

    p0()

    s1_scope = contextlib.ExitStack()

    def asb(name, shape, dt=BF16):
        return s1_scope.enter_context(nc.sbuf_tensor(name, shape, dt))

    xt = asb("xt1", [128, 2048], F32)
    xnb = asb("xnb1", [128, D])
    sqd0 = asb("sqd0", [128, 2048])
    mcol = asb("mcol", [128, 1], F32)
    sqcol = asb("sqcol", [128, 1], F32)
    vcol = asb("vcol", [128, 1], F32)
    rstd0 = asb("rstd0", [128, 1], F32)
    nmcol = asb("nmcol", [128, 1], F32)
    sq2col = asb("sq2col", [128, 1], F32)
    ln_bufs1 = (xt, xnb, sqd0, mcol, sqcol, sq2col, vcol, rstd0, nmcol)
    dtf = asb("dtf", [128, TH], F32)
    scr = asb("scr", [128, TH], F32)
    acum = asb("acum", [128, TH], F32)
    dt_tm = asb("dt_tm1", [128, TH])
    dtd_tm = asb("dtd_tm1", [128, TH])
    hi_tm = asb("hi_tm1", [128, TH])
    lo_tm = asb("lo_tm1", [128, TH])
    hi_fm = asb("hi_fm", [128, TH])
    lo_fm = asb("lo_fm", [128, TH])
    dt_fm = asb("dt_fm", [128, TH])
    dd_fm = asb("dd_fm", [128, TH])
    alt8 = asb("alt8", [8, 128])
    lam_all = asb("lam_all1", [128, TH])
    convin = asb("convin1", [128, THP])
    cdslot = asb("cdslot", [128, 512])
    Bfm = asb("Bfm1", [128, TH])
    Cfm = asb("Cfm1", [128, TH])
    Xtmg = asb("Xtmg", [128, NZ * 1024])
    btm = asb("btm1", [128, 128])
    xdtd = asb("xdtd1", [128, 1024])
    prevf = asb("prevf1", [128, 1024], F32)

    arb_free = [None]
    s1dt_dump = [None]
    sc1 = {'wslot': 0, 'cd_free': None, 'mini_free': None, 'convin_free': None,
           'xtm_free': None, 'xdump_free': None, 'last_xdump': None,
           'bfm_free': None, 'cfm_free': None, 'prevall_ok': None}

    for p in range(2):
        ln_half(p, ln_bufs1, arb_free)
        s1_dt(p, (dtf, scr, acum, dt_tm, dtd_tm, hi_tm, lo_tm, alt8, lam_all,
                  hi_fm, lo_fm, dt_fm, dd_fm))
        if "tap_xn" in tap_aps and p == 0:
            EW('q', now(DV))
            EI('q', lambda q: q.dma_start(tap_aps["tap_xn"][:, :], ARB[:, :]), SQ, 16)
        for g in range(G):
            tsv = s1_group(p, g, (convin, cdslot, Bfm, Cfm, Xtmg, btm, xdtd,
                                  prevf, dt_tm, dtd_tm, lam_all), sc1)
        sc1['prevall_ok'] = tsv
        arb_free[0] = now(PE)
    EW('q', now(DV))
    if "tap_B" in tap_aps:
        EI('q', lambda q: q.dma_start(tap_aps["tap_B"][:, :], B_dram[0, :, :]), SQ, 16)
    if "tap_X" in tap_aps:
        EI('q', lambda q: q.dma_start(tap_aps["tap_X"][:, :], X_dram[0, 0, :, :]), SQ, 16)
    tex = EI('q', lambda q: q.dma_start(ex_dram[:, :], PREVALL[:, :]), SQ, 16)
    if "tap_prevall" in tap_aps:
        EI('q', lambda q: q.dma_start(tap_aps["tap_prevall"][:, :], PREVALL[:, :]), SQ, 16)
    EW('g', tex)
    tcc = EI('g', lambda g: g.collective_compute(
        "AllGather", OP.bypass, replica_groups=[list(range(NCORE))],
        ins=[ex_dram.tensor.ap().opt()], outs=[gath_dram.tensor.ap().opt()]), CC, 1)

    s1_scope.close()

    s2_scope = contextlib.ExitStack()

    def bsb(name, shape, dt=BF16):
        return s2_scope.enter_context(nc.sbuf_tensor(name, shape, dt))

    Gbuf = bsb("Gbuf", [128, 8192])
    AEFF[0] = bsb("AEFF", [128, 2048])
    BST[0] = bsb("BST", [64, 4096])
    sgtm = Gbuf
    xt2 = bsb("xt2", [128, 2048], F32)
    xnb2 = bsb("xnb2", [128, D])
    sqd2 = bsb("sqd2", [128, 2048])
    mcol2 = bsb("mcol2", [128, 1], F32)
    sqcol2 = bsb("sqcol2", [128, 1], F32)
    vcol2 = bsb("vcol2", [128, 1], F32)
    rstd2 = bsb("rstd2", [128, 1], F32)
    nmcol2 = bsb("nmcol2", [128, 1], F32)
    sq2col2 = bsb("sq2col2", [128, 1], F32)
    ln_bufs2 = (xt2, xnb2, sqd2, mcol2, sqcol2, sq2col2, vcol2, rstd2, nmcol2)
    dt_tm2 = bsb("dt_tm2", [128, TH])
    dtd_tm2 = bsb("dtd_tm2", [128, TH])
    hi_tm2 = bsb("hi_tm2", [128, TH])
    lo_tm2 = bsb("lo_tm2", [128, TH])
    alt82 = bsb("alt82", [8, 128])
    lam2 = bsb("lam2", [128, TH])
    Bfm2 = bsb("Bfm2", [128, TH])
    Cfm2 = bsb("Cfm2", [128, TH])
    sgf = bsb("sgf", [128, TH])
    E_ = bsb("E_", [128, 1024])
    scm2 = bsb("scm2", [128, 128])
    btm2 = bsb("btm2", [128, 128])
    xz2 = bsb("xz2", [128, 2048])
    xdt2 = bsb("xdt2", [128, 1024])
    xdtd2 = bsb("xdtd2", [128, 1024])
    prevf2 = bsb("prevf2", [128, 1024], F32)
    prevbf = bsb("prevbf", [128, 1024])
    hiT8 = bsb("hiT8", [8, 128])
    loT8 = bsb("loT8", [8, 128])
    rhs1 = bsb("rhs1", [128, 1024])
    ea2 = bsb("ea2", [128, 16])
    t1b = bsb("t1b", [128, 1024])
    gzb = bsb("gzb", [128, 1024])
    varcol = bsb("varcol", [128, 1], F32)
    rstd3 = bsb("rstd3", [128, 1], F32)
    sin_scope = contextlib.ExitStack()
    SINALL[0] = sin_scope.enter_context(nc.sbuf_tensor("SINALL", [128, 8192], BF16))
    sin_scope.close()
    axb = bsb("axb", [64, 1024])
    ft2 = bsb("ft2", [128, 1024])

    p0_lora()
    tmz = EI('v', lambda v: v.memset(SINALL[0][:, :], 0.0), DV)
    EW('g', tcc)
    for r in range(NCORE):
        tlg = EI('g', lambda g, r=r: g.dma_start(Gbuf[:, :], gath_dram[128 * r:128 * (r + 1), :]), DQ, 16)
        EW('v', tlg)
        tmz = EI('v', lambda v, r=r: v.scalar_tensor_tensor(
            SINALL[0][:, :], Gbuf[:, :], SELB[:, r:r + 1], SINALL[0][:, :], OP.mult, OP.add), DV)
        EW('g', tmz)
    if "tap_sin" in tap_aps:
        EW('q', tmz)
        EI('q', lambda q: q.dma_start(tap_aps["tap_sin"][:, :], SINALL[0][:, :]), SQ, 16)

    arb_free2 = [now(PE)]
    sc2 = {'wslot': 0, 'pBig1_free': None, 'pBig2_free': None, 'ms_free': None, 'sq_free': None,
           'sgtm_free': now(DV), 'gn_free': None, 'bc_free': None, 'ddiag_free': None}
    for p in range(2):
        ln_half(p, ln_bufs2, arb_free2)
        EW('q', now(PE))
        EW('q', now(DV))
        EW('q', now(AC))
        trl = None
        for idx, dst in [(0, dt_tm2), (1, dtd_tm2), (2, hi_tm2), (3, lo_tm2)]:
            trl = EI('q', lambda q, i=idx, d=dst, p=p: q.dma_start(d[:, :], dt_dram[p, i, :, :]), SQ, 16)
        trl = EI('q', lambda q, p=p: q.dma_start(alt82[:, :], dt_dram[p, 4, 0:8, 0:128]), SQ, 16)
        EW('v', trl)
        EW('t', trl)
        EW('s', now(DV))
        for z in range(NZ):
            EW('t', pSc_free[0])
            tm = EI('t', lambda t, z=z: t.matmul(pSc[:, :], SELZ8[:, z * 128:(z + 1) * 128],
                                                 alt82[:, :], start=True, stop=True), PE)
            EW('s', tm)
            tse = EI('s', lambda s, z=z: s.activation(lam2[:, z * 128:(z + 1) * 128],
                                                      pSc[:, :], AF.Exp), AC)
            pSc_free[0] = tse
        for g in range(G):
            s2_group(p, g, (Bfm2, Cfm2, sgtm, sgf, E_, scm2, btm2,
                            xz2, xdt2, xdtd2, prevf2, prevbf, hiT8, loT8,
                            rhs1, ea2, t1b, gzb, varcol, rstd3,
                            dt_tm2, dtd_tm2, hi_tm2, lo_tm2, lam2), sc2)
        s2_ffn(p, axb, ft2)
        arb_free2[0] = now(PE)

    s2_scope.close()

    p4_scope = contextlib.ExitStack()
    wo2 = p4_scope.enter_context(nc.sbuf_tensor("wo2", [128, 8192], BF16))
    xf4 = p4_scope.enter_context(nc.sbuf_tensor("xf4", [128, 2 * 512], F32))
    ft4 = p4_scope.enter_context(nc.sbuf_tensor("ft4", [128, 2 * 512], BF16))
    ostg = p4_scope.enter_context(nc.sbuf_tensor("ostg", [128, 2 * 512], F32))
    af4 = p4_scope.enter_context(nc.sbuf_tensor("af4", [128, 2 * 512], BF16))

    wo_slot_ap = [WBUF[:, 0:8192], wo2[:, :]]
    wo_free = [None, None]
    EW('g', wslot_free[0])
    EW('g', wslot_free[1])
    for oc in range(32):
        slot = oc % 2
        EW('g', wo_free[slot])
        tl = EI('g', lambda g, oc=oc, d=wo_slot_ap[slot]: g.dma_start(
            d, w_outT[oc, :, :, :].rearrange("i ic o -> i (ic o)")), DQ, 16)
        EW('q', tl)
        tcache = EI('q', lambda q, oc=oc, d=wo_slot_ap[slot]: q.dma_start(woutbf[oc, :, :], d), SQ, 16)
        wo_free[slot] = tcache
    wo_free = [wo_free[0], wo_free[1]]

    xf_free = [None, None]
    ft_free4 = [None, None]
    ostg_free = [None, None]
    last_out = None
    for tg in range(4):
        EW('q', now(PE))
        tgn = None
        for cc in range(64):
            tgn = EI('q', lambda q, cc=cc, tg=tg: q.dma_start(
                ARB[:, cc * 512:(cc + 1) * 512],
                gn_dram[cc, :, 4 * tg: 4 * tg + 4, :].rearrange("c z l -> c (z l)")), SQ, 16)
        for oc in range(32):
            slot = oc % 2
            wos = wo_slot_ap[slot]
            EW('q', wo_free[slot])
            twl = EI('q', lambda q, oc=oc, d=wos: q.dma_start(d, woutbf[oc, :, :]), SQ, 16)
            EW('q', ft_free4[slot])
            tfl = EI('q', lambda q, oc=oc, tg=tg, sl=slot: q.dma_start(
                ft4[:, sl * 512:(sl + 1) * 512], ffn_dram[oc, :, tg, :]), SQ, 16)
            EW('q', xf_free[slot])
            txl = EI('q', lambda q, oc=oc, tg=tg, sl=slot: q.dma_start(
                xf4[:, sl * 512:(sl + 1) * 512], x_fm[oc * 128:(oc + 1) * 128, tg * 512:(tg + 1) * 512]), SQ, 16)
            pp = [p512a, p512b][slot]
            EW('t', p512_free[slot])
            EW('t', twl)
            EW('t', tgn)
            for ic in range(64):
                EI('t', lambda t, o=pp[:, :], w=wos, ic=ic: t.matmul(
                    o, w[:, ic * 128:(ic + 1) * 128], ARB[:, ic * 512:(ic + 1) * 512],
                    start=(ic == 0), stop=(ic == 63)), PE)
            tmm = now(PE)
            wo_free[slot] = tmm
            EW('v', tmm)
            EW('v', tfl)
            EW('v', txl)
            tv1 = EI('v', lambda v, pp=pp, sl=slot: v.tensor_add(
                af4[:, sl * 512:(sl + 1) * 512], pp[:, :], ft4[:, sl * 512:(sl + 1) * 512]), DV)
            p512_free[slot] = tv1
            ft_free4[slot] = tv1
            EW('v', ostg_free[slot])
            tv2 = EI('v', lambda v, sl=slot: v.tensor_add(
                ostg[:, sl * 512:(sl + 1) * 512], af4[:, sl * 512:(sl + 1) * 512],
                xf4[:, sl * 512:(sl + 1) * 512]), DV)
            xf_free[slot] = tv2
            EW('q', tv2)
            tq = EI('q', lambda q, oc=oc, tg=tg, sl=slot: q.dma_start(
                out_fm[oc * 128:(oc + 1) * 128, tg * 512:(tg + 1) * 512],
                ostg[:, sl * 512:(sl + 1) * 512]), SQ, 16)
            ostg_free[slot] = tq
            last_out = tq
    EW('q', last_out)
    p4_scope.close()

    with nc.Block() as block:
        @block.tensor
        def _(t):
            for ek, fn in PROG:
                if ek == 't':
                    fn(t)

        @block.vector
        def _(v):
            for ek, fn in PROG:
                if ek == 'v':
                    fn(v)

        @block.scalar
        def _(s):
            for ek, fn in PROG:
                if ek == 's':
                    fn(s)

        @block.gpsimd
        def _(g):
            for ek, fn in PROG:
                if ek == 'g':
                    fn(g)

        @block.sync
        def _(q):
            for ek, fn in PROG:
                if ek == 'q':
                    fn(q)

    st.close()
    return nc


def _host_prep(inputs):
    x = np.asarray(inputs["L0_attn_x"], np.float32)
    w_in = np.asarray(inputs["L0_attn_in_proj_w"], np.float32)
    conv_w = np.asarray(inputs["L0_attn_conv_w"], np.float32)
    conv_b = np.asarray(inputs["L0_attn_conv_b"], np.float32)
    dt_bias = np.asarray(inputs["L0_attn_dt_bias"], np.float32)
    A_log = np.asarray(inputs["L0_attn_A_log"], np.float32)
    D_ = np.asarray(inputs["L0_attn_D"], np.float32)
    norm_w = np.asarray(inputs["L0_attn_norm_w"], np.float32)
    w_out = np.asarray(inputs["L0_attn_out_proj_w"], np.float32)
    gate = np.asarray(inputs["L0_ffn_gate"], np.float32)
    kernel_ = np.asarray(inputs["L0_ffn_kernel"], np.float32)
    bias = np.asarray(inputs["L0_ffn_bias"], np.float32)
    lora_A = np.asarray(inputs["L0_ffn_lora_A"], np.float32)
    lora_B = np.asarray(inputs["L0_ffn_lora_B"], np.float32)
    lora_As = np.asarray(inputs["L0_ffn_lora_A_shared"], np.float32)
    lora_Bs = np.asarray(inputs["L0_ffn_lora_B_shared"], np.float32)

    shared = {}
    wt = np.empty((NCHUNK, 128, DC, 128), np.float32)
    for ci in range(NCHUNK):
        r0 = _chunk_rows(ci)
        blk = w_in[r0:r0 + 128, :]
        wt[ci] = blk.reshape(128, DC, 128).transpose(2, 1, 0)
    shared["w_inT"] = np.ascontiguousarray(wt)
    cd = np.zeros((80, 128, 4, 128), np.float32)
    cb = np.empty((128, 80), np.float32)
    for j in range(80):
        chs = _conv_ch_of_chunk(j)
        for k in range(4):
            cd[j, np.arange(128), k, np.arange(128)] = conv_w[chs, 0, k]
        cb[:, j] = conv_b[chs]
    shared["convdiag"] = cd
    shared["convb"] = np.ascontiguousarray(cb)
    shared["w_outT"] = np.ascontiguousarray(
        w_out.reshape(32, 128, 64, 128).transpose(0, 3, 2, 1))
    shared["kern"] = np.ascontiguousarray(
        kernel_.reshape(32, 128, 32, 128).transpose(2, 1, 0, 3))
    la = np.ascontiguousarray(
        lora_A.transpose(2, 0, 1).reshape(32, 128, 16, 32).transpose(2, 1, 0, 3).reshape(16, 128, 1024))
    shared["loraA"] = la
    shared["loraAs"] = np.ascontiguousarray(
        lora_As.transpose(1, 0).reshape(32, 128, 32).transpose(1, 0, 2).reshape(128, 1024))
    shared["loraB"] = np.ascontiguousarray(lora_B.transpose(0, 2, 1))
    shared["loraBs"] = np.ascontiguousarray(lora_Bs.T)
    shared["drow"] = np.ascontiguousarray(D_.reshape(1, 128))
    shared["alog"] = np.ascontiguousarray(A_log.reshape(128, 1))
    shared["dtb"] = np.ascontiguousarray(dt_bias.reshape(128, 1))
    shared["normw"] = np.ascontiguousarray(norm_w.reshape(64, 128).T)
    shared["ffnb"] = np.ascontiguousarray(bias.reshape(32, 128).T)
    shared["c_ibf"] = np.eye(128, dtype=BF)
    sidx = np.arange(128)
    tril = (sidx[:, None] <= sidx[None, :]).astype(BF)
    shared["c_tril"] = tril
    shared["c_ones"] = np.ones((128, 128), BF)
    nex8 = np.zeros((8, 1024), np.float32)
    for hh in range(8):
        nex8[hh, hh * 128:(hh + 1) * 128] = -1.0
    shared["c_negex8"] = nex8.astype(BF)
    selz8 = np.zeros((8, 1024), np.float32)
    for z in range(8):
        selz8[z, z * 128:(z + 1) * 128] = 1.0
    shared["c_selz8"] = selz8.astype(BF)

    in_maps = []
    for c in range(NCORE):
        k, j = c // 2, c % 2
        m = dict(shared)
        xtm = np.zeros((T + 3, D), np.float32)
        if j == 1:
            xtm[0:3] = x[k, 2045:2048]
        xtm[3:] = x[k, 2048 * j: 2048 * (j + 1)]
        m["x_tm"] = xtm
        m["x_fm"] = np.ascontiguousarray(x[k, 2048 * j: 2048 * (j + 1)].T)
        m["gaterow"] = np.ascontiguousarray(gate[c].reshape(1, 16))
        sb_ = np.zeros((128, 8), np.float32)
        if j == 1:
            sb_[:, c - 1] = 1.0
        m["selb"] = sb_
        cv = np.zeros((128, 8), np.float32)
        for _i, _v in enumerate([1e-5, 1.0, -1.0, -0.5, 1.0 / 1024]):
            cv[:, _i] = _v
        m["cvals"] = cv
        in_maps.append(m)
    return in_maps


def kernel(**inputs):
    in_maps = _host_prep(inputs)
    nc = build_nc()
    res = run_bass_kernel_spmd(nc, in_maps, core_ids=list(range(NCORE)))
    out = np.empty((4, 4096, 4096), np.float32)
    for c in range(NCORE):
        k, j = c // 2, c % 2
        out[k, 2048 * j: 2048 * (j + 1)] = res.results[c]["out_fm"].T
    return out


# version 04
# speedup vs baseline: 1.0135x; 1.0135x over previous
import contextlib
import numpy as np
import ml_dtypes
import concourse.bass as bass
import concourse.mybir as mybir
from concourse.bass_utils import run_bass_kernel_spmd

F32 = mybir.dt.float32
BF16 = mybir.dt.bfloat16
AF = mybir.ActivationFunctionType
OP = mybir.AluOpType
AX = mybir.AxisListType

NCORE = 8
T = 2048
TH = 1024
THP = TH + 3
D = 4096
DC = 32
H = 128
G = 8
HPG = 16
NZ = 8
EPS = 1e-5

ROW_GATE = 0
ROW_HS = 8192
ROW_B = 16384
ROW_C = 17408
ROW_DT = 18432
NCHUNK = 145

BF = ml_dtypes.bfloat16


def _chunk_rows(ci):
    if ci == 0:
        return ROW_DT
    g = (ci - 1) // 18
    r = (ci - 1) % 18
    if r == 0:
        return ROW_B + 128 * g
    if r == 1:
        return ROW_C + 128 * g
    if r < 10:
        return ROW_HS + 1024 * g + 128 * (r - 2)
    return ROW_GATE + 1024 * g + 128 * (r - 10)


def CH_B(g):
    return 1 + 18 * g


def CH_C(g):
    return 2 + 18 * g


def CH_HS(g, q):
    return 3 + 18 * g + q


def CH_GATE(g, q):
    return 11 + 18 * g + q


def CONV_B(g):
    return 10 * g


def CONV_C(g):
    return 10 * g + 1


def CONV_HS(g, q):
    return 10 * g + 2 + q


def _conv_ch_of_chunk(j):
    g = j // 10
    r = j % 10
    if r == 0:
        rows = ROW_B + 128 * g
    elif r == 1:
        rows = ROW_C + 128 * g
    else:
        rows = ROW_HS + 1024 * g + 128 * (r - 2)
    if ROW_HS <= rows < ROW_B:
        cbase = rows - ROW_HS
    elif ROW_B <= rows < ROW_C:
        cbase = 8192 + (rows - ROW_B)
    else:
        cbase = 9216 + (rows - ROW_C)
    return np.arange(cbase, cbase + 128)


class Sem:
    def __init__(self, h):
        self.h = h
        self.n = 0


def build_nc(taps=()):
    nc = bass.Bass("TRN2", target_bir_lowering=False)
    taps = set(taps)

    def din(name, shape, dt=F32):
        return nc.dram_tensor(name, shape, dt, kind="ExternalInput").ap()

    x_tm = din("x_tm", [T + 3, D])
    x_fm = din("x_fm", [D, T])
    w_inT = din("w_inT", [NCHUNK, 128, DC, 128])
    convdiag = din("convdiag", [80, 128, 4, 128])
    convb = din("convb", [128, 80])
    w_outT = din("w_outT", [32, 128, 64, 128])
    kern = din("kern", [32, 128, 32, 128])
    loraA = din("loraA", [16, 128, 1024])
    loraAs = din("loraAs", [128, 1024])
    loraB = din("loraB", [16, 32, 4096])
    loraBs = din("loraBs", [32, 4096])
    gaterow = din("gaterow", [1, 16])
    drow = din("drow", [1, 128])
    alog = din("alog", [128, 1])
    dtb_d = din("dtb", [128, 1])
    normw_d = din("normw", [128, 64])
    ffnb_d = din("ffnb", [128, 32])
    selb_d = din("selb", [128, 8])
    c_ibf = din("c_ibf", [128, 128], BF16)
    c_tril = din("c_tril", [128, 128], BF16)
    c_ones = din("c_ones", [128, 128], BF16)
    c_negex8 = din("c_negex8", [8, 1024], BF16)
    c_selz8 = din("c_selz8", [8, 1024], BF16)
    cvals = din("cvals", [128, 8])

    out_fm = nc.dram_tensor("out_fm", [D, T], F32, kind="ExternalOutput").ap()

    def dint(name, shape, dt=BF16):
        return nc.dram_tensor(name, shape, dt, kind="Internal").ap()

    X_dram = dint("X_dram", [G, 16, 128, 1024])
    B_dram = dint("B_dram", [G, 128, T])
    C_dram = dint("C_dram", [G, 128, T])
    dt_dram = dint("dt_dram", [2, 5, 128, TH])
    gn_dram = dint("gn_dram", [64, 128, 16, 128])
    ffn_dram = dint("ffn_dram", [32, 128, 4, 512])
    ex_dram = dint("ex_dram", [128, 8192])
    gath_dram = dint("gath_dram", [NCORE * 128, 8192])
    woutbf = dint("woutbf", [32, 128, 8192])

    tap_aps = {}
    for tname, tshape, tdt in [
        ("tap_xn", [128, DC * THP], BF16),
        ("tap_acum", [128, TH], F32),
        ("tap_prevall", [128, 8192], BF16),
        ("tap_sin", [128, 8192], BF16),
        ("tap_g", [128, 1024], BF16),
        ("tap_ms", [128, 2048], BF16),
        ("tap_gn", [128, 1024], BF16),
        ("tap_ln", [128, 16], F32),
        ("tap_B", [128, 2048], BF16),
        ("tap_X", [128, 1024], BF16),
        ("tap_pz", [128, 2048], F32),
    ]:
        if tname in taps:
            tap_aps[tname] = nc.dram_tensor(tname, tshape, tdt, kind="ExternalOutput").ap()

    PROG = []

    def E(ek, fn):
        PROG.append((ek, fn))

    def EI(ek, fn, sem, amt=1):
        h = sem.h
        PROG.append((ek, lambda e, fn=fn, h=h, amt=amt: fn(e).then_inc(h, amt)))
        sem.n += amt
        return (sem, sem.n)

    def EW(ek, tok):
        if tok is None:
            return
        sem, val = tok
        h = sem.h
        PROG.append((ek, lambda e, h=h, v=val: e.wait_ge(h, v)))

    st = contextlib.ExitStack()
    DQ = Sem(st.enter_context(nc.semaphore("DQ")))
    SQ = Sem(st.enter_context(nc.semaphore("SQ")))
    PE = Sem(st.enter_context(nc.semaphore("PEs")))
    DV = Sem(st.enter_context(nc.semaphore("DVs")))
    AC = Sem(st.enter_context(nc.semaphore("ACs")))
    CC = Sem(st.enter_context(nc.semaphore("CCs")))

    def now(sem):
        return (sem, sem.n)

    def sb(name, shape, dt=BF16):
        return st.enter_context(nc.sbuf_tensor(name, shape, dt))

    IBF = sb("IBF", [128, 128])
    TRIL = sb("TRIL", [128, 128])
    ONES = sb("ONES", [128, 128])
    NEGEX8 = sb("NEGEX8", [8, 1024])
    SELZ8 = sb("SELZ8", [8, 1024])
    CONVB = sb("CONVB", [128, 80], F32)
    NORMW = sb("NORMW", [128, 64], F32)
    FFNB = sb("FFNB", [128, 32], F32)
    DTB = sb("DTB", [128, 1], F32)
    ACOL = sb("ACOL", [128, 1], F32)
    SELB = sb("SELB", [128, 8], F32)
    DROWB = sb("DROWB", [1, 128])
    GROWB = sb("GROWB", [1, 16])
    DBC = sb("DBC", [128, 128], F32)
    GBC = sb("GBC", [128, 16], F32)
    CVALS = sb("CVALS", [128, 8], F32)
    for _i, _v in enumerate([EPS, 1.0, -1.0, -0.5, 1.0 / 1024]):
        nc.const_aps.register(F32, _v, CVALS[:, _i:_i + 1])
    WBUF = sb("WBUF", [128, 2 * 4096])
    ARB = sb("ARB", [128, DC * THP])
    PREVALL = sb("PREVALL", [128, 8192])
    ZZ = sb("ZZ", [128, 128], F32)
    AEFF = [None]
    BST = [None]
    SINALL = [None]

    p512a = st.enter_context(nc.psum_tensor("p512a", [128, 512], F32))
    p512b = st.enter_context(nc.psum_tensor("p512b", [128, 512], F32))
    pBig1 = st.enter_context(nc.psum_tensor("pBig1", [128, 1024], F32))
    pBig2 = st.enter_context(nc.psum_tensor("pBig2", [128, 1024], F32))
    pT = st.enter_context(nc.psum_tensor("pT", [128, 1024], BF16))
    pSmall = st.enter_context(nc.psum_tensor("pSmall", [128, 131], F32))
    pSc = pSmall[:, 0:128]
    pMini = pSmall[:, 128:131]

    wslot_free = [None, None]
    p512_free = [None, None, None, None]
    pT_free = [None]
    pSc_free = [None]

    ln_state = {}

    def load_wchunk(ci, slot):
        EW('g', wslot_free[slot])
        src = w_inT[ci, :, :, :].rearrange("d k c -> d (k c)")
        dst = WBUF[:, slot * 4096:(slot + 1) * 4096]
        return EI('g', lambda g, s=src, d=dst: g.dma_start(d, s), DQ, 16)

    def p0():
        for dst, src in [(IBF, c_ibf), (TRIL, c_tril), (ONES, c_ones),
                         (NEGEX8, c_negex8), (SELZ8, c_selz8),
                         (CONVB, convb), (NORMW, normw_d), (FFNB, ffnb_d),
                         (DTB, dtb_d), (SELB, selb_d), (CVALS, cvals)]:
            EI('q', lambda q, d=dst, s=src: q.dma_start(d[:, :], s[:, :]), SQ, 16)
        EW('s', now(SQ))
        EW('v', now(SQ))
        t1 = EI('g', lambda g: g.dma_start(DROWB[:, :], drow[:, :]), DQ, 16)
        t2 = EI('g', lambda g: g.dma_start(GROWB[:, :], gaterow[:, :]), DQ, 16)
        talog = EI('q', lambda q: q.dma_start(ACOL[:, :], alog[:, :]), SQ, 16)
        EW('s', talog)
        ta = EI('s', lambda s: s.activation(ACOL[:, :], ACOL[:, :], AF.Exp), AC)
        EW('v', ta)
        EI('v', lambda v: v.tensor_scalar_mul(ACOL[:, :], ACOL[:, :], -1.0), DV)
        EI('v', lambda v: v.memset(ZZ[:, :], 0.0), DV)
        EW('t', now(SQ))
        EW('t', t1)
        EW('t', t2)
        tm1 = EI('t', lambda t: t.matmul(pSc[:, :], ONES[0:1, :], DROWB[:, :],
                                         start=True, stop=True), PE)
        EW('v', tm1)
        td = EI('v', lambda v: v.tensor_copy(DBC[:, :], pSc[:, :]), DV)
        EW('t', td)
        tm2 = EI('t', lambda t: t.matmul(pSc[:, 0:16], ONES[0:1, :], GROWB[:, :],
                                         start=True, stop=True), PE)
        EW('v', tm2)
        tg_ = EI('v', lambda v: v.tensor_copy(GBC[:, :], pSc[:, 0:16]), DV)
        pSc_free[0] = tg_

    def p0_lora():
        av = AEFF[0].rearrange("p (dc r2) -> p dc r2", r2=64)
        for n in range(16):
            slot = n % 2
            EW('g', wslot_free[slot])
            tl = EI('g', lambda g, n=n, sl=slot:
                    g.dma_start(WBUF[:, sl * 4096: sl * 4096 + 1024], loraA[n, :, :]), DQ, 16)
            EW('v', tl)
            src = WBUF[:, slot * 4096: slot * 4096 + 1024].rearrange("p (dc r) -> p dc r", r=32)
            if n == 0:
                tv = EI('v', lambda v, s=src:
                        v.tensor_scalar(av[:, :, 0:32], s, GBC[:, 0:1], None, OP.mult), DV)
            else:
                tv = EI('v', lambda v, s=src, n=n:
                        v.scalar_tensor_tensor(av[:, :, 0:32], s, GBC[:, n:n + 1],
                                               av[:, :, 0:32], OP.mult, OP.add), DV)
            wslot_free[slot] = tv
        EW('g', wslot_free[0])
        tl = EI('g', lambda g: g.dma_start(WBUF[:, 0:1024], loraAs[:, :]), DQ, 16)
        EW('v', tl)
        srcs = WBUF[:, 0:1024].rearrange("p (dc r) -> p dc r", r=32)
        tv = EI('v', lambda v, s=srcs: v.tensor_copy(av[:, :, 32:64], s), DV)
        wslot_free[0] = tv
        for n in range(16):
            slot = n % 2
            EW('g', wslot_free[slot])
            tl = EI('g', lambda g, n=n, sl=slot:
                    g.dma_start(WBUF[0:32, sl * 4096: sl * 4096 + 4096], loraB[n, :, :]), DQ, 16)
            EW('v', tl)
            if n == 0:
                tv = EI('v', lambda v, sl=slot:
                        v.tensor_scalar(BST[0][0:32, :], WBUF[0:32, sl * 4096: sl * 4096 + 4096],
                                        GBC[0:32, 0:1], None, OP.mult), DV)
            else:
                tv = EI('v', lambda v, sl=slot, n=n:
                        v.scalar_tensor_tensor(BST[0][0:32, :], WBUF[0:32, sl * 4096: sl * 4096 + 4096],
                                               GBC[0:32, n:n + 1], BST[0][0:32, :], OP.mult, OP.add), DV)
            wslot_free[slot] = tv
        tl = EI('g', lambda g: g.dma_start(BST[0][32:64, :], loraBs[:, :]), DQ, 16)
        wslot_free[0] = None
        wslot_free[1] = None
        return tl

    def ln_half(p, bufs, arb_free):
        xt, xnb, sqd, mcol, sqcol, sq2col, vcol, rstd, nmcol = bufs
        xt_free = ln_state.get(id(bufs[0]))
        xnb_free = ln_state.get(('xnb', id(bufs[0])))
        last = None
        for i in range(9):
            rows = 3 if i == 8 else 128
            r0 = TH * p + (128 * i if i < 8 else 1024)
            col0 = 128 * i if i < 8 else 1024
            EW('q', xt_free)
            tl = EI('q', lambda q, r0=r0, rows=rows:
                    q.dma_start(xt[0:rows, :], x_tm[r0:r0 + rows, 0:2048]), SQ, 16)
            EW('v', tl)
            EW('v', xt_free)
            t1_ = EI('v', lambda v, rows=rows:
                     v.reduce_sum(mcol[0:rows, :], xt[0:rows, :], axis=AX.X), DV)
            EW('s', t1_)
            tsqa = EI('s', lambda s, rows=rows:
                      s.activation(sqd[0:rows, :], xt[0:rows, :], AF.Square,
                                   accum_out=sqcol[0:rows, :]), AC)
            EW('q', tsqa)
            EW('q', t1_)
            tl = EI('q', lambda q, r0=r0, rows=rows:
                    q.dma_start(xt[0:rows, :], x_tm[r0:r0 + rows, 2048:4096]), SQ, 16)
            EW('v', tl)
            EI('v', lambda v, rows=rows:
               v.reduce_sum(vcol[0:rows, :], xt[0:rows, :], axis=AX.X), DV)
            EW('v', now(DV))
            tsum = EI('v', lambda v, rows=rows:
                      v.tensor_add(mcol[0:rows, :], mcol[0:rows, :], vcol[0:rows, :]), DV)
            EW('v', now(DV))
            EI('v', lambda v, rows=rows:
               v.tensor_scalar_mul(mcol[0:rows, :], mcol[0:rows, :], 1.0 / D), DV)
            EW('s', tsum)
            tsq = EI('s', lambda s, rows=rows:
                     s.activation(sqd[0:rows, :], xt[0:rows, :], AF.Square,
                                  accum_out=sq2col[0:rows, :]), AC)
            EW('v', tsq)
            EI('v', lambda v, rows=rows: v.tensor_add(sqcol[0:rows, :], sqcol[0:rows, :], sq2col[0:rows, :]), DV)
            EW('v', now(DV))
            EI('v', lambda v, rows=rows: v.tensor_scalar_mul(sqcol[0:rows, :], sqcol[0:rows, :], 1.0 / D), DV)
            EW('v', now(DV))
            EI('v', lambda v, rows=rows: v.scalar_tensor_tensor(vcol[0:rows, :], mcol[0:rows, :],
                                                                1.0, mcol[0:rows, :], OP.mult, OP.mult), DV)
            EW('v', now(DV))
            tvar = EI('v', lambda v, rows=rows: v.tensor_sub(vcol[0:rows, :], sqcol[0:rows, :], vcol[0:rows, :]), DV)
            EW('v', now(DV))
            EW('s', tvar)
            EI('s', lambda s, rows=rows: s.activation(rstd[0:rows, :], vcol[0:rows, :], AF.Ln, bias=EPS), AC)
            trs = EI('s', lambda s, rows=rows: s.activation(rstd[0:rows, :], rstd[0:rows, :], AF.Exp, scale=-0.5), AC)
            EW('v', trs)
            tnm = EI('v', lambda v, rows=rows: v.scalar_tensor_tensor(nmcol[0:rows, :], mcol[0:rows, :],
                                                                     -1.0, rstd[0:rows, :], OP.mult, OP.mult), DV)
            EW('v', now(DV))
            EW('s', tnm)
            EW('s', xnb_free)
            if "tap_ln" in tap_aps and p == 0 and i == 1 and arb_free is not None and getattr(ln_half, "_tapped", False) is False:
                ln_half._tapped = True
                EW('q', tnm)
                for _j, _c in enumerate([mcol, sqcol, sq2col, vcol, rstd, nmcol]):
                    EI('q', lambda q, j=_j, c=_c: q.dma_start(tap_aps["tap_ln"][:, 2 * j: 2 * j + 1], c[:, :], allow_slow_non_contiguous=True), SQ, 16)
                EW('v', now(SQ))
                EW('s', now(SQ))
            tx1 = EI('s', lambda s, rows=rows: s.activation(xnb[0:rows, 2048:4096], xt[0:rows, :], AF.Identity,
                                                            bias=nmcol[0:rows, :], scale=rstd[0:rows, :]), AC)
            EW('q', tx1)
            EW('q', tvar)
            tl = EI('q', lambda q, r0=r0, rows=rows:
                    q.dma_start(xt[0:rows, :], x_tm[r0:r0 + rows, 0:2048]), SQ, 16)
            EW('s', tl)
            txn = EI('s', lambda s, rows=rows: s.activation(xnb[0:rows, 0:2048], xt[0:rows, :], AF.Identity,
                                                            bias=nmcol[0:rows, :], scale=rstd[0:rows, :]), AC)
            xt_free = txn
            ln_state[id(bufs[0])] = txn
            for bb in range(4):
                EW('t', txn)
                EW('t', pT_free[0])
                ttr = None
                for dd in range(8):
                    dc = bb * 8 + dd
                    ttr = EI('t', lambda t, dc=dc, dd=dd, rows=rows:
                             t.transpose(pT[:, dd * 128: dd * 128 + rows],
                                         xnb[0:rows, dc * 128:(dc + 1) * 128],
                                         IBF[0:rows, 0:rows]), PE)
                EW('v', ttr)
                EW('v', arb_free[0])
                arb_free[0] = None
                dst = ARB.rearrange("x (dc c) -> x dc c", c=THP)[:, bb * 8:(bb + 1) * 8, col0:col0 + rows]
                srcv = pT.rearrange("x (dd c) -> x dd c", c=128)[:, :, 0:rows]
                tev = EI('v', lambda v, d=dst, s=srcv: v.tensor_copy(d, s), DV)
                pT_free[0] = tev
                last = tev
            xnb_free = now(PE)
            ln_state[('xnb', id(bufs[0]))] = xnb_free
        return last

    def pair_psums(sc):
        pr = sc['pair']
        sc['pair'] = 1 - pr
        if pr == 0:
            return pr, (p512a[:, :], p512b[:, :])
        return pr, (pBig1[:, 0:512], pBig1[:, 512:1024])

    def conv_chunk(ci, cj, dest, convin, cdslot, sc, dest_wait):
        slot = sc['wslot']
        sc['wslot'] = 1 - slot
        tload = load_wchunk(ci, slot)
        EW('g', sc['cd_free'])
        tcd = EI('g', lambda g, cj=cj: g.dma_start(
            cdslot[:, :], convdiag[cj, :, :, :].rearrange("i k c -> i (k c)")), DQ, 16)
        pr, pps = pair_psums(sc)
        EW('t', tload)
        for j in range(2):
            pp = pps[j]
            EW('t', p512_free[2 * pr + j])
            for k in range(DC):
                lhsT = WBUF[:, slot * 4096 + k * 128: slot * 4096 + (k + 1) * 128]
                rhs = ARB[:, k * THP + 3 + 512 * j: k * THP + 3 + 512 * (j + 1)]
                EI('t', lambda t, o=pp, l_=lhsT, r=rhs, s=(k == 0), e=(k == DC - 1):
                   t.matmul(o, l_, r, start=s, stop=e), PE)
        EW('t', sc['mini_free'])
        EW('t', pSc_free[0])
        tmini = None
        for k in range(DC):
            lhsT = WBUF[:, slot * 4096 + k * 128: slot * 4096 + (k + 1) * 128]
            rhs = ARB[:, k * THP: k * THP + 3]
            tmini = EI('t', lambda t, l_=lhsT, r=rhs, s=(k == 0), e=(k == DC - 1):
                       t.matmul(pMini[:, :], l_, r, start=s, stop=e), PE)
        wslot_free[slot] = tmini
        EW('v', sc['convin_free'])
        EW('v', tmini)
        EI('v', lambda v: v.tensor_copy(convin[:, 0:3], pMini[:, :]), DV)
        sc['mini_free'] = now(DV)
        pSc_free[0] = now(DV)
        EI('v', lambda v, pp=pps[0]: v.tensor_copy(convin[:, 3:515], pp), DV)
        e1 = EI('v', lambda v, pp=pps[1]: v.tensor_copy(convin[:, 515:1027], pp), DV)
        p512_free[2 * pr] = e1
        p512_free[2 * pr + 1] = e1
        EW('t', tcd)
        EW('t', e1)
        EW('t', sc['conv_free'])
        for j in range(2):
            pp = [pBig2[:, 0:512], pBig2[:, 512:1024]][j]
            for k in range(4):
                lhsT = cdslot[:, k * 128:(k + 1) * 128]
                rhs = convin[:, 512 * j + k: 512 * j + k + 512]
                EI('t', lambda t, o=pp, l_=lhsT, r=rhs, s=(k == 0), e=(k == 3):
                   t.matmul(o, l_, r, start=s, stop=e), PE)
        tconv = now(PE)
        sc['cd_free'] = tconv
        sc['convin_free'] = tconv
        EW('s', tconv)
        if dest_wait is not None:
            for tok in (dest_wait if isinstance(dest_wait, list) else [dest_wait]):
                EW('s', tok)
        EI('s', lambda s, cj=cj: s.activation(dest[:, 0:512], pBig2[:, 0:512], AF.Silu,
                                              bias=CONVB[:, cj:cj + 1]), AC)
        ts = EI('s', lambda s, cj=cj: s.activation(dest[:, 512:1024], pBig2[:, 512:1024], AF.Silu,
                                                   bias=CONVB[:, cj:cj + 1]), AC)
        sc['conv_free'] = ts
        return ts

    def s1_dt(p, b):
        (dtf, scr, acum, dt_tm, dtd_tm, hi_tm, lo_tm, alt8, lam_all,
         hi_fm, lo_fm, dt_fm, dd_fm) = b
        EW('v', s1dt_dump[0])
        EW('v', now(AC))
        EW('s', now(DV))
        slot = sc1['wslot']
        sc1['wslot'] = 1 - slot
        tload = load_wchunk(0, slot)
        pr, pps = pair_psums(sc1)
        EW('t', tload)
        for j in range(2):
            pp = pps[j]
            EW('t', p512_free[2 * pr + j])
            for k in range(DC):
                lhsT = WBUF[:, slot * 4096 + k * 128: slot * 4096 + (k + 1) * 128]
                rhs = ARB[:, k * THP + 3 + 512 * j: k * THP + 3 + 512 * (j + 1)]
                EI('t', lambda t, o=pp, l_=lhsT, r=rhs, s=(k == 0), e=(k == DC - 1):
                   t.matmul(o, l_, r, start=s, stop=e), PE)
        wslot_free[slot] = now(PE)
        EW('v', now(PE))
        EI('v', lambda v, pp=pps[0]: v.tensor_copy(dtf[:, 0:512], pp), DV)
        te = EI('v', lambda v, pp=pps[1]: v.tensor_copy(dtf[:, 512:1024], pp), DV)
        p512_free[2 * pr] = te
        p512_free[2 * pr + 1] = te
        EW('v', now(DV))
        tb = EI('v', lambda v: v.tensor_scalar(dtf[:, :], dtf[:, :], DTB[:, :], None, OP.add), DV)
        EW('s', tb)
        EI('s', lambda s: s.activation(scr[:, :], dtf[:, :], AF.Abs), AC)
        EI('s', lambda s: s.activation(scr[:, :], scr[:, :], AF.Exp, scale=-1.0), AC)
        ta = EI('s', lambda s: s.activation(scr[:, :], scr[:, :], AF.Ln, bias=1.0), AC)
        EW('v', ta)
        EI('v', lambda v: v.tensor_scalar_max(dtf[:, :], dtf[:, :], 0.0), DV)
        EW('v', now(DV))
        EI('v', lambda v: v.tensor_add(dtf[:, :], dtf[:, :], scr[:, :]), DV)
        EW('v', now(DV))
        EI('v', lambda v: v.tensor_scalar_min(dtf[:, :], dtf[:, :], 100.0), DV)
        EW('v', now(DV))
        EI('v', lambda v: v.tensor_scalar(scr[:, :], dtf[:, :], ACOL[:, :], None, OP.mult), DV)
        EW('v', now(DV))
        for z in range(NZ):
            EI('v', lambda v, z=z: v.tensor_tensor_scan(
                acum[:, z * 128:(z + 1) * 128], scr[:, z * 128:(z + 1) * 128],
                ZZ[:, :], 0.0, OP.add, OP.add), DV)
        EW('v', now(DV))
        EI('v', lambda v: v.tensor_copy(hi_fm[:, :], acum[:, :]), DV)
        EW('v', now(DV))
        EI('v', lambda v: v.tensor_tensor(lo_fm[:, :], acum[:, :], hi_fm[:, :], OP.subtract), DV)
        EI('v', lambda v: v.tensor_copy(dt_fm[:, :], dtf[:, :]), DV)
        acv = acum.rearrange("h (z l) -> h z l", l=128)
        alb = acv[:, :, 127:128].broadcast_to([128, NZ, 128])
        tda = EI('v', lambda v, a=acv, b_=alb: v.tensor_tensor(
            scr.rearrange("h (z l) -> h z l", l=128), b_, a, OP.subtract), DV)
        EW('s', tda)
        tdd = EI('s', lambda s: s.activation(dd_fm[:, :], scr[:, :], AF.Exp), AC)
        EW('v', tdd)
        EW('v', now(DV))
        EI('v', lambda v: v.tensor_mul(dd_fm[:, :], dd_fm[:, :], dt_fm[:, :]), DV)
        EW('v', now(DV))
        tal = EI('v', lambda v, a=acv: v.tensor_copy(dt_tm[:, 0:NZ], a[:, :, 127]), DV)
        EW('t', tal)
        EW('t', pT_free[0])
        ttr = EI('t', lambda t: t.transpose(pT[0:NZ, 0:128], dt_tm[:, 0:NZ], IBF[:, :]), PE)
        EW('v', ttr)
        tv = EI('v', lambda v: v.tensor_copy(alt8[:, :], pT[0:NZ, 0:128]), DV)
        pT_free[0] = tv
        for (src, dst) in [(hi_fm, hi_tm), (lo_fm, lo_tm), (dt_fm, dt_tm), (dd_fm, dtd_tm)]:
            EW('t', now(DV))
            EW('t', pT_free[0])
            ttr = None
            for z in range(NZ):
                ttr = EI('t', lambda t, s_=src, z=z:
                         t.transpose(pT[:, z * 128:(z + 1) * 128],
                                     s_[:, z * 128:(z + 1) * 128], IBF[:, :]), PE)
            EW('v', ttr)
            dstv = dst.rearrange("l (z h) -> l z h", h=128)
            srcv = pT.rearrange("l (z h) -> l z h", h=128)
            tv = EI('v', lambda v, d=dstv, s_=srcv: v.tensor_copy(d, s_), DV)
            pT_free[0] = tv
        EW('t', now(DV))
        for z in range(NZ):
            EW('t', pSc_free[0])
            tm = EI('t', lambda t, z=z: t.matmul(pSc[:, :], SELZ8[:, z * 128:(z + 1) * 128],
                                                 alt8[:, :], start=True, stop=True), PE)
            EW('s', tm)
            tse = EI('s', lambda s, z=z: s.activation(lam_all[:, z * 128:(z + 1) * 128],
                                                      pSc[:, :], AF.Exp), AC)
            pSc_free[0] = tse
        EW('q', now(DV))
        EW('q', now(AC))
        for idx, src in [(0, dt_tm), (1, dtd_tm), (2, hi_tm), (3, lo_tm)]:
            EI('q', lambda q, i=idx, s_=src, p=p:
               q.dma_start(dt_dram[p, i, :, :], s_[:, :]), SQ, 16)
        tdump = EI('q', lambda q, p=p: q.dma_start(dt_dram[p, 4, 0:8, 0:128], alt8[:, :]), SQ, 16)
        s1dt_dump[0] = tdump
        if "tap_acum" in tap_aps and p == 0:
            EI('q', lambda q: q.dma_start(tap_aps["tap_acum"][:, :], acum[:, :]), SQ, 16)
        return tdump

    def s1_group(p, g, b, sc):
        (convin, cdslot, Bfm, Cfm, Xtmg, btm, xdtd, prevf,
         dt_tm, dtd_tm, lam_all) = b
        tb = conv_chunk(CH_B(g), CONV_B(g), Bfm, convin, cdslot, sc, sc['bfm_free'])
        EW('q', tb)
        tbd = EI('q', lambda q, g=g, p=p: q.dma_start(B_dram[g, :, TH * p: TH * (p + 1)], Bfm[:, :]), SQ, 16)
        tc = conv_chunk(CH_C(g), CONV_C(g), Cfm, convin, cdslot, sc, sc['cfm_free'])
        EW('q', tc)
        tcd_ = EI('q', lambda q, g=g, p=p: q.dma_start(C_dram[g, :, TH * p: TH * (p + 1)], Cfm[:, :]), SQ, 16)
        cfm_wait = [tcd_]
        for q_ in range(8):
            tx = conv_chunk(CH_HS(g, q_), CONV_HS(g, q_), Cfm, convin, cdslot, sc, cfm_wait)
            EW('t', tx)
            EW('t', pT_free[0])
            ttr = None
            for z in range(NZ):
                ttr = EI('t', lambda t, z=z: t.transpose(pT[:, z * 128:(z + 1) * 128],
                                                         Cfm[:, z * 128:(z + 1) * 128], IBF[:, :]), PE)
            cfm_wait = [ttr]
            EW('v', ttr)
            EW('v', sc['xtm_free'])
            sc['xtm_free'] = None
            EW('v', sc['xdump_free'])
            sc['xdump_free'] = None
            dstv = Xtmg.rearrange("l (z c) -> l z c", c=1024)[:, :, q_ * 128:(q_ + 1) * 128]
            srcv = pT.rearrange("l (z c) -> l z c", c=128)
            tev = EI('v', lambda v, d=dstv, s_=srcv: v.tensor_copy(d, s_), DV)
            pT_free[0] = tev
            EW('q', tev)
            txd_ = EI('q', lambda q, g=g, p=p, q_=q_, dv_=dstv: q.dma_start(
                X_dram[g, NZ * p: NZ * (p + 1), :, q_ * 128:(q_ + 1) * 128].rearrange("z l c -> l z c"),
                dv_), SQ, 16)
            sc['last_xdump'] = txd_
        if p == 0:
            tpi = EI('v', lambda v: v.memset(prevf[:, :], 0.0), DV)
        else:
            EW('v', sc['prevall_ok'])
            tpi = EI('v', lambda v, g=g: v.tensor_copy(prevf[:, :], PREVALL[:, g * 1024:(g + 1) * 1024]), DV)
        tst = None
        for z in range(NZ):
            EW('v', now(DV))
            if "tap_pz" in tap_aps and p == 0 and g == 0 and z == 1:
                EW('q', tpi)
                EI('q', lambda q: q.dma_start(tap_aps["tap_pz"][:, 0:1024], prevf[:, :]), SQ, 16)
                EI('g', lambda g_: g_.dma_start(tap_aps["tap_pz"][:, 1024:1040], lam_all[:, 0:16]), DQ, 16)
                EI('g', lambda g_: g_.dma_start(tap_aps["tap_pz"][:, 1040:1056], dtd_tm[:, 0:16]), DQ, 16)
                EI('g', lambda g_: g_.dma_start(tap_aps["tap_pz"][:, 1056:1072], dt_tm[:, 0:16]), DQ, 16)
                EI('g', lambda g_: g_.dma_start(tap_aps["tap_pz"][:, 1072:2048], xdtd[:, 0:976]), DQ, 16)
                EW('v', now(DQ))
                EW('v', now(SQ))
            EW('v', tst)
            EW('t', pT_free[0])
            ttr = EI('t', lambda t, z=z: t.transpose(pT[:, 0:128], Bfm[:, z * 128:(z + 1) * 128],
                                                     IBF[:, :]), PE)
            EW('v', ttr)
            tbm = EI('v', lambda v: v.tensor_copy(btm[:, :], pT[:, 0:128]), DV)
            pT_free[0] = tbm
            xv = Xtmg[:, z * 1024:(z + 1) * 1024].rearrange("l (hh pp) -> l hh pp", pp=64)
            dd = dtd_tm[:, z * 128 + HPG * g: z * 128 + HPG * (g + 1)].unsqueeze(2).broadcast_to([128, HPG, 64])
            txd = EI('v', lambda v, xv=xv, dd=dd: v.tensor_tensor(
                xdtd.rearrange("l (hh pp) -> l hh pp", pp=64), xv, dd, OP.mult), DV)
            EW('t', txd)
            for j in range(2):
                pp = [p512a, p512b][j]
                EW('t', p512_free[j])
                EI('t', lambda t, o=pp[:, :], j=j: t.matmul(
                    o, btm[:, :], xdtd[:, 512 * j: 512 * (j + 1)], start=True, stop=True), PE)
            tst = now(PE)
            lamv = lam_all[:, z * 128 + HPG * g: z * 128 + HPG * (g + 1)].unsqueeze(2).broadcast_to([128, HPG, 64])
            EW('v', now(DV))
            EI('v', lambda v, lamv=lamv: v.tensor_tensor(
                prevf.rearrange("n (hh pp) -> n hh pp", pp=64),
                prevf.rearrange("n (hh pp) -> n hh pp", pp=64), lamv, OP.mult), DV)
            EW('v', now(DV))
            EW('v', tst)
            EI('v', lambda v: v.tensor_add(prevf[:, 0:512], prevf[:, 0:512], p512a[:, :]), DV)
            tpu = EI('v', lambda v: v.tensor_add(prevf[:, 512:1024], prevf[:, 512:1024], p512b[:, :]), DV)
            p512_free[0] = tpu
            p512_free[1] = tpu
        EW('v', now(DV))
        tsv = EI('v', lambda v, g=g: v.tensor_copy(PREVALL[:, g * 1024:(g + 1) * 1024], prevf[:, :]), DV)
        sc['xtm_free'] = tst
        sc['xdump_free'] = sc['last_xdump']
        sc['bfm_free'] = [now(PE), tbd]
        sc['cfm_free'] = cfm_wait + [sc['last_xdump']]
        return tsv

    def s2_group(p, g, b, sc):
        (Bfm, Cfm, sgtm, sgf, E_, scm, btm, xz, xdt, xdtd,
         prevf, prevbf, hiT8, loT8, rhs1, ea, t1b, gzb,
         varcol, rstd, dt_tm, dtd_tm, hi_tm, lo_tm, lam_all) = b
        EW('q', sc['bc_free'])
        tlb = EI('q', lambda q, g=g, p=p: q.dma_start(Bfm[:, :], B_dram[g, :, TH * p: TH * (p + 1)]), SQ, 16)
        tlc = EI('q', lambda q, g=g, p=p: q.dma_start(Cfm[:, :], C_dram[g, :, TH * p: TH * (p + 1)]), SQ, 16)
        sgf_free = None
        for q_ in range(8):
            slot = sc['wslot']
            sc['wslot'] = 1 - slot
            tload = load_wchunk(CH_GATE(g, q_), slot)
            pr, pps = pair_psums(sc)
            EW('t', tload)
            for j in range(2):
                pp = pps[j]
                EW('t', p512_free[2 * pr + j])
                for k in range(DC):
                    lhsT = WBUF[:, slot * 4096 + k * 128: slot * 4096 + (k + 1) * 128]
                    rhs = ARB[:, k * THP + 3 + 512 * j: k * THP + 3 + 512 * (j + 1)]
                    EI('t', lambda t, o=pp, l_=lhsT, r=rhs, s=(k == 0), e=(k == DC - 1):
                       t.matmul(o, l_, r, start=s, stop=e), PE)
            wslot_free[slot] = now(PE)
            EW('s', now(PE))
            EW('s', sgf_free)
            EI('s', lambda s, pp=pps[0]: s.activation(sgf[:, 0:512], pp, AF.Silu), AC)
            tsg = EI('s', lambda s, pp=pps[1]: s.activation(sgf[:, 512:1024], pp, AF.Silu), AC)
            p512_free[2 * pr] = tsg
            p512_free[2 * pr + 1] = tsg
            EW('t', tsg)
            EW('t', pT_free[0])
            ttr = None
            for z in range(NZ):
                ttr = EI('t', lambda t, z=z: t.transpose(pT[:, z * 128:(z + 1) * 128],
                                                         sgf[:, z * 128:(z + 1) * 128], IBF[:, :]), PE)
            sgf_free = ttr
            EW('v', ttr)
            EW('v', sc['sgtm_free'])
            sc['sgtm_free'] = None
            dstv = sgtm.rearrange("l (z c) -> l z c", c=1024)[:, :, q_ * 128:(q_ + 1) * 128]
            srcv = pT.rearrange("l (z c) -> l z c", c=128)
            tev = EI('v', lambda v, d=dstv, s_=srcv: v.tensor_copy(d, s_), DV)
            pT_free[0] = tev
        if p == 0:
            tpi = EI('v', lambda v, g=g: v.tensor_copy(prevf[:, :], SINALL[0][:, g * 1024:(g + 1) * 1024]), DV)
        else:
            tpi = EI('v', lambda v, g=g: v.tensor_copy(prevf[:, :], PREVALL[:, g * 1024:(g + 1) * 1024]), DV)
        EW('t', tlb)
        EW('t', tlc)
        xz_free = [None, None]
        z_pe = None
        for z in range(NZ):
            gz = NZ * p + z
            xsl = z % 2
            EW('q', xz_free[xsl])
            tlx = EI('q', lambda q, g=g, gz=gz, xsl=xsl: q.dma_start(
                xz[:, xsl * 1024:(xsl + 1) * 1024], X_dram[g, gz, :, :]), SQ, 16)
            xzv = xz[:, xsl * 1024:(xsl + 1) * 1024]
            EW('v', z_pe)
            EW('v', now(AC))
            EW('s', z_pe)
            EW('s', now(DV))
            EW('v', sc['gn_free'])
            EW('t', pT_free[0])
            ttr = EI('t', lambda t, z=z: t.transpose(pT[:, 0:128], Bfm[:, z * 128:(z + 1) * 128],
                                                     IBF[:, :]), PE)
            EW('v', ttr)
            tbm = EI('v', lambda v: v.tensor_copy(btm[:, :], pT[:, 0:128]), DV)
            pT_free[0] = tbm
            EW('t', pSc_free[0])
            tsc = EI('t', lambda t, z=z: t.matmul(pSc[:, :], Bfm[:, z * 128:(z + 1) * 128],
                                                  Cfm[:, z * 128:(z + 1) * 128], start=True, stop=True), PE)
            EW('v', tsc)
            tscm = EI('v', lambda v: v.tensor_tensor(scm[:, :], pSc[:, :], TRIL[:, :], OP.mult), DV)
            pSc_free[0] = tscm
            EW('v', tlx)
            dtv = dt_tm[:, z * 128 + HPG * g: z * 128 + HPG * (g + 1)].unsqueeze(2).broadcast_to([128, HPG, 64])
            ddv = dtd_tm[:, z * 128 + HPG * g: z * 128 + HPG * (g + 1)].unsqueeze(2).broadcast_to([128, HPG, 64])
            xvv = xzv.rearrange("l (hh pp) -> l hh pp", pp=64)
            txdt = EI('v', lambda v, xvv=xvv, dtv=dtv: v.tensor_tensor(
                xdt.rearrange("l (hh pp) -> l hh pp", pp=64), xvv, dtv, OP.mult), DV)
            txdd = EI('v', lambda v, xvv=xvv, ddv=ddv: v.tensor_tensor(
                xdtd.rearrange("l (hh pp) -> l hh pp", pp=64), xvv, ddv, OP.mult), DV)
            dbv = DBC[:, HPG * g: HPG * (g + 1)].unsqueeze(2).broadcast_to([128, HPG, 64])
            txD = EI('v', lambda v, xvv=xvv, dbv=dbv: v.tensor_tensor(
                gzb.rearrange("l (hh pp) -> l hh pp", pp=64), xvv, dbv, OP.mult), DV)
            if z == 0:
                EW('t', p512_free[2])
                EW('t', p512_free[3])
            first_y = True
            for q2 in range(2):
                h0 = z * 128 + HPG * g + 8 * q2
                EW('v', sc['ms_free'])
                sc['ms_free'] = None
                EW('t', pT_free[0])
                EI('t', lambda t, h0=h0: t.transpose(pT[0:8, 0:128], hi_tm[:, h0:h0 + 8], IBF[:, :]), PE)
                t2_ = EI('t', lambda t, h0=h0: t.transpose(pT[0:8, 128:256], lo_tm[:, h0:h0 + 8], IBF[:, :]), PE)
                EW('v', t2_)
                EI('v', lambda v: v.tensor_copy(hiT8[:, :], pT[0:8, 0:128]), DV)
                tcl = EI('v', lambda v: v.tensor_copy(loT8[:, :], pT[0:8, 128:256]), DV)
                pT_free[0] = tcl
                ex = IBF[:, :].unsqueeze(1).broadcast_to([128, 8, 128])
                hb = hi_tm[:, h0:h0 + 8].unsqueeze(2).broadcast_to([128, 8, 128])
                lb = lo_tm[:, h0:h0 + 8].unsqueeze(2).broadcast_to([128, 8, 128])
                tr1 = EI('v', lambda v, ex=ex, hb=hb: v.tensor_tensor(
                    rhs1.rearrange("l (hh c) -> l hh c", c=128), ex, hb, OP.mult), DV)
                EW('t', sc['pBig2_free'])
                sc['pBig2_free'] = None
                EW('t', tcl)
                EW('t', tr1)
                for j in range(2):
                    EI('t', lambda t, j=j: t.matmul(pBig2[:, 512 * j:512 * (j + 1)], hiT8[:, :],
                                                    NEGEX8[:, 512 * j: 512 * (j + 1)], start=True, stop=False), PE)
                    EI('t', lambda t, j=j: t.matmul(pBig2[:, 512 * j:512 * (j + 1)], loT8[:, :],
                                                    NEGEX8[:, 512 * j: 512 * (j + 1)], start=False, stop=False), PE)
                    EI('t', lambda t, j=j: t.matmul(pBig2[:, 512 * j:512 * (j + 1)], ONES[:, :],
                                                    rhs1[:, 512 * j: 512 * (j + 1)], start=False, stop=False), PE)
                th1 = now(PE)
                EW('v', th1)
                tr2 = EI('v', lambda v, ex=ex, lb=lb: v.tensor_tensor(
                    rhs1.rearrange("l (hh c) -> l hh c", c=128), ex, lb, OP.mult), DV)
                EW('t', tr2)
                tmd = None
                for j in range(2):
                    tmd = EI('t', lambda t, j=j: t.matmul(pBig2[:, 512 * j:512 * (j + 1)], ONES[:, :],
                                                          rhs1[:, 512 * j: 512 * (j + 1)], start=False, stop=True), PE)
                EW('v', tmd)
                EW('v', sc['sq_free'])
                sc['sq_free'] = None
                tmn = EI('v', lambda v: v.tensor_scalar_min(E_[:, :], pBig2[:, :], 0.0), DV)
                sc['pBig2_free'] = tmn
                EW('s', tmn)
                tex = EI('s', lambda s: s.activation(E_[:, :], E_[:, :], AF.Exp), AC)
                EW('v', tex)
                EW('v', tscm)
                scb = scm[:, :].unsqueeze(1).broadcast_to([128, 8, 128])
                tms = EI('v', lambda v, scb=scb: v.tensor_tensor(
                    E_.rearrange("s (hh c) -> s hh c", c=128),
                    E_.rearrange("s (hh c) -> s hh c", c=128), scb, OP.mult), DV)
                if first_y:
                    EW('t', sc['pBig1_free'])
                    sc['pBig1_free'] = None
                    first_y = False
                EW('t', tms)
                EW('t', txdt)
                EW('t', txD)
                for hh8 in range(8):
                    hh = 8 * q2 + hh8
                    EI('t', lambda t, hh=hh, hh8=hh8: t.matmul(pBig1[:, hh * 64:(hh + 1) * 64],
                                                               E_[:, hh8 * 128:(hh8 + 1) * 128],
                                                               xdt[:, hh * 64:(hh + 1) * 64], start=True, stop=False), PE)
                    EI('t', lambda t, hh=hh: t.matmul(pBig1[:, hh * 64:(hh + 1) * 64],
                                                      IBF[:, :],
                                                      gzb[:, hh * 64:(hh + 1) * 64], start=False, stop=True), PE)
                sc['ms_free'] = now(PE)
            tyd = now(PE)
            EW('v', tpi)
            tpb = EI('v', lambda v: v.tensor_copy(prevbf[:, :], prevf[:, :]), DV)
            EW('t', sc['pBig2_free'])
            sc['pBig2_free'] = None
            EW('t', tpb)
            for j in range(2):
                EI('t', lambda t, z=z, j=j: t.matmul(pBig2[:, 512 * j:512 * (j + 1)],
                                                     Cfm[:, z * 128:(z + 1) * 128],
                                                     prevbf[:, 512 * j:512 * (j + 1)], start=True, stop=True), PE)
            tya = now(PE)
            EW('t', txdd)
            for j in range(2):
                pp = [p512a, p512b][j]
                EW('t', p512_free[j])
                EI('t', lambda t, o=pp[:, :], j=j: t.matmul(
                    o, btm[:, :], xdtd[:, 512 * j: 512 * (j + 1)], start=True, stop=True), PE)
            tst = now(PE)
            lamv = lam_all[:, z * 128 + HPG * g: z * 128 + HPG * (g + 1)].unsqueeze(2).broadcast_to([128, HPG, 64])
            EW('v', tya)
            EW('v', now(DV))
            EI('v', lambda v, lamv=lamv: v.tensor_tensor(
                prevf.rearrange("n (hh pp) -> n hh pp", pp=64),
                prevf.rearrange("n (hh pp) -> n hh pp", pp=64), lamv, OP.mult), DV)
            EW('v', now(DV))
            EW('v', tst)
            EI('v', lambda v: v.tensor_add(prevf[:, 0:512], prevf[:, 0:512], p512a[:, :]), DV)
            tpu = EI('v', lambda v: v.tensor_add(prevf[:, 512:1024], prevf[:, 512:1024], p512b[:, :]), DV)
            p512_free[0] = tpu
            p512_free[1] = tpu
            tpi = tpu
            tea = EI('s', lambda s, z=z, g=g: s.activation(
                ea[:, :], hi_tm[:, z * 128 + HPG * g: z * 128 + HPG * (g + 1)], AF.Exp), AC)
            EW('v', tea)
            eav = ea[:, :].unsqueeze(2).broadcast_to([128, HPG, 64])
            tt1 = EI('v', lambda v, eav=eav: v.tensor_tensor(
                t1b.rearrange("l (hh pp) -> l hh pp", pp=64),
                pBig2.rearrange("l (hh pp) -> l hh pp", pp=64), eav, OP.mult), DV)
            sc['pBig2_free'] = tt1
            EW('v', tyd)
            EW('v', now(DV))
            tt2 = EI('v', lambda v: v.tensor_add(gzb[:, :], pBig1[:, :], t1b[:, :]), DV)
            sc['pBig1_free'] = tt2
            EW('v', now(DV))
            tgt = EI('v', lambda v, z=z: v.tensor_mul(gzb[:, :], gzb[:, :],
                                                      sgtm[:, z * 1024:(z + 1) * 1024]), DV)
            if "tap_g" in tap_aps and p == 0 and g == 0 and z == 0:
                EW('q', tgt)
                EI('q', lambda q: q.dma_start(tap_aps["tap_g"][:, :], gzb[:, :]), SQ, 16)
            EW('s', tgt)
            tsqg = EI('s', lambda s: s.activation(E_[:, :], gzb[:, :], AF.Square,
                                                  accum_out=varcol[:, :]), AC)
            sc['sq_free'] = tsqg
            EI('s', lambda s: s.activation(rstd[:, :], varcol[:, :], AF.Ln,
                                           bias=EPS, scale=1.0 / 1024), AC)
            tsr = EI('s', lambda s: s.activation(rstd[:, :], rstd[:, :], AF.Exp, scale=-0.5), AC)
            EW('v', tsr)
            tgn = EI('v', lambda v: v.tensor_scalar(gzb[:, :], gzb[:, :], rstd[:, :], None, OP.mult), DV)
            EW('t', tgn)
            EW('t', pT_free[0])
            ttr = None
            for cc in range(8):
                ttr = EI('t', lambda t, cc=cc: t.transpose(pT[:, cc * 128:(cc + 1) * 128],
                                                           gzb[:, cc * 128:(cc + 1) * 128], IBF[:, :]), PE)
            EW('v', ttr)
            tev = None
            for cc in range(8):
                tev = EI('v', lambda v, cc=cc, g=g: v.tensor_scalar(
                    t1b[:, cc * 128:(cc + 1) * 128], pT[:, cc * 128:(cc + 1) * 128],
                    NORMW[:, 8 * g + cc: 8 * g + cc + 1], None, OP.mult), DV)
            pT_free[0] = tev
            EW('q', tev)
            tdq = None
            for cc in range(8):
                tdq = EI('q', lambda q, cc=cc, g=g, gz=gz: q.dma_start(
                    gn_dram[8 * g + cc, :, gz, :], t1b[:, cc * 128:(cc + 1) * 128]), SQ, 16)
            sc['gn_free'] = tdq
            if "tap_gn" in tap_aps and p == 0 and g == 0 and z == 0:
                EI('q', lambda q: q.dma_start(tap_aps["tap_gn"][:, :], t1b[:, :]), SQ, 16)
            xz_free[xsl] = txD
            z_pe = now(PE)
        EW('v', now(DV))
        tsv = EI('v', lambda v, g=g: v.tensor_copy(PREVALL[:, g * 1024:(g + 1) * 1024], prevf[:, :]), DV)
        if sc['pBig1_free'] is not None:
            p512_free[2] = sc['pBig1_free']
            p512_free[3] = sc['pBig1_free']
        sc['bc_free'] = z_pe
        sc['ddiag_free'] = z_pe
        return tsv

    def s2_ffn(p, axb, ft):
        for j in range(2):
            EW('t', p512_free[0])
            tmm = None
            for dc in range(DC):
                lhsT = AEFF[0][:, dc * 64:(dc + 1) * 64]
                rhs = ARB[:, dc * THP + 3 + 512 * j: dc * THP + 3 + 512 * (j + 1)]
                tmm = EI('t', lambda t, l_=lhsT, r=rhs, s=(dc == 0), e=(dc == DC - 1):
                         t.matmul(p512a[0:64, :], l_, r, start=s, stop=e), PE)
            EW('v', tmm)
            EW('v', p == 1 and ffn_axfree[0] or None)
            tev = EI('v', lambda v, j=j: v.tensor_copy(axb[0:64, 512 * j: 512 * (j + 1)], p512a[0:64, :]), DV)
            p512_free[0] = tev
        axtok = now(DV)
        ft_free = [None, None]
        for oc in range(32):
            slot = oc % 2
            EW('g', wslot_free[slot])
            tload = EI('g', lambda g, oc=oc, sl=slot: g.dma_start(
                WBUF[:, sl * 4096:(sl + 1) * 4096],
                kern[oc, :, :, :].rearrange("d k c -> d (k c)")), DQ, 16)
            pr, pps = pair_psums(sc2)
            EW('t', tload)
            EW('t', axtok)
            for j in range(2):
                pp = pps[j]
                EW('t', p512_free[2 * pr + j])
                for k in range(DC):
                    lhsT = WBUF[:, slot * 4096 + k * 128: slot * 4096 + (k + 1) * 128]
                    rhs = ARB[:, k * THP + 3 + 512 * j: k * THP + 3 + 512 * (j + 1)]
                    EI('t', lambda t, o=pp, l_=lhsT, r=rhs, s=(k == 0):
                       t.matmul(o, l_, r, start=s, stop=False), PE)
                EI('t', lambda t, o=pp, oc=oc, j=j: t.matmul(
                    o, BST[0][:, oc * 128:(oc + 1) * 128], axb[0:64, 512 * j: 512 * (j + 1)],
                    start=False, stop=True), PE)
            tmm = now(PE)
            wslot_free[slot] = tmm
            EW('s', tmm)
            EW('s', ft_free[pr])
            tev = None
            for j in range(2):
                tev = EI('s', lambda s, pp=pps[j], oc=oc, pr=pr, j=j: s.activation(
                    ft[:, pr * 1024 + 512 * j: pr * 1024 + 512 * (j + 1)], pp, AF.Identity,
                    bias=FFNB[:, oc:oc + 1]), AC)
            p512_free[2 * pr] = tev
            p512_free[2 * pr + 1] = tev
            EW('q', tev)
            tq = EI('q', lambda q, oc=oc, p=p, pr=pr: q.dma_start(
                ffn_dram[oc, :, 2 * p: 2 * p + 2, :].rearrange("c tg t -> c (tg t)"),
                ft[:, pr * 1024:(pr + 1) * 1024]), SQ, 16)
            ft_free[pr] = tq
        ffn_axfree[0] = now(PE)
        return ft_free[pr]

    ffn_axfree = [None]

    p0()

    s1_scope = contextlib.ExitStack()

    def asb(name, shape, dt=BF16):
        return s1_scope.enter_context(nc.sbuf_tensor(name, shape, dt))

    xt = asb("xt1", [128, 2048], F32)
    xnb = asb("xnb1", [128, D])
    sqd0 = asb("sqd0", [128, 2048])
    mcol = asb("mcol", [128, 1], F32)
    sqcol = asb("sqcol", [128, 1], F32)
    vcol = asb("vcol", [128, 1], F32)
    rstd0 = asb("rstd0", [128, 1], F32)
    nmcol = asb("nmcol", [128, 1], F32)
    sq2col = asb("sq2col", [128, 1], F32)
    ln_bufs1 = (xt, xnb, sqd0, mcol, sqcol, sq2col, vcol, rstd0, nmcol)
    dtf = asb("dtf", [128, TH], F32)
    scr = asb("scr", [128, TH], F32)
    acum = asb("acum", [128, TH], F32)
    dt_tm = asb("dt_tm1", [128, TH])
    dtd_tm = asb("dtd_tm1", [128, TH])
    hi_tm = asb("hi_tm1", [128, TH])
    lo_tm = asb("lo_tm1", [128, TH])
    hi_fm = asb("hi_fm", [128, TH])
    lo_fm = asb("lo_fm", [128, TH])
    dt_fm = asb("dt_fm", [128, TH])
    dd_fm = asb("dd_fm", [128, TH])
    alt8 = asb("alt8", [8, 128])
    lam_all = asb("lam_all1", [128, TH])
    convin = asb("convin1", [128, THP])
    cdslot = asb("cdslot", [128, 512])
    Bfm = asb("Bfm1", [128, TH])
    Cfm = asb("Cfm1", [128, TH])
    Xtmg = asb("Xtmg", [128, NZ * 1024])
    btm = asb("btm1", [128, 128])
    xdtd = asb("xdtd1", [128, 1024])
    prevf = asb("prevf1", [128, 1024], F32)

    arb_free = [None]
    s1dt_dump = [None]
    sc1 = {'wslot': 0, 'pair': 0, 'conv_free': None, 'cd_free': None, 'mini_free': None, 'convin_free': None,
           'xtm_free': None, 'xdump_free': None, 'last_xdump': None,
           'bfm_free': None, 'cfm_free': None, 'prevall_ok': None}

    for p in range(2):
        ln_half(p, ln_bufs1, arb_free)
        s1_dt(p, (dtf, scr, acum, dt_tm, dtd_tm, hi_tm, lo_tm, alt8, lam_all,
                  hi_fm, lo_fm, dt_fm, dd_fm))
        if "tap_xn" in tap_aps and p == 0:
            EW('q', now(DV))
            EI('q', lambda q: q.dma_start(tap_aps["tap_xn"][:, :], ARB[:, :]), SQ, 16)
        for g in range(G):
            tsv = s1_group(p, g, (convin, cdslot, Bfm, Cfm, Xtmg, btm, xdtd,
                                  prevf, dt_tm, dtd_tm, lam_all), sc1)
        sc1['prevall_ok'] = tsv
        arb_free[0] = now(PE)
    EW('q', now(DV))
    if "tap_B" in tap_aps:
        EI('q', lambda q: q.dma_start(tap_aps["tap_B"][:, :], B_dram[0, :, :]), SQ, 16)
    if "tap_X" in tap_aps:
        EI('q', lambda q: q.dma_start(tap_aps["tap_X"][:, :], X_dram[0, 0, :, :]), SQ, 16)
    tex = EI('q', lambda q: q.dma_start(ex_dram[:, :], PREVALL[:, :]), SQ, 16)
    if "tap_prevall" in tap_aps:
        EI('q', lambda q: q.dma_start(tap_aps["tap_prevall"][:, :], PREVALL[:, :]), SQ, 16)
    EW('g', tex)
    tcc = EI('g', lambda g: g.collective_compute(
        "AllGather", OP.bypass, replica_groups=[list(range(NCORE))],
        ins=[ex_dram.tensor.ap().opt()], outs=[gath_dram.tensor.ap().opt()]), CC, 1)

    s1_scope.close()

    s2_scope = contextlib.ExitStack()

    def bsb(name, shape, dt=BF16):
        return s2_scope.enter_context(nc.sbuf_tensor(name, shape, dt))

    Gbuf = bsb("Gbuf", [128, 8192])
    AEFF[0] = bsb("AEFF", [128, 2048])
    BST[0] = bsb("BST", [64, 4096])
    sgtm = Gbuf
    xt2 = bsb("xt2", [128, 2048], F32)
    xnb2 = bsb("xnb2", [128, D])
    sqd2 = bsb("sqd2", [128, 2048])
    mcol2 = bsb("mcol2", [128, 1], F32)
    sqcol2 = bsb("sqcol2", [128, 1], F32)
    vcol2 = bsb("vcol2", [128, 1], F32)
    rstd2 = bsb("rstd2", [128, 1], F32)
    nmcol2 = bsb("nmcol2", [128, 1], F32)
    sq2col2 = bsb("sq2col2", [128, 1], F32)
    ln_bufs2 = (xt2, xnb2, sqd2, mcol2, sqcol2, sq2col2, vcol2, rstd2, nmcol2)
    dt_tm2 = bsb("dt_tm2", [128, TH])
    dtd_tm2 = bsb("dtd_tm2", [128, TH])
    hi_tm2 = bsb("hi_tm2", [128, TH])
    lo_tm2 = bsb("lo_tm2", [128, TH])
    alt82 = bsb("alt82", [8, 128])
    lam2 = bsb("lam2", [128, TH])
    Bfm2 = bsb("Bfm2", [128, TH])
    Cfm2 = bsb("Cfm2", [128, TH])
    sgf = bsb("sgf", [128, TH])
    E_ = bsb("E_", [128, 1024])
    scm2 = bsb("scm2", [128, 128])
    btm2 = bsb("btm2", [128, 128])
    xz2 = bsb("xz2", [128, 2048])
    xdt2 = bsb("xdt2", [128, 1024])
    xdtd2 = bsb("xdtd2", [128, 1024])
    prevf2 = bsb("prevf2", [128, 1024], F32)
    prevbf = bsb("prevbf", [128, 1024])
    hiT8 = bsb("hiT8", [8, 128])
    loT8 = bsb("loT8", [8, 128])
    rhs1 = bsb("rhs1", [128, 1024])
    ea2 = bsb("ea2", [128, 16])
    t1b = bsb("t1b", [128, 1024])
    gzb = bsb("gzb", [128, 1024])
    varcol = bsb("varcol", [128, 1], F32)
    rstd3 = bsb("rstd3", [128, 1], F32)
    sin_scope = contextlib.ExitStack()
    SINALL[0] = sin_scope.enter_context(nc.sbuf_tensor("SINALL", [128, 8192], BF16))
    sin_scope.close()
    axb = bsb("axb", [64, 1024])
    ft2 = bsb("ft2", [128, 2048])

    p0_lora()
    tmz = EI('v', lambda v: v.memset(SINALL[0][:, :], 0.0), DV)
    EW('g', tcc)
    for r in range(NCORE):
        tlg = EI('g', lambda g, r=r: g.dma_start(Gbuf[:, :], gath_dram[128 * r:128 * (r + 1), :]), DQ, 16)
        EW('v', tlg)
        tmz = EI('v', lambda v, r=r: v.scalar_tensor_tensor(
            SINALL[0][:, :], Gbuf[:, :], SELB[:, r:r + 1], SINALL[0][:, :], OP.mult, OP.add), DV)
        EW('g', tmz)
    if "tap_sin" in tap_aps:
        EW('q', tmz)
        EI('q', lambda q: q.dma_start(tap_aps["tap_sin"][:, :], SINALL[0][:, :]), SQ, 16)

    arb_free2 = [now(PE)]
    sc2 = {'wslot': 0, 'pair': 0, 'conv_free': None, 'pBig1_free': None, 'pBig2_free': now(AC), 'ms_free': None, 'sq_free': None,
           'sgtm_free': now(DV), 'gn_free': None, 'bc_free': None, 'ddiag_free': None}
    for p in range(2):
        ln_half(p, ln_bufs2, arb_free2)
        EW('q', now(PE))
        EW('q', now(DV))
        EW('q', now(AC))
        trl = None
        for idx, dst in [(0, dt_tm2), (1, dtd_tm2), (2, hi_tm2), (3, lo_tm2)]:
            trl = EI('q', lambda q, i=idx, d=dst, p=p: q.dma_start(d[:, :], dt_dram[p, i, :, :]), SQ, 16)
        trl = EI('q', lambda q, p=p: q.dma_start(alt82[:, :], dt_dram[p, 4, 0:8, 0:128]), SQ, 16)
        EW('v', trl)
        EW('t', trl)
        EW('s', now(DV))
        for z in range(NZ):
            EW('t', pSc_free[0])
            tm = EI('t', lambda t, z=z: t.matmul(pSc[:, :], SELZ8[:, z * 128:(z + 1) * 128],
                                                 alt82[:, :], start=True, stop=True), PE)
            EW('s', tm)
            tse = EI('s', lambda s, z=z: s.activation(lam2[:, z * 128:(z + 1) * 128],
                                                      pSc[:, :], AF.Exp), AC)
            pSc_free[0] = tse
        for g in range(G):
            s2_group(p, g, (Bfm2, Cfm2, sgtm, sgf, E_, scm2, btm2,
                            xz2, xdt2, xdtd2, prevf2, prevbf, hiT8, loT8,
                            rhs1, ea2, t1b, gzb, varcol, rstd3,
                            dt_tm2, dtd_tm2, hi_tm2, lo_tm2, lam2), sc2)
        s2_ffn(p, axb, ft2)
        arb_free2[0] = now(PE)

    s2_scope.close()

    p4_scope = contextlib.ExitStack()
    wo2 = p4_scope.enter_context(nc.sbuf_tensor("wo2", [128, 8192], BF16))
    xf4 = p4_scope.enter_context(nc.sbuf_tensor("xf4", [128, 2 * 512], F32))
    ft4 = p4_scope.enter_context(nc.sbuf_tensor("ft4", [128, 2 * 512], BF16))
    ostg = p4_scope.enter_context(nc.sbuf_tensor("ostg", [128, 2 * 512], F32))
    af4 = p4_scope.enter_context(nc.sbuf_tensor("af4", [128, 2 * 512], BF16))

    wo_slot_ap = [WBUF[:, 0:8192], wo2[:, :]]
    wo_free = [None, None]
    EW('g', wslot_free[0])
    EW('g', wslot_free[1])
    for oc in range(32):
        slot = oc % 2
        EW('g', wo_free[slot])
        tl = EI('g', lambda g, oc=oc, d=wo_slot_ap[slot]: g.dma_start(
            d, w_outT[oc, :, :, :].rearrange("i ic o -> i (ic o)")), DQ, 16)
        EW('q', tl)
        tcache = EI('q', lambda q, oc=oc, d=wo_slot_ap[slot]: q.dma_start(woutbf[oc, :, :], d), SQ, 16)
        wo_free[slot] = tcache
    wo_free = [wo_free[0], wo_free[1]]

    xf_free = [None, None]
    ft_free4 = [None, None]
    ostg_free = [None, None]
    last_out = None
    for tg in range(4):
        EW('q', now(PE))
        tgn = None
        for cc in range(64):
            tgn = EI('q', lambda q, cc=cc, tg=tg: q.dma_start(
                ARB[:, cc * 512:(cc + 1) * 512],
                gn_dram[cc, :, 4 * tg: 4 * tg + 4, :].rearrange("c z l -> c (z l)")), SQ, 16)
        for oc in range(32):
            slot = oc % 2
            wos = wo_slot_ap[slot]
            EW('q', wo_free[slot])
            twl = EI('q', lambda q, oc=oc, d=wos: q.dma_start(d, woutbf[oc, :, :]), SQ, 16)
            EW('q', ft_free4[slot])
            tfl = EI('q', lambda q, oc=oc, tg=tg, sl=slot: q.dma_start(
                ft4[:, sl * 512:(sl + 1) * 512], ffn_dram[oc, :, tg, :]), SQ, 16)
            EW('q', xf_free[slot])
            txl = EI('q', lambda q, oc=oc, tg=tg, sl=slot: q.dma_start(
                xf4[:, sl * 512:(sl + 1) * 512], x_fm[oc * 128:(oc + 1) * 128, tg * 512:(tg + 1) * 512]), SQ, 16)
            pp = [p512a, p512b][slot]
            EW('t', p512_free[slot])
            EW('t', twl)
            EW('t', tgn)
            for ic in range(64):
                EI('t', lambda t, o=pp[:, :], w=wos, ic=ic: t.matmul(
                    o, w[:, ic * 128:(ic + 1) * 128], ARB[:, ic * 512:(ic + 1) * 512],
                    start=(ic == 0), stop=(ic == 63)), PE)
            tmm = now(PE)
            wo_free[slot] = tmm
            EW('v', tmm)
            EW('v', tfl)
            EW('v', txl)
            tv1 = EI('v', lambda v, pp=pp, sl=slot: v.tensor_add(
                af4[:, sl * 512:(sl + 1) * 512], pp[:, :], ft4[:, sl * 512:(sl + 1) * 512]), DV)
            p512_free[slot] = tv1
            ft_free4[slot] = tv1
            EW('v', ostg_free[slot])
            tv2 = EI('v', lambda v, sl=slot: v.tensor_add(
                ostg[:, sl * 512:(sl + 1) * 512], af4[:, sl * 512:(sl + 1) * 512],
                xf4[:, sl * 512:(sl + 1) * 512]), DV)
            xf_free[slot] = tv2
            EW('q', tv2)
            tq = EI('q', lambda q, oc=oc, tg=tg, sl=slot: q.dma_start(
                out_fm[oc * 128:(oc + 1) * 128, tg * 512:(tg + 1) * 512],
                ostg[:, sl * 512:(sl + 1) * 512]), SQ, 16)
            ostg_free[slot] = tq
            last_out = tq
    EW('q', last_out)
    p4_scope.close()

    with nc.Block() as block:
        @block.tensor
        def _(t):
            for ek, fn in PROG:
                if ek == 't':
                    fn(t)

        @block.vector
        def _(v):
            for ek, fn in PROG:
                if ek == 'v':
                    fn(v)

        @block.scalar
        def _(s):
            for ek, fn in PROG:
                if ek == 's':
                    fn(s)

        @block.gpsimd
        def _(g):
            for ek, fn in PROG:
                if ek == 'g':
                    fn(g)

        @block.sync
        def _(q):
            for ek, fn in PROG:
                if ek == 'q':
                    fn(q)

    st.close()
    return nc


def _host_prep(inputs):
    x = np.asarray(inputs["L0_attn_x"], np.float32)
    w_in = np.asarray(inputs["L0_attn_in_proj_w"], np.float32)
    conv_w = np.asarray(inputs["L0_attn_conv_w"], np.float32)
    conv_b = np.asarray(inputs["L0_attn_conv_b"], np.float32)
    dt_bias = np.asarray(inputs["L0_attn_dt_bias"], np.float32)
    A_log = np.asarray(inputs["L0_attn_A_log"], np.float32)
    D_ = np.asarray(inputs["L0_attn_D"], np.float32)
    norm_w = np.asarray(inputs["L0_attn_norm_w"], np.float32)
    w_out = np.asarray(inputs["L0_attn_out_proj_w"], np.float32)
    gate = np.asarray(inputs["L0_ffn_gate"], np.float32)
    kernel_ = np.asarray(inputs["L0_ffn_kernel"], np.float32)
    bias = np.asarray(inputs["L0_ffn_bias"], np.float32)
    lora_A = np.asarray(inputs["L0_ffn_lora_A"], np.float32)
    lora_B = np.asarray(inputs["L0_ffn_lora_B"], np.float32)
    lora_As = np.asarray(inputs["L0_ffn_lora_A_shared"], np.float32)
    lora_Bs = np.asarray(inputs["L0_ffn_lora_B_shared"], np.float32)

    shared = {}
    wt = np.empty((NCHUNK, 128, DC, 128), np.float32)
    for ci in range(NCHUNK):
        r0 = _chunk_rows(ci)
        blk = w_in[r0:r0 + 128, :]
        wt[ci] = blk.reshape(128, DC, 128).transpose(2, 1, 0)
    shared["w_inT"] = np.ascontiguousarray(wt)
    cd = np.zeros((80, 128, 4, 128), np.float32)
    cb = np.empty((128, 80), np.float32)
    for j in range(80):
        chs = _conv_ch_of_chunk(j)
        for k in range(4):
            cd[j, np.arange(128), k, np.arange(128)] = conv_w[chs, 0, k]
        cb[:, j] = conv_b[chs]
    shared["convdiag"] = cd
    shared["convb"] = np.ascontiguousarray(cb)
    shared["w_outT"] = np.ascontiguousarray(
        w_out.reshape(32, 128, 64, 128).transpose(0, 3, 2, 1))
    shared["kern"] = np.ascontiguousarray(
        kernel_.reshape(32, 128, 32, 128).transpose(2, 1, 0, 3))
    la = np.ascontiguousarray(
        lora_A.transpose(2, 0, 1).reshape(32, 128, 16, 32).transpose(2, 1, 0, 3).reshape(16, 128, 1024))
    shared["loraA"] = la
    shared["loraAs"] = np.ascontiguousarray(
        lora_As.transpose(1, 0).reshape(32, 128, 32).transpose(1, 0, 2).reshape(128, 1024))
    shared["loraB"] = np.ascontiguousarray(lora_B.transpose(0, 2, 1))
    shared["loraBs"] = np.ascontiguousarray(lora_Bs.T)
    shared["drow"] = np.ascontiguousarray(D_.reshape(1, 128))
    shared["alog"] = np.ascontiguousarray(A_log.reshape(128, 1))
    shared["dtb"] = np.ascontiguousarray(dt_bias.reshape(128, 1))
    shared["normw"] = np.ascontiguousarray(norm_w.reshape(64, 128).T)
    shared["ffnb"] = np.ascontiguousarray(bias.reshape(32, 128).T)
    shared["c_ibf"] = np.eye(128, dtype=BF)
    sidx = np.arange(128)
    tril = (sidx[:, None] <= sidx[None, :]).astype(BF)
    shared["c_tril"] = tril
    shared["c_ones"] = np.ones((128, 128), BF)
    nex8 = np.zeros((8, 1024), np.float32)
    for hh in range(8):
        nex8[hh, hh * 128:(hh + 1) * 128] = -1.0
    shared["c_negex8"] = nex8.astype(BF)
    selz8 = np.zeros((8, 1024), np.float32)
    for z in range(8):
        selz8[z, z * 128:(z + 1) * 128] = 1.0
    shared["c_selz8"] = selz8.astype(BF)

    in_maps = []
    for c in range(NCORE):
        k, j = c // 2, c % 2
        m = dict(shared)
        xtm = np.zeros((T + 3, D), np.float32)
        if j == 1:
            xtm[0:3] = x[k, 2045:2048]
        xtm[3:] = x[k, 2048 * j: 2048 * (j + 1)]
        m["x_tm"] = xtm
        m["x_fm"] = np.ascontiguousarray(x[k, 2048 * j: 2048 * (j + 1)].T)
        m["gaterow"] = np.ascontiguousarray(gate[c].reshape(1, 16))
        sb_ = np.zeros((128, 8), np.float32)
        if j == 1:
            sb_[:, c - 1] = 1.0
        m["selb"] = sb_
        cv = np.zeros((128, 8), np.float32)
        for _i, _v in enumerate([1e-5, 1.0, -1.0, -0.5, 1.0 / 1024]):
            cv[:, _i] = _v
        m["cvals"] = cv
        in_maps.append(m)
    return in_maps


def kernel(**inputs):
    in_maps = _host_prep(inputs)
    nc = build_nc()
    res = run_bass_kernel_spmd(nc, in_maps, core_ids=list(range(NCORE)))
    out = np.empty((4, 4096, 4096), np.float32)
    for c in range(NCORE):
        k, j = c // 2, c % 2
        out[k, 2048 * j: 2048 * (j + 1)] = res.results[c]["out_fm"].T
    return out


# version 05
# speedup vs baseline: 1.0191x; 1.0055x over previous
import contextlib
import numpy as np
import ml_dtypes
import concourse.bass as bass
import concourse.mybir as mybir
from concourse.bass_utils import run_bass_kernel_spmd

F32 = mybir.dt.float32
BF16 = mybir.dt.bfloat16
AF = mybir.ActivationFunctionType
OP = mybir.AluOpType
AX = mybir.AxisListType

NCORE = 8
T = 2048
TH = 1024
THP = TH + 3
D = 4096
DC = 32
H = 128
G = 8
HPG = 16
NZ = 8
EPS = 1e-5

ROW_GATE = 0
ROW_HS = 8192
ROW_B = 16384
ROW_C = 17408
ROW_DT = 18432
NCHUNK = 145

BF = ml_dtypes.bfloat16


def _chunk_rows(ci):
    if ci == 0:
        return ROW_DT
    g = (ci - 1) // 18
    r = (ci - 1) % 18
    if r == 0:
        return ROW_B + 128 * g
    if r == 1:
        return ROW_C + 128 * g
    if r < 10:
        return ROW_HS + 1024 * g + 128 * (r - 2)
    return ROW_GATE + 1024 * g + 128 * (r - 10)


def CH_B(g):
    return 1 + 18 * g


def CH_C(g):
    return 2 + 18 * g


def CH_HS(g, q):
    return 3 + 18 * g + q


def CH_GATE(g, q):
    return 11 + 18 * g + q


def CONV_B(g):
    return 10 * g


def CONV_C(g):
    return 10 * g + 1


def CONV_HS(g, q):
    return 10 * g + 2 + q


def _conv_ch_of_chunk(j):
    g = j // 10
    r = j % 10
    if r == 0:
        rows = ROW_B + 128 * g
    elif r == 1:
        rows = ROW_C + 128 * g
    else:
        rows = ROW_HS + 1024 * g + 128 * (r - 2)
    if ROW_HS <= rows < ROW_B:
        cbase = rows - ROW_HS
    elif ROW_B <= rows < ROW_C:
        cbase = 8192 + (rows - ROW_B)
    else:
        cbase = 9216 + (rows - ROW_C)
    return np.arange(cbase, cbase + 128)


class Sem:
    def __init__(self, h):
        self.h = h
        self.n = 0


def build_nc(taps=()):
    nc = bass.Bass("TRN2", target_bir_lowering=False)
    taps = set(taps)

    def din(name, shape, dt=F32):
        return nc.dram_tensor(name, shape, dt, kind="ExternalInput").ap()

    x_tm = din("x_tm", [T + 3, D])
    x_fm = din("x_fm", [D, T])
    w_inT = din("w_inT", [NCHUNK, 128, DC, 128])
    convdiag = din("convdiag", [80, 128, 4, 128])
    convb = din("convb", [128, 80])
    w_outT = din("w_outT", [32, 128, 64, 128])
    kern = din("kern", [32, 128, 32, 128])
    loraA = din("loraA", [16, 128, 1024])
    loraAs = din("loraAs", [128, 1024])
    loraB = din("loraB", [16, 32, 4096])
    loraBs = din("loraBs", [32, 4096])
    gaterow = din("gaterow", [1, 16])
    drow = din("drow", [1, 128])
    alog = din("alog", [128, 1])
    dtb_d = din("dtb", [128, 1])
    normw_d = din("normw", [128, 64])
    ffnb_d = din("ffnb", [128, 32])
    selb_d = din("selb", [128, 8])
    c_ibf = din("c_ibf", [128, 128], BF16)
    c_tril = din("c_tril", [128, 128], BF16)
    c_ones = din("c_ones", [128, 128], BF16)
    c_negex8 = din("c_negex8", [8, 1024], BF16)
    c_selz8 = din("c_selz8", [8, 1024], BF16)
    cvals = din("cvals", [128, 8])

    out_fm = nc.dram_tensor("out_fm", [D, T], F32, kind="ExternalOutput").ap()

    def dint(name, shape, dt=BF16):
        return nc.dram_tensor(name, shape, dt, kind="Internal").ap()

    X_dram = dint("X_dram", [G, 16, 128, 1024])
    B_dram = dint("B_dram", [G, 128, T])
    C_dram = dint("C_dram", [G, 128, T])
    dt_dram = dint("dt_dram", [2, 5, 128, TH])
    gn_dram = dint("gn_dram", [64, 128, 16, 128])
    ffn_dram = dint("ffn_dram", [32, 128, 4, 512])
    ex_dram = dint("ex_dram", [128, 8192])
    gath_dram = dint("gath_dram", [NCORE * 128, 8192])
    woutbf = dint("woutbf", [32, 128, 8192])

    tap_aps = {}
    for tname, tshape, tdt in [
        ("tap_xn", [128, DC * THP], BF16),
        ("tap_acum", [128, TH], F32),
        ("tap_prevall", [128, 8192], BF16),
        ("tap_sin", [128, 8192], BF16),
        ("tap_g", [128, 1024], BF16),
        ("tap_ms", [128, 2048], BF16),
        ("tap_gn", [128, 1024], BF16),
        ("tap_ln", [128, 16], F32),
        ("tap_B", [128, 2048], BF16),
        ("tap_X", [128, 1024], BF16),
        ("tap_pz", [128, 2048], F32),
    ]:
        if tname in taps:
            tap_aps[tname] = nc.dram_tensor(tname, tshape, tdt, kind="ExternalOutput").ap()

    PROG = []

    def E(ek, fn):
        PROG.append((ek, fn))

    def EI(ek, fn, sem, amt=1):
        h = sem.h
        PROG.append((ek, lambda e, fn=fn, h=h, amt=amt: fn(e).then_inc(h, amt)))
        sem.n += amt
        return (sem, sem.n)

    def EW(ek, tok):
        if tok is None:
            return
        sem, val = tok
        h = sem.h
        PROG.append((ek, lambda e, h=h, v=val: e.wait_ge(h, v)))

    st = contextlib.ExitStack()
    DQ = Sem(st.enter_context(nc.semaphore("DQ")))
    SQ = Sem(st.enter_context(nc.semaphore("SQ")))
    PE = Sem(st.enter_context(nc.semaphore("PEs")))
    DV = Sem(st.enter_context(nc.semaphore("DVs")))
    AC = Sem(st.enter_context(nc.semaphore("ACs")))
    CC = Sem(st.enter_context(nc.semaphore("CCs")))

    def now(sem):
        return (sem, sem.n)

    def sb(name, shape, dt=BF16):
        return st.enter_context(nc.sbuf_tensor(name, shape, dt))

    IBF = sb("IBF", [128, 128])
    TRIL = sb("TRIL", [128, 128])
    ONES = sb("ONES", [128, 128])
    NEGEX8 = sb("NEGEX8", [8, 1024])
    SELZ8 = sb("SELZ8", [8, 1024])
    CONVB = sb("CONVB", [128, 80], F32)
    NORMW = sb("NORMW", [128, 64], F32)
    FFNB = sb("FFNB", [128, 32], F32)
    DTB = sb("DTB", [128, 1], F32)
    ACOL = sb("ACOL", [128, 1], F32)
    SELB = sb("SELB", [128, 8], F32)
    DROWB = sb("DROWB", [1, 128])
    GROWB = sb("GROWB", [1, 16])
    DBC = sb("DBC", [128, 128], F32)
    GBC = sb("GBC", [128, 16], F32)
    CVALS = sb("CVALS", [128, 8], F32)
    for _i, _v in enumerate([EPS, 1.0, -1.0, -0.5, 1.0 / 1024]):
        nc.const_aps.register(F32, _v, CVALS[:, _i:_i + 1])
    WBUF = sb("WBUF", [128, 2 * 4096])
    ARB = sb("ARB", [128, DC * THP])
    PREVALL = sb("PREVALL", [128, 8192])
    ZZ = sb("ZZ", [128, 128], F32)
    AEFF = [None]
    BST = [None]
    SINALL = [None]

    p512a = st.enter_context(nc.psum_tensor("p512a", [128, 512], F32))
    p512b = st.enter_context(nc.psum_tensor("p512b", [128, 512], F32))
    pBig1 = st.enter_context(nc.psum_tensor("pBig1", [128, 1024], F32))
    pBig2 = st.enter_context(nc.psum_tensor("pBig2", [128, 1024], F32))
    pT = st.enter_context(nc.psum_tensor("pT", [128, 1024], BF16))
    pSmall = st.enter_context(nc.psum_tensor("pSmall", [128, 131], F32))
    pSc = pSmall[:, 0:128]
    pMini = pSmall[:, 128:131]

    wslot_free = [None, None]
    p512_free = [None, None, None, None]
    pT_free = [None]
    pSc_free = [None]

    ln_state = {}

    def load_wchunk(ci, slot):
        EW('g', wslot_free[slot])
        src = w_inT[ci, :, :, :].rearrange("d k c -> d (k c)")
        dst = WBUF[:, slot * 4096:(slot + 1) * 4096]
        return EI('g', lambda g, s=src, d=dst: g.dma_start(d, s), DQ, 16)

    def p0():
        for dst, src in [(IBF, c_ibf), (TRIL, c_tril), (ONES, c_ones),
                         (NEGEX8, c_negex8), (SELZ8, c_selz8),
                         (CONVB, convb), (NORMW, normw_d), (FFNB, ffnb_d),
                         (DTB, dtb_d), (SELB, selb_d), (CVALS, cvals)]:
            EI('q', lambda q, d=dst, s=src: q.dma_start(d[:, :], s[:, :]), SQ, 16)
        EW('s', now(SQ))
        EW('v', now(SQ))
        t1 = EI('g', lambda g: g.dma_start(DROWB[:, :], drow[:, :]), DQ, 16)
        t2 = EI('g', lambda g: g.dma_start(GROWB[:, :], gaterow[:, :]), DQ, 16)
        talog = EI('q', lambda q: q.dma_start(ACOL[:, :], alog[:, :]), SQ, 16)
        EW('s', talog)
        ta = EI('s', lambda s: s.activation(ACOL[:, :], ACOL[:, :], AF.Exp), AC)
        EW('v', ta)
        EI('v', lambda v: v.tensor_scalar_mul(ACOL[:, :], ACOL[:, :], -1.0), DV)
        EI('v', lambda v: v.memset(ZZ[:, :], 0.0), DV)
        EW('t', now(SQ))
        EW('t', t1)
        EW('t', t2)
        tm1 = EI('t', lambda t: t.matmul(pSc[:, :], ONES[0:1, :], DROWB[:, :],
                                         start=True, stop=True), PE)
        EW('v', tm1)
        td = EI('v', lambda v: v.tensor_copy(DBC[:, :], pSc[:, :]), DV)
        EW('t', td)
        tm2 = EI('t', lambda t: t.matmul(pSc[:, 0:16], ONES[0:1, :], GROWB[:, :],
                                         start=True, stop=True), PE)
        EW('v', tm2)
        tg_ = EI('v', lambda v: v.tensor_copy(GBC[:, :], pSc[:, 0:16]), DV)
        pSc_free[0] = tg_

    def p0_lora():
        av = AEFF[0].rearrange("p (dc r2) -> p dc r2", r2=64)
        for n in range(16):
            slot = n % 2
            EW('g', wslot_free[slot])
            tl = EI('g', lambda g, n=n, sl=slot:
                    g.dma_start(WBUF[:, sl * 4096: sl * 4096 + 1024], loraA[n, :, :]), DQ, 16)
            EW('v', tl)
            src = WBUF[:, slot * 4096: slot * 4096 + 1024].rearrange("p (dc r) -> p dc r", r=32)
            if n == 0:
                tv = EI('v', lambda v, s=src:
                        v.tensor_scalar(av[:, :, 0:32], s, GBC[:, 0:1], None, OP.mult), DV)
            else:
                tv = EI('v', lambda v, s=src, n=n:
                        v.scalar_tensor_tensor(av[:, :, 0:32], s, GBC[:, n:n + 1],
                                               av[:, :, 0:32], OP.mult, OP.add), DV)
            wslot_free[slot] = tv
        EW('g', wslot_free[0])
        tl = EI('g', lambda g: g.dma_start(WBUF[:, 0:1024], loraAs[:, :]), DQ, 16)
        EW('v', tl)
        srcs = WBUF[:, 0:1024].rearrange("p (dc r) -> p dc r", r=32)
        tv = EI('v', lambda v, s=srcs: v.tensor_copy(av[:, :, 32:64], s), DV)
        wslot_free[0] = tv
        for n in range(16):
            slot = n % 2
            EW('g', wslot_free[slot])
            tl = EI('g', lambda g, n=n, sl=slot:
                    g.dma_start(WBUF[0:32, sl * 4096: sl * 4096 + 4096], loraB[n, :, :]), DQ, 16)
            EW('v', tl)
            if n == 0:
                tv = EI('v', lambda v, sl=slot:
                        v.tensor_scalar(BST[0][0:32, :], WBUF[0:32, sl * 4096: sl * 4096 + 4096],
                                        GBC[0:32, 0:1], None, OP.mult), DV)
            else:
                tv = EI('v', lambda v, sl=slot, n=n:
                        v.scalar_tensor_tensor(BST[0][0:32, :], WBUF[0:32, sl * 4096: sl * 4096 + 4096],
                                               GBC[0:32, n:n + 1], BST[0][0:32, :], OP.mult, OP.add), DV)
            wslot_free[slot] = tv
        tl = EI('g', lambda g: g.dma_start(BST[0][32:64, :], loraBs[:, :]), DQ, 16)
        wslot_free[0] = None
        wslot_free[1] = None
        return tl

    def ln_half(p, bufs, arb_free):
        xt, xnb, mcol, sqcol, sq2col, vcol, rstd, nmcol = bufs
        xt_free = ln_state.get(id(bufs[0]))
        xnb_free = ln_state.get(('xnb', id(bufs[0])))
        last = None
        for i in range(9):
            rows = 3 if i == 8 else 128
            r0 = TH * p + (128 * i if i < 8 else 1024)
            col0 = 128 * i if i < 8 else 1024
            EW('q', xt_free)
            tl = EI('q', lambda q, r0=r0, rows=rows:
                    q.dma_start(xt[0:rows, :], x_tm[r0:r0 + rows, 0:2048]), SQ, 16)
            EW('v', tl)
            EW('v', xt_free)
            t1_ = EI('v', lambda v, rows=rows:
                     v.reduce_sum(mcol[0:rows, :], xt[0:rows, :], axis=AX.X), DV)
            EW('s', t1_)
            EW('s', xnb_free)
            tsqa = EI('s', lambda s, rows=rows:
                      s.activation(xnb[0:rows, 0:2048], xt[0:rows, :], AF.Square,
                                   accum_out=sqcol[0:rows, :]), AC)
            EW('q', tsqa)
            EW('q', t1_)
            tl = EI('q', lambda q, r0=r0, rows=rows:
                    q.dma_start(xt[0:rows, :], x_tm[r0:r0 + rows, 2048:4096]), SQ, 16)
            EW('v', tl)
            EI('v', lambda v, rows=rows:
               v.reduce_sum(vcol[0:rows, :], xt[0:rows, :], axis=AX.X), DV)
            EW('v', now(DV))
            tsum = EI('v', lambda v, rows=rows:
                      v.tensor_add(mcol[0:rows, :], mcol[0:rows, :], vcol[0:rows, :]), DV)
            EW('v', now(DV))
            EI('v', lambda v, rows=rows:
               v.tensor_scalar_mul(mcol[0:rows, :], mcol[0:rows, :], 1.0 / D), DV)
            EW('s', tsum)
            tsq = EI('s', lambda s, rows=rows:
                     s.activation(xnb[0:rows, 0:2048], xt[0:rows, :], AF.Square,
                                  accum_out=sq2col[0:rows, :]), AC)
            EW('v', tsq)
            EI('v', lambda v, rows=rows: v.tensor_add(sqcol[0:rows, :], sqcol[0:rows, :], sq2col[0:rows, :]), DV)
            EW('v', now(DV))
            EI('v', lambda v, rows=rows: v.tensor_scalar_mul(sqcol[0:rows, :], sqcol[0:rows, :], 1.0 / D), DV)
            EW('v', now(DV))
            EI('v', lambda v, rows=rows: v.scalar_tensor_tensor(vcol[0:rows, :], mcol[0:rows, :],
                                                                1.0, mcol[0:rows, :], OP.mult, OP.mult), DV)
            EW('v', now(DV))
            tvar = EI('v', lambda v, rows=rows: v.tensor_sub(vcol[0:rows, :], sqcol[0:rows, :], vcol[0:rows, :]), DV)
            EW('v', now(DV))
            EW('s', tvar)
            EI('s', lambda s, rows=rows: s.activation(rstd[0:rows, :], vcol[0:rows, :], AF.Ln, bias=EPS), AC)
            trs = EI('s', lambda s, rows=rows: s.activation(rstd[0:rows, :], rstd[0:rows, :], AF.Exp, scale=-0.5), AC)
            EW('v', trs)
            tnm = EI('v', lambda v, rows=rows: v.scalar_tensor_tensor(nmcol[0:rows, :], mcol[0:rows, :],
                                                                     -1.0, rstd[0:rows, :], OP.mult, OP.mult), DV)
            EW('v', now(DV))
            EW('s', tnm)
            if "tap_ln" in tap_aps and p == 0 and i == 1 and arb_free is not None and getattr(ln_half, "_tapped", False) is False:
                ln_half._tapped = True
                EW('q', tnm)
                for _j, _c in enumerate([mcol, sqcol, sq2col, vcol, rstd, nmcol]):
                    EI('q', lambda q, j=_j, c=_c: q.dma_start(tap_aps["tap_ln"][:, 2 * j: 2 * j + 1], c[:, :], allow_slow_non_contiguous=True), SQ, 16)
                EW('v', now(SQ))
                EW('s', now(SQ))
            tx1 = EI('s', lambda s, rows=rows: s.activation(xnb[0:rows, 2048:4096], xt[0:rows, :], AF.Identity,
                                                            bias=nmcol[0:rows, :], scale=rstd[0:rows, :]), AC)
            EW('q', tx1)
            EW('q', tvar)
            tl = EI('q', lambda q, r0=r0, rows=rows:
                    q.dma_start(xt[0:rows, :], x_tm[r0:r0 + rows, 0:2048]), SQ, 16)
            EW('s', tl)
            txn = EI('s', lambda s, rows=rows: s.activation(xnb[0:rows, 0:2048], xt[0:rows, :], AF.Identity,
                                                            bias=nmcol[0:rows, :], scale=rstd[0:rows, :]), AC)
            xt_free = txn
            ln_state[id(bufs[0])] = txn
            for bb in range(4):
                EW('t', txn)
                EW('t', pT_free[0])
                ttr = None
                for dd in range(8):
                    dc = bb * 8 + dd
                    ttr = EI('t', lambda t, dc=dc, dd=dd, rows=rows:
                             t.transpose(pT[:, dd * 128: dd * 128 + rows],
                                         xnb[0:rows, dc * 128:(dc + 1) * 128],
                                         IBF[0:rows, 0:rows]), PE)
                EW('v', ttr)
                EW('v', arb_free[0])
                arb_free[0] = None
                dst = ARB.rearrange("x (dc c) -> x dc c", c=THP)[:, bb * 8:(bb + 1) * 8, col0:col0 + rows]
                srcv = pT.rearrange("x (dd c) -> x dd c", c=128)[:, :, 0:rows]
                tev = EI('v', lambda v, d=dst, s=srcv: v.tensor_copy(d, s), DV)
                pT_free[0] = tev
                last = tev
            xnb_free = now(PE)
            ln_state[('xnb', id(bufs[0]))] = xnb_free
        return last

    def pair_psums(sc):
        pr = sc['pair']
        sc['pair'] = 1 - pr
        if pr == 0:
            return pr, (p512a[:, :], p512b[:, :])
        return pr, (pBig1[:, 0:512], pBig1[:, 512:1024])

    def conv_chunk(ci, cj, dest, convin, cdslot, sc, dest_wait):
        slot = sc['wslot']
        sc['wslot'] = 1 - slot
        tload = load_wchunk(ci, slot)
        EW('g', sc['cd_free'])
        tcd = EI('g', lambda g, cj=cj: g.dma_start(
            cdslot[:, :], convdiag[cj, :, :, :].rearrange("i k c -> i (k c)")), DQ, 16)
        pr, pps = pair_psums(sc)
        EW('t', tload)
        for j in range(2):
            pp = pps[j]
            EW('t', p512_free[2 * pr + j])
            for k in range(DC):
                lhsT = WBUF[:, slot * 4096 + k * 128: slot * 4096 + (k + 1) * 128]
                rhs = ARB[:, k * THP + 3 + 512 * j: k * THP + 3 + 512 * (j + 1)]
                EI('t', lambda t, o=pp, l_=lhsT, r=rhs, s=(k == 0), e=(k == DC - 1):
                   t.matmul(o, l_, r, start=s, stop=e), PE)
        EW('t', sc['mini_free'])
        EW('t', pSc_free[0])
        tmini = None
        for k in range(DC):
            lhsT = WBUF[:, slot * 4096 + k * 128: slot * 4096 + (k + 1) * 128]
            rhs = ARB[:, k * THP: k * THP + 3]
            tmini = EI('t', lambda t, l_=lhsT, r=rhs, s=(k == 0), e=(k == DC - 1):
                       t.matmul(pMini[:, :], l_, r, start=s, stop=e), PE)
        wslot_free[slot] = tmini
        EW('v', sc['convin_free'])
        EW('v', tmini)
        EI('v', lambda v: v.tensor_copy(convin[:, 0:3], pMini[:, :]), DV)
        sc['mini_free'] = now(DV)
        pSc_free[0] = now(DV)
        EI('v', lambda v, pp=pps[0]: v.tensor_copy(convin[:, 3:515], pp), DV)
        e1 = EI('v', lambda v, pp=pps[1]: v.tensor_copy(convin[:, 515:1027], pp), DV)
        p512_free[2 * pr] = e1
        p512_free[2 * pr + 1] = e1
        EW('t', tcd)
        EW('t', e1)
        EW('t', sc['conv_free'])
        for j in range(2):
            pp = [pBig2[:, 0:512], pBig2[:, 512:1024]][j]
            for k in range(4):
                lhsT = cdslot[:, k * 128:(k + 1) * 128]
                rhs = convin[:, 512 * j + k: 512 * j + k + 512]
                EI('t', lambda t, o=pp, l_=lhsT, r=rhs, s=(k == 0), e=(k == 3):
                   t.matmul(o, l_, r, start=s, stop=e), PE)
        tconv = now(PE)
        sc['cd_free'] = tconv
        sc['convin_free'] = tconv
        EW('s', tconv)
        if dest_wait is not None:
            for tok in (dest_wait if isinstance(dest_wait, list) else [dest_wait]):
                EW('s', tok)
        EI('s', lambda s, cj=cj: s.activation(dest[:, 0:512], pBig2[:, 0:512], AF.Silu,
                                              bias=CONVB[:, cj:cj + 1]), AC)
        ts = EI('s', lambda s, cj=cj: s.activation(dest[:, 512:1024], pBig2[:, 512:1024], AF.Silu,
                                                   bias=CONVB[:, cj:cj + 1]), AC)
        sc['conv_free'] = ts
        return ts

    def s1_dt(p, b):
        (dtf, scr, acum, dt_tm, dtd_tm, hi_tm, lo_tm, alt8, lam_all,
         hi_fm, lo_fm, dt_fm, dd_fm) = b
        EW('v', s1dt_dump[0])
        EW('v', now(AC))
        EW('s', now(DV))
        slot = sc1['wslot']
        sc1['wslot'] = 1 - slot
        tload = load_wchunk(0, slot)
        pr, pps = pair_psums(sc1)
        EW('t', tload)
        for j in range(2):
            pp = pps[j]
            EW('t', p512_free[2 * pr + j])
            for k in range(DC):
                lhsT = WBUF[:, slot * 4096 + k * 128: slot * 4096 + (k + 1) * 128]
                rhs = ARB[:, k * THP + 3 + 512 * j: k * THP + 3 + 512 * (j + 1)]
                EI('t', lambda t, o=pp, l_=lhsT, r=rhs, s=(k == 0), e=(k == DC - 1):
                   t.matmul(o, l_, r, start=s, stop=e), PE)
        wslot_free[slot] = now(PE)
        EW('v', now(PE))
        EI('v', lambda v, pp=pps[0]: v.tensor_copy(dtf[:, 0:512], pp), DV)
        te = EI('v', lambda v, pp=pps[1]: v.tensor_copy(dtf[:, 512:1024], pp), DV)
        p512_free[2 * pr] = te
        p512_free[2 * pr + 1] = te
        EW('v', now(DV))
        tb = EI('v', lambda v: v.tensor_scalar(dtf[:, :], dtf[:, :], DTB[:, :], None, OP.add), DV)
        EW('s', tb)
        EI('s', lambda s: s.activation(scr[:, :], dtf[:, :], AF.Abs), AC)
        EI('s', lambda s: s.activation(scr[:, :], scr[:, :], AF.Exp, scale=-1.0), AC)
        ta = EI('s', lambda s: s.activation(scr[:, :], scr[:, :], AF.Ln, bias=1.0), AC)
        EW('v', ta)
        EI('v', lambda v: v.tensor_scalar_max(dtf[:, :], dtf[:, :], 0.0), DV)
        EW('v', now(DV))
        EI('v', lambda v: v.tensor_add(dtf[:, :], dtf[:, :], scr[:, :]), DV)
        EW('v', now(DV))
        EI('v', lambda v: v.tensor_scalar_min(dtf[:, :], dtf[:, :], 100.0), DV)
        EW('v', now(DV))
        EI('v', lambda v: v.tensor_scalar(scr[:, :], dtf[:, :], ACOL[:, :], None, OP.mult), DV)
        EW('v', now(DV))
        for z in range(NZ):
            EI('v', lambda v, z=z: v.tensor_tensor_scan(
                acum[:, z * 128:(z + 1) * 128], scr[:, z * 128:(z + 1) * 128],
                ZZ[:, :], 0.0, OP.add, OP.add), DV)
        EW('v', now(DV))
        EI('v', lambda v: v.tensor_copy(hi_fm[:, :], acum[:, :]), DV)
        EW('v', now(DV))
        EI('v', lambda v: v.tensor_tensor(lo_fm[:, :], acum[:, :], hi_fm[:, :], OP.subtract), DV)
        EI('v', lambda v: v.tensor_copy(dt_fm[:, :], dtf[:, :]), DV)
        acv = acum.rearrange("h (z l) -> h z l", l=128)
        alb = acv[:, :, 127:128].broadcast_to([128, NZ, 128])
        tda = EI('v', lambda v, a=acv, b_=alb: v.tensor_tensor(
            scr.rearrange("h (z l) -> h z l", l=128), b_, a, OP.subtract), DV)
        EW('s', tda)
        tdd = EI('s', lambda s: s.activation(dd_fm[:, :], scr[:, :], AF.Exp), AC)
        EW('v', tdd)
        EW('v', now(DV))
        EI('v', lambda v: v.tensor_mul(dd_fm[:, :], dd_fm[:, :], dt_fm[:, :]), DV)
        EW('v', now(DV))
        tal = EI('v', lambda v, a=acv: v.tensor_copy(dt_tm[:, 0:NZ], a[:, :, 127]), DV)
        EW('t', tal)
        EW('t', pT_free[0])
        ttr = EI('t', lambda t: t.transpose(pT[0:NZ, 0:128], dt_tm[:, 0:NZ], IBF[:, :]), PE)
        EW('v', ttr)
        tv = EI('v', lambda v: v.tensor_copy(alt8[:, :], pT[0:NZ, 0:128]), DV)
        pT_free[0] = tv
        for (src, dst) in [(hi_fm, hi_tm), (lo_fm, lo_tm), (dt_fm, dt_tm), (dd_fm, dtd_tm)]:
            EW('t', now(DV))
            EW('t', pT_free[0])
            ttr = None
            for z in range(NZ):
                ttr = EI('t', lambda t, s_=src, z=z:
                         t.transpose(pT[:, z * 128:(z + 1) * 128],
                                     s_[:, z * 128:(z + 1) * 128], IBF[:, :]), PE)
            EW('v', ttr)
            dstv = dst.rearrange("l (z h) -> l z h", h=128)
            srcv = pT.rearrange("l (z h) -> l z h", h=128)
            tv = EI('v', lambda v, d=dstv, s_=srcv: v.tensor_copy(d, s_), DV)
            pT_free[0] = tv
        EW('t', now(DV))
        for z in range(NZ):
            EW('t', pSc_free[0])
            tm = EI('t', lambda t, z=z: t.matmul(pSc[:, :], SELZ8[:, z * 128:(z + 1) * 128],
                                                 alt8[:, :], start=True, stop=True), PE)
            EW('s', tm)
            tse = EI('s', lambda s, z=z: s.activation(lam_all[:, z * 128:(z + 1) * 128],
                                                      pSc[:, :], AF.Exp), AC)
            pSc_free[0] = tse
        EW('q', now(DV))
        EW('q', now(AC))
        for idx, src in [(0, dt_tm), (1, dtd_tm), (2, hi_tm), (3, lo_tm)]:
            EI('q', lambda q, i=idx, s_=src, p=p:
               q.dma_start(dt_dram[p, i, :, :], s_[:, :]), SQ, 16)
        tdump = EI('q', lambda q, p=p: q.dma_start(dt_dram[p, 4, 0:8, 0:128], alt8[:, :]), SQ, 16)
        s1dt_dump[0] = tdump
        if "tap_acum" in tap_aps and p == 0:
            EI('q', lambda q: q.dma_start(tap_aps["tap_acum"][:, :], acum[:, :]), SQ, 16)
        return tdump

    def s1_group(p, g, b, sc):
        (convin, cdslot, Bfm, Cfm, Xtmg, btm, xdtd, prevf,
         dt_tm, dtd_tm, lam_all) = b
        tb = conv_chunk(CH_B(g), CONV_B(g), Bfm, convin, cdslot, sc, sc['bfm_free'])
        EW('q', tb)
        tbd = EI('q', lambda q, g=g, p=p: q.dma_start(B_dram[g, :, TH * p: TH * (p + 1)], Bfm[:, :]), SQ, 16)
        tc = conv_chunk(CH_C(g), CONV_C(g), Cfm, convin, cdslot, sc, sc['cfm_free'])
        EW('q', tc)
        tcd_ = EI('q', lambda q, g=g, p=p: q.dma_start(C_dram[g, :, TH * p: TH * (p + 1)], Cfm[:, :]), SQ, 16)
        cfm_wait = [tcd_]
        for q_ in range(8):
            tx = conv_chunk(CH_HS(g, q_), CONV_HS(g, q_), Cfm, convin, cdslot, sc, cfm_wait)
            EW('t', tx)
            EW('t', pT_free[0])
            ttr = None
            for z in range(NZ):
                ttr = EI('t', lambda t, z=z: t.transpose(pT[:, z * 128:(z + 1) * 128],
                                                         Cfm[:, z * 128:(z + 1) * 128], IBF[:, :]), PE)
            cfm_wait = [ttr]
            EW('v', ttr)
            EW('v', sc['xtm_free'])
            sc['xtm_free'] = None
            EW('v', sc['xdump_free'])
            sc['xdump_free'] = None
            dstv = Xtmg.rearrange("l (z c) -> l z c", c=1024)[:, :, q_ * 128:(q_ + 1) * 128]
            srcv = pT.rearrange("l (z c) -> l z c", c=128)
            tev = EI('v', lambda v, d=dstv, s_=srcv: v.tensor_copy(d, s_), DV)
            pT_free[0] = tev
            EW('q', tev)
            txd_ = EI('q', lambda q, g=g, p=p, q_=q_, dv_=dstv: q.dma_start(
                X_dram[g, NZ * p: NZ * (p + 1), :, q_ * 128:(q_ + 1) * 128].rearrange("z l c -> l z c"),
                dv_), SQ, 16)
            sc['last_xdump'] = txd_
        if p == 0:
            tpi = EI('v', lambda v: v.memset(prevf[:, :], 0.0), DV)
        else:
            EW('v', sc['prevall_ok'])
            tpi = EI('v', lambda v, g=g: v.tensor_copy(prevf[:, :], PREVALL[:, g * 1024:(g + 1) * 1024]), DV)
        tst = None
        for z in range(NZ):
            EW('v', now(DV))
            if "tap_pz" in tap_aps and p == 0 and g == 0 and z == 1:
                EW('q', tpi)
                EI('q', lambda q: q.dma_start(tap_aps["tap_pz"][:, 0:1024], prevf[:, :]), SQ, 16)
                EI('g', lambda g_: g_.dma_start(tap_aps["tap_pz"][:, 1024:1040], lam_all[:, 0:16]), DQ, 16)
                EI('g', lambda g_: g_.dma_start(tap_aps["tap_pz"][:, 1040:1056], dtd_tm[:, 0:16]), DQ, 16)
                EI('g', lambda g_: g_.dma_start(tap_aps["tap_pz"][:, 1056:1072], dt_tm[:, 0:16]), DQ, 16)
                EI('g', lambda g_: g_.dma_start(tap_aps["tap_pz"][:, 1072:2048], xdtd[:, 0:976]), DQ, 16)
                EW('v', now(DQ))
                EW('v', now(SQ))
            EW('v', tst)
            EW('t', pT_free[0])
            ttr = EI('t', lambda t, z=z: t.transpose(pT[:, 0:128], Bfm[:, z * 128:(z + 1) * 128],
                                                     IBF[:, :]), PE)
            EW('v', ttr)
            tbm = EI('v', lambda v: v.tensor_copy(btm[:, :], pT[:, 0:128]), DV)
            pT_free[0] = tbm
            xv = Xtmg[:, z * 1024:(z + 1) * 1024].rearrange("l (hh pp) -> l hh pp", pp=64)
            dd = dtd_tm[:, z * 128 + HPG * g: z * 128 + HPG * (g + 1)].unsqueeze(2).broadcast_to([128, HPG, 64])
            txd = EI('v', lambda v, xv=xv, dd=dd: v.tensor_tensor(
                xdtd.rearrange("l (hh pp) -> l hh pp", pp=64), xv, dd, OP.mult), DV)
            EW('t', txd)
            for j in range(2):
                pp = [p512a, p512b][j]
                EW('t', p512_free[j])
                EI('t', lambda t, o=pp[:, :], j=j: t.matmul(
                    o, btm[:, :], xdtd[:, 512 * j: 512 * (j + 1)], start=True, stop=True), PE)
            tst = now(PE)
            lamv = lam_all[:, z * 128 + HPG * g: z * 128 + HPG * (g + 1)].unsqueeze(2).broadcast_to([128, HPG, 64])
            EW('v', now(DV))
            EI('v', lambda v, lamv=lamv: v.tensor_tensor(
                prevf.rearrange("n (hh pp) -> n hh pp", pp=64),
                prevf.rearrange("n (hh pp) -> n hh pp", pp=64), lamv, OP.mult), DV)
            EW('v', now(DV))
            EW('v', tst)
            EI('v', lambda v: v.tensor_add(prevf[:, 0:512], prevf[:, 0:512], p512a[:, :]), DV)
            tpu = EI('v', lambda v: v.tensor_add(prevf[:, 512:1024], prevf[:, 512:1024], p512b[:, :]), DV)
            p512_free[0] = tpu
            p512_free[1] = tpu
        EW('v', now(DV))
        tsv = EI('v', lambda v, g=g: v.tensor_copy(PREVALL[:, g * 1024:(g + 1) * 1024], prevf[:, :]), DV)
        sc['xtm_free'] = tst
        sc['xdump_free'] = sc['last_xdump']
        sc['bfm_free'] = [now(PE), tbd]
        sc['cfm_free'] = cfm_wait + [sc['last_xdump']]
        return tsv

    def s2_group(p, g, b, sc):
        (Bfm, Cfm, sgtm, sgf, E_, scm, btm, xz, xdt, xdtd,
         prevf, prevbf, hiT8, loT8, rhs1, ea, t1b, gzb,
         varcol, rstd, dt_tm, dtd_tm, hi_tm, lo_tm, lam_all) = b
        EW('q', sc['bc_free'])
        tlb = EI('q', lambda q, g=g, p=p: q.dma_start(Bfm[:, :], B_dram[g, :, TH * p: TH * (p + 1)]), SQ, 16)
        tlc = EI('q', lambda q, g=g, p=p: q.dma_start(Cfm[:, :], C_dram[g, :, TH * p: TH * (p + 1)]), SQ, 16)
        sgf_free = None
        for q_ in range(8):
            slot = sc['wslot']
            sc['wslot'] = 1 - slot
            tload = load_wchunk(CH_GATE(g, q_), slot)
            pr, pps = pair_psums(sc)
            EW('t', tload)
            for j in range(2):
                pp = pps[j]
                EW('t', p512_free[2 * pr + j])
                for k in range(DC):
                    lhsT = WBUF[:, slot * 4096 + k * 128: slot * 4096 + (k + 1) * 128]
                    rhs = ARB[:, k * THP + 3 + 512 * j: k * THP + 3 + 512 * (j + 1)]
                    EI('t', lambda t, o=pp, l_=lhsT, r=rhs, s=(k == 0), e=(k == DC - 1):
                       t.matmul(o, l_, r, start=s, stop=e), PE)
            wslot_free[slot] = now(PE)
            EW('s', now(PE))
            EW('s', sgf_free)
            EI('s', lambda s, pp=pps[0]: s.activation(sgf[:, 0:512], pp, AF.Silu), AC)
            tsg = EI('s', lambda s, pp=pps[1]: s.activation(sgf[:, 512:1024], pp, AF.Silu), AC)
            p512_free[2 * pr] = tsg
            p512_free[2 * pr + 1] = tsg
            EW('t', tsg)
            EW('t', pT_free[0])
            ttr = None
            for z in range(NZ):
                ttr = EI('t', lambda t, z=z: t.transpose(pT[:, z * 128:(z + 1) * 128],
                                                         sgf[:, z * 128:(z + 1) * 128], IBF[:, :]), PE)
            sgf_free = ttr
            EW('v', ttr)
            EW('v', sc['sgtm_free'])
            sc['sgtm_free'] = None
            dstv = sgtm.rearrange("l (z c) -> l z c", c=1024)[:, :, q_ * 128:(q_ + 1) * 128]
            srcv = pT.rearrange("l (z c) -> l z c", c=128)
            tev = EI('v', lambda v, d=dstv, s_=srcv: v.tensor_copy(d, s_), DV)
            pT_free[0] = tev
        if p == 0:
            tpi = EI('v', lambda v, g=g: v.tensor_copy(prevf[:, :], SINALL[0][:, g * 1024:(g + 1) * 1024]), DV)
        else:
            tpi = EI('v', lambda v, g=g: v.tensor_copy(prevf[:, :], PREVALL[:, g * 1024:(g + 1) * 1024]), DV)
        EW('t', tlb)
        EW('t', tlc)
        xz_free = [None, None]
        z_pe = None
        for z in range(NZ):
            gz = NZ * p + z
            xsl = z % 2
            EW('q', xz_free[xsl])
            tlx = EI('q', lambda q, g=g, gz=gz, xsl=xsl: q.dma_start(
                xz[:, xsl * 1024:(xsl + 1) * 1024], X_dram[g, gz, :, :]), SQ, 16)
            xzv = xz[:, xsl * 1024:(xsl + 1) * 1024]
            eav_ = ea[:, 16 * xsl: 16 * (xsl + 1)]
            t1v = t1b[:, 1024 * xsl: 1024 * (xsl + 1)]
            EW('v', z_pe)
            EW('t', pT_free[0])
            ttr = EI('t', lambda t, z=z: t.transpose(pT[:, 0:128], Bfm[:, z * 128:(z + 1) * 128],
                                                     IBF[:, :]), PE)
            EW('v', ttr)
            tbm = EI('v', lambda v: v.tensor_copy(btm[:, :], pT[:, 0:128]), DV)
            pT_free[0] = tbm
            EW('t', pSc_free[0])
            tsc = EI('t', lambda t, z=z: t.matmul(pSc[:, :], Bfm[:, z * 128:(z + 1) * 128],
                                                  Cfm[:, z * 128:(z + 1) * 128], start=True, stop=True), PE)
            EW('v', tsc)
            tscm = EI('v', lambda v: v.tensor_tensor(scm[:, :], pSc[:, :], TRIL[:, :], OP.mult), DV)
            pSc_free[0] = tscm
            EW('v', tlx)
            dtv = dt_tm[:, z * 128 + HPG * g: z * 128 + HPG * (g + 1)].unsqueeze(2).broadcast_to([128, HPG, 64])
            ddv = dtd_tm[:, z * 128 + HPG * g: z * 128 + HPG * (g + 1)].unsqueeze(2).broadcast_to([128, HPG, 64])
            xvv = xzv.rearrange("l (hh pp) -> l hh pp", pp=64)
            txdt = EI('v', lambda v, xvv=xvv, dtv=dtv: v.tensor_tensor(
                xdt.rearrange("l (hh pp) -> l hh pp", pp=64), xvv, dtv, OP.mult), DV)
            txdd = EI('v', lambda v, xvv=xvv, ddv=ddv: v.tensor_tensor(
                xdtd.rearrange("l (hh pp) -> l hh pp", pp=64), xvv, ddv, OP.mult), DV)
            dbv = DBC[:, HPG * g: HPG * (g + 1)].unsqueeze(2).broadcast_to([128, HPG, 64])
            EW('v', sc['sq_free'])
            sc['sq_free'] = None
            txD = EI('v', lambda v, xvv=xvv, dbv=dbv: v.tensor_tensor(
                gzb.rearrange("l (hh pp) -> l hh pp", pp=64), xvv, dbv, OP.mult), DV)
            if z == 0:
                EW('t', p512_free[2])
                EW('t', p512_free[3])
            first_y = True
            for q2 in range(2):
                h0 = z * 128 + HPG * g + 8 * q2
                EW('v', sc['ms_free'])
                sc['ms_free'] = None
                EW('t', pT_free[0])
                EI('t', lambda t, h0=h0: t.transpose(pT[0:8, 0:128], hi_tm[:, h0:h0 + 8], IBF[:, :]), PE)
                t2_ = EI('t', lambda t, h0=h0: t.transpose(pT[0:8, 128:256], lo_tm[:, h0:h0 + 8], IBF[:, :]), PE)
                EW('v', t2_)
                EI('v', lambda v: v.tensor_copy(hiT8[:, :], pT[0:8, 0:128]), DV)
                tcl = EI('v', lambda v: v.tensor_copy(loT8[:, :], pT[0:8, 128:256]), DV)
                pT_free[0] = tcl
                ex = IBF[:, :].unsqueeze(1).broadcast_to([128, 8, 128])
                hb = hi_tm[:, h0:h0 + 8].unsqueeze(2).broadcast_to([128, 8, 128])
                lb = lo_tm[:, h0:h0 + 8].unsqueeze(2).broadcast_to([128, 8, 128])
                tr1 = EI('v', lambda v, ex=ex, hb=hb: v.tensor_tensor(
                    rhs1.rearrange("l (hh c) -> l hh c", c=128), ex, hb, OP.mult), DV)
                EW('t', sc['pBig2_free'])
                sc['pBig2_free'] = None
                EW('t', tcl)
                EW('t', tr1)
                for j in range(2):
                    EI('t', lambda t, j=j: t.matmul(pBig2[:, 512 * j:512 * (j + 1)], hiT8[:, :],
                                                    NEGEX8[:, 512 * j: 512 * (j + 1)], start=True, stop=False), PE)
                    EI('t', lambda t, j=j: t.matmul(pBig2[:, 512 * j:512 * (j + 1)], loT8[:, :],
                                                    NEGEX8[:, 512 * j: 512 * (j + 1)], start=False, stop=False), PE)
                    EI('t', lambda t, j=j: t.matmul(pBig2[:, 512 * j:512 * (j + 1)], ONES[:, :],
                                                    rhs1[:, 512 * j: 512 * (j + 1)], start=False, stop=False), PE)
                th1 = now(PE)
                EW('v', th1)
                tr2 = EI('v', lambda v, ex=ex, lb=lb: v.tensor_tensor(
                    rhs1.rearrange("l (hh c) -> l hh c", c=128), ex, lb, OP.mult), DV)
                EW('t', tr2)
                tmd = None
                for j in range(2):
                    tmd = EI('t', lambda t, j=j: t.matmul(pBig2[:, 512 * j:512 * (j + 1)], ONES[:, :],
                                                          rhs1[:, 512 * j: 512 * (j + 1)], start=False, stop=True), PE)
                EW('v', tmd)
                EW('v', sc['sq_free'])
                sc['sq_free'] = None
                tmn = EI('v', lambda v: v.tensor_scalar_min(E_[:, :], pBig2[:, :], 0.0), DV)
                sc['pBig2_free'] = tmn
                EW('s', tmn)
                EW('s', sc['ms_free'])
                tex = EI('s', lambda s: s.activation(E_[:, :], E_[:, :], AF.Exp), AC)
                EW('v', tex)
                EW('v', tscm)
                scb = scm[:, :].unsqueeze(1).broadcast_to([128, 8, 128])
                tms = EI('v', lambda v, scb=scb: v.tensor_tensor(
                    E_.rearrange("s (hh c) -> s hh c", c=128),
                    E_.rearrange("s (hh c) -> s hh c", c=128), scb, OP.mult), DV)
                if first_y:
                    EW('t', sc['pBig1_free'])
                    sc['pBig1_free'] = None
                    first_y = False
                EW('t', tms)
                EW('t', txdt)
                EW('t', txD)
                for hh8 in range(8):
                    hh = 8 * q2 + hh8
                    EI('t', lambda t, hh=hh, hh8=hh8: t.matmul(pBig1[:, hh * 64:(hh + 1) * 64],
                                                               E_[:, hh8 * 128:(hh8 + 1) * 128],
                                                               xdt[:, hh * 64:(hh + 1) * 64], start=True, stop=False), PE)
                    EI('t', lambda t, hh=hh: t.matmul(pBig1[:, hh * 64:(hh + 1) * 64],
                                                      IBF[:, :],
                                                      gzb[:, hh * 64:(hh + 1) * 64], start=False, stop=True), PE)
                sc['ms_free'] = now(PE)
            tyd = now(PE)
            EW('v', tpi)
            tpb = EI('v', lambda v: v.tensor_copy(prevbf[:, :], prevf[:, :]), DV)
            EW('t', sc['pBig2_free'])
            sc['pBig2_free'] = None
            EW('t', tpb)
            for j in range(2):
                EI('t', lambda t, z=z, j=j: t.matmul(pBig2[:, 512 * j:512 * (j + 1)],
                                                     Cfm[:, z * 128:(z + 1) * 128],
                                                     prevbf[:, 512 * j:512 * (j + 1)], start=True, stop=True), PE)
            tya = now(PE)
            EW('t', txdd)
            for j in range(2):
                pp = [p512a, p512b][j]
                EW('t', p512_free[j])
                EI('t', lambda t, o=pp[:, :], j=j: t.matmul(
                    o, btm[:, :], xdtd[:, 512 * j: 512 * (j + 1)], start=True, stop=True), PE)
            tst = now(PE)
            lamv = lam_all[:, z * 128 + HPG * g: z * 128 + HPG * (g + 1)].unsqueeze(2).broadcast_to([128, HPG, 64])
            EW('v', tya)
            EW('v', now(DV))
            EI('v', lambda v, lamv=lamv: v.tensor_tensor(
                prevf.rearrange("n (hh pp) -> n hh pp", pp=64),
                prevf.rearrange("n (hh pp) -> n hh pp", pp=64), lamv, OP.mult), DV)
            EW('v', now(DV))
            EW('v', tst)
            EI('v', lambda v: v.tensor_add(prevf[:, 0:512], prevf[:, 0:512], p512a[:, :]), DV)
            tpu = EI('v', lambda v: v.tensor_add(prevf[:, 512:1024], prevf[:, 512:1024], p512b[:, :]), DV)
            p512_free[0] = tpu
            p512_free[1] = tpu
            tpi = tpu
            tea = EI('s', lambda s, z=z, g=g, eav_=eav_: s.activation(
                eav_, hi_tm[:, z * 128 + HPG * g: z * 128 + HPG * (g + 1)], AF.Exp), AC)
            EW('v', tea)
            EW('v', sc['gn_free'])
            sc['gn_free'] = None
            eav = eav_.unsqueeze(2).broadcast_to([128, HPG, 64])
            tt1 = EI('v', lambda v, eav=eav, t1v=t1v: v.tensor_tensor(
                t1v.rearrange("l (hh pp) -> l hh pp", pp=64),
                pBig2.rearrange("l (hh pp) -> l hh pp", pp=64), eav, OP.mult), DV)
            sc['pBig2_free'] = tt1
            EW('v', tyd)
            EW('v', now(DV))
            tt2 = EI('v', lambda v, t1v=t1v: v.tensor_add(gzb[:, :], pBig1[:, :], t1v), DV)
            sc['pBig1_free'] = tt2
            EW('v', now(DV))
            tgt = EI('v', lambda v, z=z: v.tensor_mul(gzb[:, :], gzb[:, :],
                                                      sgtm[:, z * 1024:(z + 1) * 1024]), DV)
            if "tap_g" in tap_aps and p == 0 and g == 0 and z == 0:
                EW('q', tgt)
                EI('q', lambda q: q.dma_start(tap_aps["tap_g"][:, :], gzb[:, :]), SQ, 16)
            EW('s', tgt)
            tsqg = EI('s', lambda s: s.activation(E_[:, :], gzb[:, :], AF.Square,
                                                  accum_out=varcol[:, :]), AC)
            sc['sq_free'] = tsqg
            EI('s', lambda s: s.activation(rstd[:, :], varcol[:, :], AF.Ln,
                                           bias=EPS, scale=1.0 / 1024), AC)
            tsr = EI('s', lambda s: s.activation(rstd[:, :], rstd[:, :], AF.Exp, scale=-0.5), AC)
            EW('v', tsr)
            tgn = EI('v', lambda v: v.tensor_scalar(gzb[:, :], gzb[:, :], rstd[:, :], None, OP.mult), DV)
            EW('t', tgn)
            EW('t', pT_free[0])
            ttr = None
            for cc in range(8):
                ttr = EI('t', lambda t, cc=cc: t.transpose(pT[:, cc * 128:(cc + 1) * 128],
                                                           gzb[:, cc * 128:(cc + 1) * 128], IBF[:, :]), PE)
            EW('v', ttr)
            tev = None
            for cc in range(8):
                tev = EI('v', lambda v, cc=cc, g=g, t1v=t1v: v.tensor_scalar(
                    t1v[:, cc * 128:(cc + 1) * 128], pT[:, cc * 128:(cc + 1) * 128],
                    NORMW[:, 8 * g + cc: 8 * g + cc + 1], None, OP.mult), DV)
            pT_free[0] = tev
            EW('q', tev)
            tdq = None
            for cc in range(8):
                tdq = EI('q', lambda q, cc=cc, g=g, gz=gz, t1v=t1v: q.dma_start(
                    gn_dram[8 * g + cc, :, gz, :], t1v[:, cc * 128:(cc + 1) * 128]), SQ, 16)
            sc['gn_free'] = tdq
            if "tap_gn" in tap_aps and p == 0 and g == 0 and z == 0:
                EI('q', lambda q, t1v=t1v: q.dma_start(tap_aps["tap_gn"][:, :], t1v[:, 0:1024]), SQ, 16)
            xz_free[xsl] = txD
            z_pe = now(PE)
        EW('v', now(DV))
        tsv = EI('v', lambda v, g=g: v.tensor_copy(PREVALL[:, g * 1024:(g + 1) * 1024], prevf[:, :]), DV)
        if sc['pBig1_free'] is not None:
            p512_free[2] = sc['pBig1_free']
            p512_free[3] = sc['pBig1_free']
        sc['bc_free'] = z_pe
        sc['ddiag_free'] = z_pe
        return tsv

    def s2_ffn(p, axb, ft):
        for j in range(2):
            EW('t', p512_free[0])
            tmm = None
            for dc in range(DC):
                lhsT = AEFF[0][:, dc * 64:(dc + 1) * 64]
                rhs = ARB[:, dc * THP + 3 + 512 * j: dc * THP + 3 + 512 * (j + 1)]
                tmm = EI('t', lambda t, l_=lhsT, r=rhs, s=(dc == 0), e=(dc == DC - 1):
                         t.matmul(p512a[0:64, :], l_, r, start=s, stop=e), PE)
            EW('v', tmm)
            EW('v', p == 1 and ffn_axfree[0] or None)
            tev = EI('v', lambda v, j=j: v.tensor_copy(axb[0:64, 512 * j: 512 * (j + 1)], p512a[0:64, :]), DV)
            p512_free[0] = tev
        axtok = now(DV)
        ft_free = [None, None]
        for oc in range(32):
            slot = oc % 2
            EW('g', wslot_free[slot])
            tload = EI('g', lambda g, oc=oc, sl=slot: g.dma_start(
                WBUF[:, sl * 4096:(sl + 1) * 4096],
                kern[oc, :, :, :].rearrange("d k c -> d (k c)")), DQ, 16)
            pr, pps = pair_psums(sc2)
            EW('t', tload)
            EW('t', axtok)
            for j in range(2):
                pp = pps[j]
                EW('t', p512_free[2 * pr + j])
                for k in range(DC):
                    lhsT = WBUF[:, slot * 4096 + k * 128: slot * 4096 + (k + 1) * 128]
                    rhs = ARB[:, k * THP + 3 + 512 * j: k * THP + 3 + 512 * (j + 1)]
                    EI('t', lambda t, o=pp, l_=lhsT, r=rhs, s=(k == 0):
                       t.matmul(o, l_, r, start=s, stop=False), PE)
                EI('t', lambda t, o=pp, oc=oc, j=j: t.matmul(
                    o, BST[0][:, oc * 128:(oc + 1) * 128], axb[0:64, 512 * j: 512 * (j + 1)],
                    start=False, stop=True), PE)
            tmm = now(PE)
            wslot_free[slot] = tmm
            EW('s', tmm)
            EW('s', ft_free[0])
            tev = None
            for j in range(2):
                tev = EI('s', lambda s, pp=pps[j], oc=oc, j=j: s.activation(
                    ft[:, 512 * j: 512 * (j + 1)], pp, AF.Identity,
                    bias=FFNB[:, oc:oc + 1]), AC)
            p512_free[2 * pr] = tev
            p512_free[2 * pr + 1] = tev
            EW('q', tev)
            tq = EI('q', lambda q, oc=oc, p=p: q.dma_start(
                ffn_dram[oc, :, 2 * p: 2 * p + 2, :].rearrange("c tg t -> c (tg t)"),
                ft[:, 0:1024]), SQ, 16)
            ft_free[0] = tq
        ffn_axfree[0] = now(PE)
        return ft_free[0]

    ffn_axfree = [None]

    p0()

    s1_scope = contextlib.ExitStack()

    def asb(name, shape, dt=BF16):
        return s1_scope.enter_context(nc.sbuf_tensor(name, shape, dt))

    xt = asb("xt1", [128, 2048], F32)
    xnb = asb("xnb1", [128, D])
    mcol = asb("mcol", [128, 1], F32)
    sqcol = asb("sqcol", [128, 1], F32)
    vcol = asb("vcol", [128, 1], F32)
    rstd0 = asb("rstd0", [128, 1], F32)
    nmcol = asb("nmcol", [128, 1], F32)
    sq2col = asb("sq2col", [128, 1], F32)
    ln_bufs1 = (xt, xnb, mcol, sqcol, sq2col, vcol, rstd0, nmcol)
    dtf = asb("dtf", [128, TH], F32)
    scr = asb("scr", [128, TH], F32)
    acum = asb("acum", [128, TH], F32)
    dt_tm = asb("dt_tm1", [128, TH])
    dtd_tm = asb("dtd_tm1", [128, TH])
    hi_tm = asb("hi_tm1", [128, TH])
    lo_tm = asb("lo_tm1", [128, TH])
    hi_fm = asb("hi_fm", [128, TH])
    lo_fm = asb("lo_fm", [128, TH])
    dt_fm = asb("dt_fm", [128, TH])
    dd_fm = asb("dd_fm", [128, TH])
    alt8 = asb("alt8", [8, 128])
    lam_all = asb("lam_all1", [128, TH])
    convin = asb("convin1", [128, THP])
    cdslot = asb("cdslot", [128, 512])
    Bfm = asb("Bfm1", [128, TH])
    Cfm = asb("Cfm1", [128, TH])
    Xtmg = asb("Xtmg", [128, NZ * 1024])
    btm = asb("btm1", [128, 128])
    xdtd = asb("xdtd1", [128, 1024])
    prevf = asb("prevf1", [128, 1024], F32)

    arb_free = [None]
    s1dt_dump = [None]
    sc1 = {'wslot': 0, 'pair': 0, 'conv_free': None, 'cd_free': None, 'mini_free': None, 'convin_free': None,
           'xtm_free': None, 'xdump_free': None, 'last_xdump': None,
           'bfm_free': None, 'cfm_free': None, 'prevall_ok': None}

    for p in range(2):
        ln_half(p, ln_bufs1, arb_free)
        s1_dt(p, (dtf, scr, acum, dt_tm, dtd_tm, hi_tm, lo_tm, alt8, lam_all,
                  hi_fm, lo_fm, dt_fm, dd_fm))
        if "tap_xn" in tap_aps and p == 0:
            EW('q', now(DV))
            EI('q', lambda q: q.dma_start(tap_aps["tap_xn"][:, :], ARB[:, :]), SQ, 16)
        for g in range(G):
            tsv = s1_group(p, g, (convin, cdslot, Bfm, Cfm, Xtmg, btm, xdtd,
                                  prevf, dt_tm, dtd_tm, lam_all), sc1)
        sc1['prevall_ok'] = tsv
        arb_free[0] = now(PE)
    EW('q', now(DV))
    if "tap_B" in tap_aps:
        EI('q', lambda q: q.dma_start(tap_aps["tap_B"][:, :], B_dram[0, :, :]), SQ, 16)
    if "tap_X" in tap_aps:
        EI('q', lambda q: q.dma_start(tap_aps["tap_X"][:, :], X_dram[0, 0, :, :]), SQ, 16)
    tex = EI('q', lambda q: q.dma_start(ex_dram[:, :], PREVALL[:, :]), SQ, 16)
    if "tap_prevall" in tap_aps:
        EI('q', lambda q: q.dma_start(tap_aps["tap_prevall"][:, :], PREVALL[:, :]), SQ, 16)
    EW('g', tex)
    tcc = EI('g', lambda g: g.collective_compute(
        "AllGather", OP.bypass, replica_groups=[list(range(NCORE))],
        ins=[ex_dram.tensor.ap().opt()], outs=[gath_dram.tensor.ap().opt()]), CC, 1)

    s1_scope.close()

    s2_scope = contextlib.ExitStack()

    def bsb(name, shape, dt=BF16):
        return s2_scope.enter_context(nc.sbuf_tensor(name, shape, dt))

    Gbuf = bsb("Gbuf", [128, 8192])
    AEFF[0] = bsb("AEFF", [128, 2048])
    BST[0] = bsb("BST", [64, 4096])
    sgtm = Gbuf
    xt2 = bsb("xt2", [128, 2048], F32)
    xnb2 = bsb("xnb2", [128, D])
    mcol2 = bsb("mcol2", [128, 1], F32)
    sqcol2 = bsb("sqcol2", [128, 1], F32)
    vcol2 = bsb("vcol2", [128, 1], F32)
    rstd2 = bsb("rstd2", [128, 1], F32)
    nmcol2 = bsb("nmcol2", [128, 1], F32)
    sq2col2 = bsb("sq2col2", [128, 1], F32)
    ln_bufs2 = (xt2, xnb2, mcol2, sqcol2, sq2col2, vcol2, rstd2, nmcol2)
    dt_tm2 = bsb("dt_tm2", [128, TH])
    dtd_tm2 = bsb("dtd_tm2", [128, TH])
    hi_tm2 = bsb("hi_tm2", [128, TH])
    lo_tm2 = bsb("lo_tm2", [128, TH])
    alt82 = bsb("alt82", [8, 128])
    lam2 = bsb("lam2", [128, TH])
    Bfm2 = bsb("Bfm2", [128, TH])
    Cfm2 = bsb("Cfm2", [128, TH])
    sgf = bsb("sgf", [128, TH])
    E_ = bsb("E_", [128, 1024])
    scm2 = bsb("scm2", [128, 128])
    btm2 = bsb("btm2", [128, 128])
    xz2 = bsb("xz2", [128, 2048])
    xdt2 = bsb("xdt2", [128, 1024])
    xdtd2 = bsb("xdtd2", [128, 1024])
    prevf2 = bsb("prevf2", [128, 1024], F32)
    prevbf = bsb("prevbf", [128, 1024])
    hiT8 = bsb("hiT8", [8, 128])
    loT8 = bsb("loT8", [8, 128])
    rhs1 = bsb("rhs1", [128, 1024])
    ea2 = bsb("ea2", [128, 32])
    t1b = bsb("t1b", [128, 2048])
    gzb = bsb("gzb", [128, 1024])
    varcol = bsb("varcol", [128, 1], F32)
    rstd3 = bsb("rstd3", [128, 1], F32)
    sin_scope = contextlib.ExitStack()
    SINALL[0] = sin_scope.enter_context(nc.sbuf_tensor("SINALL", [128, 8192], BF16))
    sin_scope.close()
    axb = bsb("axb", [64, 1024])
    ft2 = bsb("ft2", [128, 1024])

    p0_lora()
    tmz = EI('v', lambda v: v.memset(SINALL[0][:, :], 0.0), DV)
    EW('g', tcc)
    for r in range(NCORE):
        tlg = EI('g', lambda g, r=r: g.dma_start(Gbuf[:, :], gath_dram[128 * r:128 * (r + 1), :]), DQ, 16)
        EW('v', tlg)
        tmz = EI('v', lambda v, r=r: v.scalar_tensor_tensor(
            SINALL[0][:, :], Gbuf[:, :], SELB[:, r:r + 1], SINALL[0][:, :], OP.mult, OP.add), DV)
        EW('g', tmz)
    if "tap_sin" in tap_aps:
        EW('q', tmz)
        EI('q', lambda q: q.dma_start(tap_aps["tap_sin"][:, :], SINALL[0][:, :]), SQ, 16)

    arb_free2 = [now(PE)]
    sc2 = {'wslot': 0, 'pair': 0, 'conv_free': None, 'pBig1_free': None, 'pBig2_free': now(AC), 'ms_free': None, 'sq_free': None,
           'sgtm_free': now(DV), 'gn_free': None, 'bc_free': None, 'ddiag_free': None}
    for p in range(2):
        ln_half(p, ln_bufs2, arb_free2)
        EW('q', now(PE))
        EW('q', now(DV))
        EW('q', now(AC))
        trl = None
        for idx, dst in [(0, dt_tm2), (1, dtd_tm2), (2, hi_tm2), (3, lo_tm2)]:
            trl = EI('q', lambda q, i=idx, d=dst, p=p: q.dma_start(d[:, :], dt_dram[p, i, :, :]), SQ, 16)
        trl = EI('q', lambda q, p=p: q.dma_start(alt82[:, :], dt_dram[p, 4, 0:8, 0:128]), SQ, 16)
        EW('v', trl)
        EW('t', trl)
        EW('s', now(DV))
        for z in range(NZ):
            EW('t', pSc_free[0])
            tm = EI('t', lambda t, z=z: t.matmul(pSc[:, :], SELZ8[:, z * 128:(z + 1) * 128],
                                                 alt82[:, :], start=True, stop=True), PE)
            EW('s', tm)
            tse = EI('s', lambda s, z=z: s.activation(lam2[:, z * 128:(z + 1) * 128],
                                                      pSc[:, :], AF.Exp), AC)
            pSc_free[0] = tse
        for g in range(G):
            s2_group(p, g, (Bfm2, Cfm2, sgtm, sgf, E_, scm2, btm2,
                            xz2, xdt2, xdtd2, prevf2, prevbf, hiT8, loT8,
                            rhs1, ea2, t1b, gzb, varcol, rstd3,
                            dt_tm2, dtd_tm2, hi_tm2, lo_tm2, lam2), sc2)
        s2_ffn(p, axb, ft2)
        arb_free2[0] = now(PE)

    s2_scope.close()

    p4_scope = contextlib.ExitStack()
    wo2 = p4_scope.enter_context(nc.sbuf_tensor("wo2", [128, 8192], BF16))
    xf4 = p4_scope.enter_context(nc.sbuf_tensor("xf4", [128, 2 * 512], F32))
    ft4 = p4_scope.enter_context(nc.sbuf_tensor("ft4", [128, 2 * 512], BF16))
    ostg = p4_scope.enter_context(nc.sbuf_tensor("ostg", [128, 2 * 512], F32))
    af4 = p4_scope.enter_context(nc.sbuf_tensor("af4", [128, 2 * 512], BF16))

    wo_slot_ap = [WBUF[:, 0:8192], wo2[:, :]]
    wo_free = [None, None]
    EW('g', wslot_free[0])
    EW('g', wslot_free[1])
    for oc in range(32):
        slot = oc % 2
        EW('g', wo_free[slot])
        tl = EI('g', lambda g, oc=oc, d=wo_slot_ap[slot]: g.dma_start(
            d, w_outT[oc, :, :, :].rearrange("i ic o -> i (ic o)")), DQ, 16)
        EW('q', tl)
        tcache = EI('q', lambda q, oc=oc, d=wo_slot_ap[slot]: q.dma_start(woutbf[oc, :, :], d), SQ, 16)
        wo_free[slot] = tcache
    wo_free = [wo_free[0], wo_free[1]]

    xf_free = [None, None]
    ft_free4 = [None, None]
    ostg_free = [None, None]
    last_out = None
    for tg in range(4):
        EW('q', now(PE))
        tgn = None
        for cc in range(64):
            tgn = EI('q', lambda q, cc=cc, tg=tg: q.dma_start(
                ARB[:, cc * 512:(cc + 1) * 512],
                gn_dram[cc, :, 4 * tg: 4 * tg + 4, :].rearrange("c z l -> c (z l)")), SQ, 16)
        for oc in range(32):
            slot = oc % 2
            wos = wo_slot_ap[slot]
            EW('q', wo_free[slot])
            twl = EI('q', lambda q, oc=oc, d=wos: q.dma_start(d, woutbf[oc, :, :]), SQ, 16)
            EW('q', ft_free4[slot])
            tfl = EI('q', lambda q, oc=oc, tg=tg, sl=slot: q.dma_start(
                ft4[:, sl * 512:(sl + 1) * 512], ffn_dram[oc, :, tg, :]), SQ, 16)
            EW('q', xf_free[slot])
            txl = EI('q', lambda q, oc=oc, tg=tg, sl=slot: q.dma_start(
                xf4[:, sl * 512:(sl + 1) * 512], x_fm[oc * 128:(oc + 1) * 128, tg * 512:(tg + 1) * 512]), SQ, 16)
            pp = [p512a, p512b][slot]
            EW('t', p512_free[slot])
            EW('t', twl)
            EW('t', tgn)
            for ic in range(64):
                EI('t', lambda t, o=pp[:, :], w=wos, ic=ic: t.matmul(
                    o, w[:, ic * 128:(ic + 1) * 128], ARB[:, ic * 512:(ic + 1) * 512],
                    start=(ic == 0), stop=(ic == 63)), PE)
            tmm = now(PE)
            wo_free[slot] = tmm
            EW('v', tmm)
            EW('v', tfl)
            EW('v', txl)
            tv1 = EI('v', lambda v, pp=pp, sl=slot: v.tensor_add(
                af4[:, sl * 512:(sl + 1) * 512], pp[:, :], ft4[:, sl * 512:(sl + 1) * 512]), DV)
            p512_free[slot] = tv1
            ft_free4[slot] = tv1
            EW('v', ostg_free[slot])
            tv2 = EI('v', lambda v, sl=slot: v.tensor_add(
                ostg[:, sl * 512:(sl + 1) * 512], af4[:, sl * 512:(sl + 1) * 512],
                xf4[:, sl * 512:(sl + 1) * 512]), DV)
            xf_free[slot] = tv2
            EW('q', tv2)
            tq = EI('q', lambda q, oc=oc, tg=tg, sl=slot: q.dma_start(
                out_fm[oc * 128:(oc + 1) * 128, tg * 512:(tg + 1) * 512],
                ostg[:, sl * 512:(sl + 1) * 512]), SQ, 16)
            ostg_free[slot] = tq
            last_out = tq
    EW('q', last_out)
    p4_scope.close()

    with nc.Block() as block:
        @block.tensor
        def _(t):
            for ek, fn in PROG:
                if ek == 't':
                    fn(t)

        @block.vector
        def _(v):
            for ek, fn in PROG:
                if ek == 'v':
                    fn(v)

        @block.scalar
        def _(s):
            for ek, fn in PROG:
                if ek == 's':
                    fn(s)

        @block.gpsimd
        def _(g):
            for ek, fn in PROG:
                if ek == 'g':
                    fn(g)

        @block.sync
        def _(q):
            for ek, fn in PROG:
                if ek == 'q':
                    fn(q)

    st.close()
    return nc


def _host_prep(inputs):
    x = np.asarray(inputs["L0_attn_x"], np.float32)
    w_in = np.asarray(inputs["L0_attn_in_proj_w"], np.float32)
    conv_w = np.asarray(inputs["L0_attn_conv_w"], np.float32)
    conv_b = np.asarray(inputs["L0_attn_conv_b"], np.float32)
    dt_bias = np.asarray(inputs["L0_attn_dt_bias"], np.float32)
    A_log = np.asarray(inputs["L0_attn_A_log"], np.float32)
    D_ = np.asarray(inputs["L0_attn_D"], np.float32)
    norm_w = np.asarray(inputs["L0_attn_norm_w"], np.float32)
    w_out = np.asarray(inputs["L0_attn_out_proj_w"], np.float32)
    gate = np.asarray(inputs["L0_ffn_gate"], np.float32)
    kernel_ = np.asarray(inputs["L0_ffn_kernel"], np.float32)
    bias = np.asarray(inputs["L0_ffn_bias"], np.float32)
    lora_A = np.asarray(inputs["L0_ffn_lora_A"], np.float32)
    lora_B = np.asarray(inputs["L0_ffn_lora_B"], np.float32)
    lora_As = np.asarray(inputs["L0_ffn_lora_A_shared"], np.float32)
    lora_Bs = np.asarray(inputs["L0_ffn_lora_B_shared"], np.float32)

    shared = {}
    wt = np.empty((NCHUNK, 128, DC, 128), np.float32)
    for ci in range(NCHUNK):
        r0 = _chunk_rows(ci)
        blk = w_in[r0:r0 + 128, :]
        wt[ci] = blk.reshape(128, DC, 128).transpose(2, 1, 0)
    shared["w_inT"] = np.ascontiguousarray(wt)
    cd = np.zeros((80, 128, 4, 128), np.float32)
    cb = np.empty((128, 80), np.float32)
    for j in range(80):
        chs = _conv_ch_of_chunk(j)
        for k in range(4):
            cd[j, np.arange(128), k, np.arange(128)] = conv_w[chs, 0, k]
        cb[:, j] = conv_b[chs]
    shared["convdiag"] = cd
    shared["convb"] = np.ascontiguousarray(cb)
    shared["w_outT"] = np.ascontiguousarray(
        w_out.reshape(32, 128, 64, 128).transpose(0, 3, 2, 1))
    shared["kern"] = np.ascontiguousarray(
        kernel_.reshape(32, 128, 32, 128).transpose(2, 1, 0, 3))
    la = np.ascontiguousarray(
        lora_A.transpose(2, 0, 1).reshape(32, 128, 16, 32).transpose(2, 1, 0, 3).reshape(16, 128, 1024))
    shared["loraA"] = la
    shared["loraAs"] = np.ascontiguousarray(
        lora_As.transpose(1, 0).reshape(32, 128, 32).transpose(1, 0, 2).reshape(128, 1024))
    shared["loraB"] = np.ascontiguousarray(lora_B.transpose(0, 2, 1))
    shared["loraBs"] = np.ascontiguousarray(lora_Bs.T)
    shared["drow"] = np.ascontiguousarray(D_.reshape(1, 128))
    shared["alog"] = np.ascontiguousarray(A_log.reshape(128, 1))
    shared["dtb"] = np.ascontiguousarray(dt_bias.reshape(128, 1))
    shared["normw"] = np.ascontiguousarray(norm_w.reshape(64, 128).T)
    shared["ffnb"] = np.ascontiguousarray(bias.reshape(32, 128).T)
    shared["c_ibf"] = np.eye(128, dtype=BF)
    sidx = np.arange(128)
    tril = (sidx[:, None] <= sidx[None, :]).astype(BF)
    shared["c_tril"] = tril
    shared["c_ones"] = np.ones((128, 128), BF)
    nex8 = np.zeros((8, 1024), np.float32)
    for hh in range(8):
        nex8[hh, hh * 128:(hh + 1) * 128] = -1.0
    shared["c_negex8"] = nex8.astype(BF)
    selz8 = np.zeros((8, 1024), np.float32)
    for z in range(8):
        selz8[z, z * 128:(z + 1) * 128] = 1.0
    shared["c_selz8"] = selz8.astype(BF)

    in_maps = []
    for c in range(NCORE):
        k, j = c // 2, c % 2
        m = dict(shared)
        xtm = np.zeros((T + 3, D), np.float32)
        if j == 1:
            xtm[0:3] = x[k, 2045:2048]
        xtm[3:] = x[k, 2048 * j: 2048 * (j + 1)]
        m["x_tm"] = xtm
        m["x_fm"] = np.ascontiguousarray(x[k, 2048 * j: 2048 * (j + 1)].T)
        m["gaterow"] = np.ascontiguousarray(gate[c].reshape(1, 16))
        sb_ = np.zeros((128, 8), np.float32)
        if j == 1:
            sb_[:, c - 1] = 1.0
        m["selb"] = sb_
        cv = np.zeros((128, 8), np.float32)
        for _i, _v in enumerate([1e-5, 1.0, -1.0, -0.5, 1.0 / 1024]):
            cv[:, _i] = _v
        m["cvals"] = cv
        in_maps.append(m)
    return in_maps


def kernel(**inputs):
    in_maps = _host_prep(inputs)
    nc = build_nc()
    res = run_bass_kernel_spmd(nc, in_maps, core_ids=list(range(NCORE)))
    out = np.empty((4, 4096, 4096), np.float32)
    for c in range(NCORE):
        k, j = c // 2, c % 2
        out[k, 2048 * j: 2048 * (j + 1)] = res.results[c]["out_fm"].T
    return out


# version 07
# speedup vs baseline: 1.0354x; 1.0159x over previous
import contextlib
import numpy as np
import ml_dtypes
import concourse.bass as bass
import concourse.mybir as mybir
from concourse.bass_utils import run_bass_kernel_spmd

F32 = mybir.dt.float32
BF16 = mybir.dt.bfloat16
AF = mybir.ActivationFunctionType
OP = mybir.AluOpType
AX = mybir.AxisListType

NCORE = 8
T = 2048
TH = 1024
THP = TH + 3
D = 4096
DC = 32
H = 128
G = 8
HPG = 16
NZ = 8
EPS = 1e-5

ROW_GATE = 0
ROW_HS = 8192
ROW_B = 16384
ROW_C = 17408
ROW_DT = 18432
NCHUNK = 145

BF = ml_dtypes.bfloat16


def _chunk_rows(ci):
    if ci == 0:
        return ROW_DT
    g = (ci - 1) // 18
    r = (ci - 1) % 18
    if r == 0:
        return ROW_B + 128 * g
    if r == 1:
        return ROW_C + 128 * g
    if r < 10:
        return ROW_HS + 1024 * g + 128 * (r - 2)
    return ROW_GATE + 1024 * g + 128 * (r - 10)


def CH_B(g):
    return 1 + 18 * g


def CH_C(g):
    return 2 + 18 * g


def CH_HS(g, q):
    return 3 + 18 * g + q


def CH_GATE(g, q):
    return 11 + 18 * g + q


def CONV_B(g):
    return 10 * g


def CONV_C(g):
    return 10 * g + 1


def CONV_HS(g, q):
    return 10 * g + 2 + q


def _conv_ch_of_chunk(j):
    g = j // 10
    r = j % 10
    if r == 0:
        rows = ROW_B + 128 * g
    elif r == 1:
        rows = ROW_C + 128 * g
    else:
        rows = ROW_HS + 1024 * g + 128 * (r - 2)
    if ROW_HS <= rows < ROW_B:
        cbase = rows - ROW_HS
    elif ROW_B <= rows < ROW_C:
        cbase = 8192 + (rows - ROW_B)
    else:
        cbase = 9216 + (rows - ROW_C)
    return np.arange(cbase, cbase + 128)


class Sem:
    def __init__(self, h):
        self.h = h
        self.n = 0


def build_nc(taps=()):
    nc = bass.Bass("TRN2", target_bir_lowering=False)
    taps = set(taps)

    def din(name, shape, dt=F32):
        return nc.dram_tensor(name, shape, dt, kind="ExternalInput").ap()

    x_tm = din("x_tm", [T + 3, D])
    x_fm = din("x_fm", [D, T])
    w_inT = din("w_inT", [NCHUNK, 128, DC, 128])
    convdiag = din("convdiag", [80, 128, 4, 128])
    convb = din("convb", [128, 80])
    w_outT = din("w_outT", [32, 128, 64, 128])
    kern = din("kern", [32, 128, 32, 128])
    loraA = din("loraA", [16, 128, 1024])
    loraAs = din("loraAs", [128, 1024])
    loraB = din("loraB", [16, 32, 4096])
    loraBs = din("loraBs", [32, 4096])
    gaterow = din("gaterow", [1, 16])
    drow = din("drow", [1, 128])
    alog = din("alog", [128, 1])
    dtb_d = din("dtb", [128, 1])
    normw_d = din("normw", [128, 64])
    ffnb_d = din("ffnb", [128, 32])
    selb_d = din("selb", [128, 8])
    c_ibf = din("c_ibf", [128, 128], BF16)
    c_tril = din("c_tril", [128, 128], BF16)
    c_ones = din("c_ones", [128, 128], BF16)
    c_negex8 = din("c_negex8", [8, 1024], BF16)
    c_selz8 = din("c_selz8", [8, 1024], BF16)
    cvals = din("cvals", [128, 8])

    out_fm = nc.dram_tensor("out_fm", [D, T], F32, kind="ExternalOutput").ap()

    def dint(name, shape, dt=BF16):
        return nc.dram_tensor(name, shape, dt, kind="Internal").ap()

    X_dram = dint("X_dram", [G, 16, 128, 1024])
    B_dram = dint("B_dram", [G, 128, T])
    C_dram = dint("C_dram", [G, 128, T])
    dt_dram = dint("dt_dram", [2, 5, 128, TH])
    gn_dram = dint("gn_dram", [64, 128, 16, 128])
    ffn_dram = dint("ffn_dram", [32, 128, 4, 512])
    ex_dram = dint("ex_dram", [128, 8192])
    gath_dram = dint("gath_dram", [NCORE * 128, 8192])
    woutbf = dint("woutbf", [32, 128, 8192])

    tap_aps = {}
    for tname, tshape, tdt in [
        ("tap_xn", [128, DC * THP], BF16),
        ("tap_acum", [128, TH], F32),
        ("tap_prevall", [128, 8192], BF16),
        ("tap_sin", [128, 8192], BF16),
        ("tap_g", [128, 1024], BF16),
        ("tap_ms", [128, 2048], BF16),
        ("tap_gn", [128, 1024], BF16),
        ("tap_ln", [128, 16], F32),
        ("tap_B", [128, 2048], BF16),
        ("tap_X", [128, 1024], BF16),
        ("tap_pz", [128, 2048], F32),
    ]:
        if tname in taps:
            tap_aps[tname] = nc.dram_tensor(tname, tshape, tdt, kind="ExternalOutput").ap()

    PROG = []

    def E(ek, fn):
        PROG.append((ek, fn))

    def EI(ek, fn, sem, amt=1):
        h = sem.h
        PROG.append((ek, lambda e, fn=fn, h=h, amt=amt: fn(e).then_inc(h, amt)))
        sem.n += amt
        return (sem, sem.n)

    def EW(ek, tok):
        if tok is None:
            return
        sem, val = tok
        h = sem.h
        PROG.append((ek, lambda e, h=h, v=val: e.wait_ge(h, v)))

    st = contextlib.ExitStack()
    DQ = Sem(st.enter_context(nc.semaphore("DQ")))
    SQ = Sem(st.enter_context(nc.semaphore("SQ")))
    PE = Sem(st.enter_context(nc.semaphore("PEs")))
    DV = Sem(st.enter_context(nc.semaphore("DVs")))
    AC = Sem(st.enter_context(nc.semaphore("ACs")))
    CC = Sem(st.enter_context(nc.semaphore("CCs")))

    def now(sem):
        return (sem, sem.n)

    def sb(name, shape, dt=BF16):
        return st.enter_context(nc.sbuf_tensor(name, shape, dt))

    IBF = sb("IBF", [128, 128])
    TRIL = sb("TRIL", [128, 128])
    ONES = sb("ONES", [128, 128])
    NEGEX8 = sb("NEGEX8", [8, 1024])
    SELZ8 = sb("SELZ8", [8, 1024])
    CONVB = sb("CONVB", [128, 80], F32)
    NORMW = sb("NORMW", [128, 64], F32)
    FFNB = sb("FFNB", [128, 32], F32)
    DTB = sb("DTB", [128, 1], F32)
    ACOL = sb("ACOL", [128, 1], F32)
    SELB = sb("SELB", [128, 8], F32)
    DROWB = sb("DROWB", [1, 128])
    GROWB = sb("GROWB", [1, 16])
    DBC = sb("DBC", [128, 128], F32)
    GBC = sb("GBC", [128, 16], F32)
    CVALS = sb("CVALS", [128, 8], F32)
    for _i, _v in enumerate([EPS, 1.0, -1.0, -0.5, 1.0 / 1024]):
        nc.const_aps.register(F32, _v, CVALS[:, _i:_i + 1])
    WBUF = sb("WBUF", [128, 2 * 4096])
    ARB = sb("ARB", [128, DC * THP])
    PREVALL = sb("PREVALL", [128, 8192])
    ZZ = sb("ZZ", [128, 128], F32)
    AEFF = [None]
    BST = [None]
    SINALL = [None]

    p512a = st.enter_context(nc.psum_tensor("p512a", [128, 512], F32))
    p512b = st.enter_context(nc.psum_tensor("p512b", [128, 512], F32))
    pBig1 = st.enter_context(nc.psum_tensor("pBig1", [128, 1024], F32))
    pBig2 = st.enter_context(nc.psum_tensor("pBig2", [128, 1024], F32))
    pT = st.enter_context(nc.psum_tensor("pT", [128, 1024], BF16))
    pSmall = st.enter_context(nc.psum_tensor("pSmall", [128, 131], F32))
    pSc = pSmall[:, 0:128]
    pMini = pSmall[:, 128:131]

    wslot_free = [None, None]
    p512_free = [None, None, None, None]
    pT_free = [None]
    pSc_free = [None]

    ln_state = {}

    def load_wchunk(ci, slot):
        EW('g', wslot_free[slot])
        src = w_inT[ci, :, :, :].rearrange("d k c -> d (k c)")
        dst = WBUF[:, slot * 4096:(slot + 1) * 4096]
        return EI('g', lambda g, s=src, d=dst: g.dma_start(d, s), DQ, 16)

    def p0():
        for dst, src in [(IBF, c_ibf), (TRIL, c_tril), (ONES, c_ones),
                         (NEGEX8, c_negex8), (SELZ8, c_selz8),
                         (CONVB, convb), (NORMW, normw_d), (FFNB, ffnb_d),
                         (DTB, dtb_d), (SELB, selb_d), (CVALS, cvals)]:
            EI('q', lambda q, d=dst, s=src: q.dma_start(d[:, :], s[:, :]), SQ, 16)
        EW('s', now(SQ))
        EW('v', now(SQ))
        t1 = EI('g', lambda g: g.dma_start(DROWB[:, :], drow[:, :]), DQ, 16)
        t2 = EI('g', lambda g: g.dma_start(GROWB[:, :], gaterow[:, :]), DQ, 16)
        talog = EI('q', lambda q: q.dma_start(ACOL[:, :], alog[:, :]), SQ, 16)
        EW('s', talog)
        ta = EI('s', lambda s: s.activation(ACOL[:, :], ACOL[:, :], AF.Exp), AC)
        EW('v', ta)
        EI('v', lambda v: v.tensor_scalar_mul(ACOL[:, :], ACOL[:, :], -1.0), DV)
        EI('v', lambda v: v.memset(ZZ[:, :], 0.0), DV)
        EW('t', now(SQ))
        EW('t', t1)
        EW('t', t2)
        tm1 = EI('t', lambda t: t.matmul(pSc[:, :], ONES[0:1, :], DROWB[:, :],
                                         start=True, stop=True), PE)
        EW('v', tm1)
        td = EI('v', lambda v: v.tensor_copy(DBC[:, :], pSc[:, :]), DV)
        EW('t', td)
        tm2 = EI('t', lambda t: t.matmul(pSc[:, 0:16], ONES[0:1, :], GROWB[:, :],
                                         start=True, stop=True), PE)
        EW('v', tm2)
        tg_ = EI('v', lambda v: v.tensor_copy(GBC[:, :], pSc[:, 0:16]), DV)
        pSc_free[0] = tg_

    def p0_lora():
        av = AEFF[0].rearrange("p (dc r2) -> p dc r2", r2=64)
        for n in range(16):
            slot = n % 2
            EW('g', wslot_free[slot])
            tl = EI('g', lambda g, n=n, sl=slot:
                    g.dma_start(WBUF[:, sl * 4096: sl * 4096 + 1024], loraA[n, :, :]), DQ, 16)
            EW('v', tl)
            src = WBUF[:, slot * 4096: slot * 4096 + 1024].rearrange("p (dc r) -> p dc r", r=32)
            if n == 0:
                tv = EI('v', lambda v, s=src:
                        v.tensor_scalar(av[:, :, 0:32], s, GBC[:, 0:1], None, OP.mult), DV)
            else:
                tv = EI('v', lambda v, s=src, n=n:
                        v.scalar_tensor_tensor(av[:, :, 0:32], s, GBC[:, n:n + 1],
                                               av[:, :, 0:32], OP.mult, OP.add), DV)
            wslot_free[slot] = tv
        EW('g', wslot_free[0])
        tl = EI('g', lambda g: g.dma_start(WBUF[:, 0:1024], loraAs[:, :]), DQ, 16)
        EW('v', tl)
        srcs = WBUF[:, 0:1024].rearrange("p (dc r) -> p dc r", r=32)
        tv = EI('v', lambda v, s=srcs: v.tensor_copy(av[:, :, 32:64], s), DV)
        wslot_free[0] = tv
        for n in range(16):
            slot = n % 2
            EW('g', wslot_free[slot])
            tl = EI('g', lambda g, n=n, sl=slot:
                    g.dma_start(WBUF[0:32, sl * 4096: sl * 4096 + 4096], loraB[n, :, :]), DQ, 16)
            EW('v', tl)
            if n == 0:
                tv = EI('v', lambda v, sl=slot:
                        v.tensor_scalar(BST[0][0:32, :], WBUF[0:32, sl * 4096: sl * 4096 + 4096],
                                        GBC[0:32, 0:1], None, OP.mult), DV)
            else:
                tv = EI('v', lambda v, sl=slot, n=n:
                        v.scalar_tensor_tensor(BST[0][0:32, :], WBUF[0:32, sl * 4096: sl * 4096 + 4096],
                                               GBC[0:32, n:n + 1], BST[0][0:32, :], OP.mult, OP.add), DV)
            wslot_free[slot] = tv
        tl = EI('g', lambda g: g.dma_start(BST[0][32:64, :], loraBs[:, :]), DQ, 16)
        wslot_free[0] = None
        wslot_free[1] = None
        return tl

    def ln_half(p, bufs, arb_free):
        xt, xnb, mcol, sqcol, sq2col, vcol, rstd, nmcol = bufs
        xt_free = ln_state.get(id(bufs[0]))
        xnb_free = ln_state.get(('xnb', id(bufs[0])))
        last = None
        for i in range(9):
            rows = 3 if i == 8 else 128
            r0 = TH * p + (128 * i if i < 8 else 1024)
            col0 = 128 * i if i < 8 else 1024
            EW('q', xt_free)
            EW('g', xt_free)
            tl0 = EI('q', lambda q, r0=r0, rows=rows:
                     q.dma_start(xt[0:rows, 0:2048], x_tm[r0:r0 + rows, 0:2048]), SQ, 16)
            tl1 = EI('g', lambda g_, r0=r0, rows=rows:
                     g_.dma_start(xt[0:rows, 2048:4096], x_tm[r0:r0 + rows, 2048:4096]), DQ, 16)
            EW('v', tl0)
            EW('v', tl1)
            EW('v', xt_free)
            t1_ = EI('v', lambda v, rows=rows:
                     v.reduce_sum(mcol[0:rows, :], xt[0:rows, :], axis=AX.X), DV)
            EW('s', t1_)
            EW('s', xnb_free)
            tsq = EI('s', lambda s, rows=rows:
                     s.activation(xnb[0:rows, :], xt[0:rows, :], AF.Square,
                                  accum_out=sqcol[0:rows, :]), AC)
            EW('v', now(DV))
            EI('v', lambda v, rows=rows:
               v.tensor_scalar_mul(mcol[0:rows, :], mcol[0:rows, :], 1.0 / D), DV)
            EW('v', tsq)
            EW('v', now(DV))
            EI('v', lambda v, rows=rows: v.tensor_scalar_mul(sqcol[0:rows, :], sqcol[0:rows, :], 1.0 / D), DV)
            EW('v', now(DV))
            EI('v', lambda v, rows=rows: v.scalar_tensor_tensor(vcol[0:rows, :], mcol[0:rows, :],
                                                                1.0, mcol[0:rows, :], OP.mult, OP.mult), DV)
            EW('v', now(DV))
            tvar = EI('v', lambda v, rows=rows: v.tensor_sub(vcol[0:rows, :], sqcol[0:rows, :], vcol[0:rows, :]), DV)
            EW('v', now(DV))
            EW('s', tvar)
            EI('s', lambda s, rows=rows: s.activation(rstd[0:rows, :], vcol[0:rows, :], AF.Ln, bias=EPS), AC)
            trs = EI('s', lambda s, rows=rows: s.activation(rstd[0:rows, :], rstd[0:rows, :], AF.Exp, scale=-0.5), AC)
            EW('v', trs)
            tnm = EI('v', lambda v, rows=rows: v.scalar_tensor_tensor(nmcol[0:rows, :], mcol[0:rows, :],
                                                                     -1.0, rstd[0:rows, :], OP.mult, OP.mult), DV)
            EW('v', now(DV))
            EW('s', tnm)
            if "tap_ln" in tap_aps and p == 0 and i == 1 and arb_free is not None and getattr(ln_half, "_tapped", False) is False:
                ln_half._tapped = True
                EW('q', tnm)
                for _j, _c in enumerate([mcol, sqcol, sq2col, vcol, rstd, nmcol]):
                    EI('q', lambda q, j=_j, c=_c: q.dma_start(tap_aps["tap_ln"][:, 2 * j: 2 * j + 1], c[:, :], allow_slow_non_contiguous=True), SQ, 16)
                EW('v', now(SQ))
                EW('s', now(SQ))
            txn = EI('s', lambda s, rows=rows: s.activation(xnb[0:rows, :], xt[0:rows, :], AF.Identity,
                                                            bias=nmcol[0:rows, :], scale=rstd[0:rows, :]), AC)
            xt_free = txn
            ln_state[id(bufs[0])] = txn
            for bb in range(4):
                EW('t', txn)
                EW('t', pT_free[0])
                ttr = None
                for dd in range(8):
                    dc = bb * 8 + dd
                    ttr = EI('t', lambda t, dc=dc, dd=dd, rows=rows:
                             t.transpose(pT[:, dd * 128: dd * 128 + rows],
                                         xnb[0:rows, dc * 128:(dc + 1) * 128],
                                         IBF[0:rows, 0:rows]), PE)
                EW('v', ttr)
                EW('v', arb_free[0])
                arb_free[0] = None
                dst = ARB.rearrange("x (dc c) -> x dc c", c=THP)[:, bb * 8:(bb + 1) * 8, col0:col0 + rows]
                srcv = pT.rearrange("x (dd c) -> x dd c", c=128)[:, :, 0:rows]
                tev = EI('v', lambda v, d=dst, s_=srcv: v.tensor_copy(d, s_), DV)
                pT_free[0] = tev
                last = tev
            xnb_free = now(PE)
            ln_state[('xnb', id(bufs[0]))] = xnb_free
        return last

    def pair_psums(sc):
        pr = sc['pair']
        sc['pair'] = 1 - pr
        if pr == 0:
            return pr, (p512a[:, :], p512b[:, :])
        return pr, (pBig1[:, 0:512], pBig1[:, 512:1024])

    def conv_chunk(ci, cj, dest, convin, cdslot, sc, dest_wait):
        slot = sc['wslot']
        sc['wslot'] = 1 - slot
        tload = load_wchunk(ci, slot)
        EW('g', sc['cd_free'])
        tcd = EI('g', lambda g, cj=cj: g.dma_start(
            cdslot[:, :], convdiag[cj, :, :, :].rearrange("i k c -> i (k c)")), DQ, 16)
        pr, pps = pair_psums(sc)
        EW('t', tload)
        for j in range(2):
            pp = pps[j]
            EW('t', p512_free[2 * pr + j])
            for k in range(DC):
                lhsT = WBUF[:, slot * 4096 + k * 128: slot * 4096 + (k + 1) * 128]
                rhs = ARB[:, k * THP + 3 + 512 * j: k * THP + 3 + 512 * (j + 1)]
                EI('t', lambda t, o=pp, l_=lhsT, r=rhs, s=(k == 0), e=(k == DC - 1):
                   t.matmul(o, l_, r, start=s, stop=e), PE)
        EW('t', sc['mini_free'])
        EW('t', pSc_free[0])
        tmini = None
        for k in range(DC):
            lhsT = WBUF[:, slot * 4096 + k * 128: slot * 4096 + (k + 1) * 128]
            rhs = ARB[:, k * THP: k * THP + 3]
            tmini = EI('t', lambda t, l_=lhsT, r=rhs, s=(k == 0), e=(k == DC - 1):
                       t.matmul(pMini[:, :], l_, r, start=s, stop=e), PE)
        wslot_free[slot] = tmini
        EW('v', sc['convin_free'])
        EW('v', tmini)
        EI('v', lambda v: v.tensor_copy(convin[:, 0:3], pMini[:, :]), DV)
        sc['mini_free'] = now(DV)
        pSc_free[0] = now(DV)
        EI('v', lambda v, pp=pps[0]: v.tensor_copy(convin[:, 3:515], pp), DV)
        e1 = EI('v', lambda v, pp=pps[1]: v.tensor_copy(convin[:, 515:1027], pp), DV)
        p512_free[2 * pr] = e1
        p512_free[2 * pr + 1] = e1
        EW('t', tcd)
        EW('t', e1)
        EW('t', sc['conv_free'])
        for j in range(2):
            pp = [pBig2[:, 0:512], pBig2[:, 512:1024]][j]
            for k in range(4):
                lhsT = cdslot[:, k * 128:(k + 1) * 128]
                rhs = convin[:, 512 * j + k: 512 * j + k + 512]
                EI('t', lambda t, o=pp, l_=lhsT, r=rhs, s=(k == 0), e=(k == 3):
                   t.matmul(o, l_, r, start=s, stop=e), PE)
        tconv = now(PE)
        sc['cd_free'] = tconv
        sc['convin_free'] = tconv
        EW('s', tconv)
        if dest_wait is not None:
            for tok in (dest_wait if isinstance(dest_wait, list) else [dest_wait]):
                EW('s', tok)
        EI('s', lambda s, cj=cj: s.activation(dest[:, 0:512], pBig2[:, 0:512], AF.Silu,
                                              bias=CONVB[:, cj:cj + 1]), AC)
        ts = EI('s', lambda s, cj=cj: s.activation(dest[:, 512:1024], pBig2[:, 512:1024], AF.Silu,
                                                   bias=CONVB[:, cj:cj + 1]), AC)
        sc['conv_free'] = ts
        return ts

    def s1_dt(p, b):
        (dtf, scr, acum, dt_tm, dtd_tm, hi_tm, lo_tm, alt8, lam_all,
         hi_fm, lo_fm, dt_fm, dd_fm) = b
        EW('v', s1dt_dump[0])
        EW('v', now(AC))
        EW('s', now(DV))
        slot = sc1['wslot']
        sc1['wslot'] = 1 - slot
        tload = load_wchunk(0, slot)
        pr, pps = pair_psums(sc1)
        EW('t', tload)
        for j in range(2):
            pp = pps[j]
            EW('t', p512_free[2 * pr + j])
            for k in range(DC):
                lhsT = WBUF[:, slot * 4096 + k * 128: slot * 4096 + (k + 1) * 128]
                rhs = ARB[:, k * THP + 3 + 512 * j: k * THP + 3 + 512 * (j + 1)]
                EI('t', lambda t, o=pp, l_=lhsT, r=rhs, s=(k == 0), e=(k == DC - 1):
                   t.matmul(o, l_, r, start=s, stop=e), PE)
        wslot_free[slot] = now(PE)
        EW('v', now(PE))
        EI('v', lambda v, pp=pps[0]: v.tensor_copy(dtf[:, 0:512], pp), DV)
        te = EI('v', lambda v, pp=pps[1]: v.tensor_copy(dtf[:, 512:1024], pp), DV)
        p512_free[2 * pr] = te
        p512_free[2 * pr + 1] = te
        EW('v', now(DV))
        tb = EI('v', lambda v: v.tensor_scalar(dtf[:, :], dtf[:, :], DTB[:, :], None, OP.add), DV)
        EW('s', tb)
        EI('s', lambda s: s.activation(scr[:, :], dtf[:, :], AF.Abs), AC)
        EI('s', lambda s: s.activation(scr[:, :], scr[:, :], AF.Exp, scale=-1.0), AC)
        ta = EI('s', lambda s: s.activation(scr[:, :], scr[:, :], AF.Ln, bias=1.0), AC)
        EW('v', ta)
        EI('v', lambda v: v.tensor_scalar_max(dtf[:, :], dtf[:, :], 0.0), DV)
        EW('v', now(DV))
        EI('v', lambda v: v.tensor_add(dtf[:, :], dtf[:, :], scr[:, :]), DV)
        EW('v', now(DV))
        EI('v', lambda v: v.tensor_scalar_min(dtf[:, :], dtf[:, :], 100.0), DV)
        EW('v', now(DV))
        EI('v', lambda v: v.tensor_scalar(scr[:, :], dtf[:, :], ACOL[:, :], None, OP.mult), DV)
        EW('v', now(DV))
        for z in range(NZ):
            EI('v', lambda v, z=z: v.tensor_tensor_scan(
                acum[:, z * 128:(z + 1) * 128], scr[:, z * 128:(z + 1) * 128],
                ZZ[:, :], 0.0, OP.add, OP.add), DV)
        EW('v', now(DV))
        EI('v', lambda v: v.tensor_copy(hi_fm[:, :], acum[:, :]), DV)
        EW('v', now(DV))
        EI('v', lambda v: v.tensor_tensor(lo_fm[:, :], acum[:, :], hi_fm[:, :], OP.subtract), DV)
        EI('v', lambda v: v.tensor_copy(dt_fm[:, :], dtf[:, :]), DV)
        acv = acum.rearrange("h (z l) -> h z l", l=128)
        alb = acv[:, :, 127:128].broadcast_to([128, NZ, 128])
        tda = EI('v', lambda v, a=acv, b_=alb: v.tensor_tensor(
            scr.rearrange("h (z l) -> h z l", l=128), b_, a, OP.subtract), DV)
        EW('s', tda)
        tdd = EI('s', lambda s: s.activation(dd_fm[:, :], scr[:, :], AF.Exp), AC)
        EW('v', tdd)
        EW('v', now(DV))
        EI('v', lambda v: v.tensor_mul(dd_fm[:, :], dd_fm[:, :], dt_fm[:, :]), DV)
        EW('v', now(DV))
        tal = EI('v', lambda v, a=acv: v.tensor_copy(dt_tm[:, 0:NZ], a[:, :, 127]), DV)
        EW('t', tal)
        EW('t', pT_free[0])
        ttr = EI('t', lambda t: t.transpose(pT[0:NZ, 0:128], dt_tm[:, 0:NZ], IBF[:, :]), PE)
        EW('v', ttr)
        tv = EI('v', lambda v: v.tensor_copy(alt8[:, :], pT[0:NZ, 0:128]), DV)
        pT_free[0] = tv
        for (src, dst) in [(hi_fm, hi_tm), (lo_fm, lo_tm), (dt_fm, dt_tm), (dd_fm, dtd_tm)]:
            EW('t', now(DV))
            EW('t', pT_free[0])
            ttr = None
            for z in range(NZ):
                ttr = EI('t', lambda t, s_=src, z=z:
                         t.transpose(pT[:, z * 128:(z + 1) * 128],
                                     s_[:, z * 128:(z + 1) * 128], IBF[:, :]), PE)
            EW('v', ttr)
            dstv = dst.rearrange("l (z h) -> l z h", h=128)
            srcv = pT.rearrange("l (z h) -> l z h", h=128)
            tv = EI('v', lambda v, d=dstv, s_=srcv: v.tensor_copy(d, s_), DV)
            pT_free[0] = tv
        EW('t', now(DV))
        for z in range(NZ):
            EW('t', pSc_free[0])
            tm = EI('t', lambda t, z=z: t.matmul(pSc[:, :], SELZ8[:, z * 128:(z + 1) * 128],
                                                 alt8[:, :], start=True, stop=True), PE)
            EW('s', tm)
            tse = EI('s', lambda s, z=z: s.activation(lam_all[:, z * 128:(z + 1) * 128],
                                                      pSc[:, :], AF.Exp), AC)
            pSc_free[0] = tse
        EW('q', now(DV))
        EW('q', now(AC))
        for idx, src in [(0, dt_tm), (1, dtd_tm), (2, hi_tm), (3, lo_tm)]:
            EI('q', lambda q, i=idx, s_=src, p=p:
               q.dma_start(dt_dram[p, i, :, :], s_[:, :]), SQ, 16)
        tdump = EI('q', lambda q, p=p: q.dma_start(dt_dram[p, 4, 0:8, 0:128], alt8[:, :]), SQ, 16)
        s1dt_dump[0] = tdump
        if "tap_acum" in tap_aps and p == 0:
            EI('q', lambda q: q.dma_start(tap_aps["tap_acum"][:, :], acum[:, :]), SQ, 16)
        return tdump

    def s1_group(p, g, b, sc):
        (convin, cdslot, Bfm, Cfm, Xtmg, btm, xdtd, prevf,
         dt_tm, dtd_tm, lam_all) = b
        tb = conv_chunk(CH_B(g), CONV_B(g), Bfm, convin, cdslot, sc, sc['bfm_free'])
        EW('q', tb)
        tbd = EI('q', lambda q, g=g, p=p: q.dma_start(B_dram[g, :, TH * p: TH * (p + 1)], Bfm[:, :]), SQ, 16)
        tc = conv_chunk(CH_C(g), CONV_C(g), Cfm, convin, cdslot, sc, sc['cfm_free'])
        EW('q', tc)
        tcd_ = EI('q', lambda q, g=g, p=p: q.dma_start(C_dram[g, :, TH * p: TH * (p + 1)], Cfm[:, :]), SQ, 16)
        cfm_wait = [tcd_]
        for q_ in range(8):
            tx = conv_chunk(CH_HS(g, q_), CONV_HS(g, q_), Cfm, convin, cdslot, sc, cfm_wait)
            EW('t', tx)
            EW('t', pT_free[0])
            ttr = None
            for z in range(NZ):
                ttr = EI('t', lambda t, z=z: t.transpose(pT[:, z * 128:(z + 1) * 128],
                                                         Cfm[:, z * 128:(z + 1) * 128], IBF[:, :]), PE)
            cfm_wait = [ttr]
            EW('v', ttr)
            EW('v', sc['xtm_free'])
            sc['xtm_free'] = None
            EW('v', sc['xdump_free'])
            sc['xdump_free'] = None
            dstv = Xtmg.rearrange("l (z c) -> l z c", c=1024)[:, :, q_ * 128:(q_ + 1) * 128]
            srcv = pT.rearrange("l (z c) -> l z c", c=128)
            tev = EI('v', lambda v, d=dstv, s_=srcv: v.tensor_copy(d, s_), DV)
            pT_free[0] = tev
            EW('q', tev)
            txd_ = EI('q', lambda q, g=g, p=p, q_=q_, dv_=dstv: q.dma_start(
                X_dram[g, NZ * p: NZ * (p + 1), :, q_ * 128:(q_ + 1) * 128].rearrange("z l c -> l z c"),
                dv_), SQ, 16)
            sc['last_xdump'] = txd_
        if p == 0:
            tpi = EI('v', lambda v: v.memset(prevf[:, :], 0.0), DV)
        else:
            EW('v', sc['prevall_ok'])
            tpi = EI('v', lambda v, g=g: v.tensor_copy(prevf[:, :], PREVALL[:, g * 1024:(g + 1) * 1024]), DV)
        tst = None
        for z in range(NZ):
            EW('v', now(DV))
            if "tap_pz" in tap_aps and p == 0 and g == 0 and z == 1:
                EW('q', tpi)
                EI('q', lambda q: q.dma_start(tap_aps["tap_pz"][:, 0:1024], prevf[:, :]), SQ, 16)
                EI('g', lambda g_: g_.dma_start(tap_aps["tap_pz"][:, 1024:1040], lam_all[:, 0:16]), DQ, 16)
                EI('g', lambda g_: g_.dma_start(tap_aps["tap_pz"][:, 1040:1056], dtd_tm[:, 0:16]), DQ, 16)
                EI('g', lambda g_: g_.dma_start(tap_aps["tap_pz"][:, 1056:1072], dt_tm[:, 0:16]), DQ, 16)
                EI('g', lambda g_: g_.dma_start(tap_aps["tap_pz"][:, 1072:2048], xdtd[:, 0:976]), DQ, 16)
                EW('v', now(DQ))
                EW('v', now(SQ))
            EW('v', tst)
            EW('t', pT_free[0])
            ttr = EI('t', lambda t, z=z: t.transpose(pT[:, 0:128], Bfm[:, z * 128:(z + 1) * 128],
                                                     IBF[:, :]), PE)
            EW('v', ttr)
            tbm = EI('v', lambda v: v.tensor_copy(btm[:, :], pT[:, 0:128]), DV)
            pT_free[0] = tbm
            xv = Xtmg[:, z * 1024:(z + 1) * 1024].rearrange("l (hh pp) -> l hh pp", pp=64)
            dd = dtd_tm[:, z * 128 + HPG * g: z * 128 + HPG * (g + 1)].unsqueeze(2).broadcast_to([128, HPG, 64])
            txd = EI('v', lambda v, xv=xv, dd=dd: v.tensor_tensor(
                xdtd.rearrange("l (hh pp) -> l hh pp", pp=64), xv, dd, OP.mult), DV)
            EW('t', txd)
            for j in range(2):
                pp = [p512a, p512b][j]
                EW('t', p512_free[j])
                EI('t', lambda t, o=pp[:, :], j=j: t.matmul(
                    o, btm[:, :], xdtd[:, 512 * j: 512 * (j + 1)], start=True, stop=True), PE)
            tst = now(PE)
            lamv = lam_all[:, z * 128 + HPG * g: z * 128 + HPG * (g + 1)].unsqueeze(2).broadcast_to([128, HPG, 64])
            EW('v', now(DV))
            EI('v', lambda v, lamv=lamv: v.tensor_tensor(
                prevf.rearrange("n (hh pp) -> n hh pp", pp=64),
                prevf.rearrange("n (hh pp) -> n hh pp", pp=64), lamv, OP.mult), DV)
            EW('v', now(DV))
            EW('v', tst)
            EI('v', lambda v: v.tensor_add(prevf[:, 0:512], prevf[:, 0:512], p512a[:, :]), DV)
            tpu = EI('v', lambda v: v.tensor_add(prevf[:, 512:1024], prevf[:, 512:1024], p512b[:, :]), DV)
            p512_free[0] = tpu
            p512_free[1] = tpu
        EW('v', now(DV))
        tsv = EI('v', lambda v, g=g: v.tensor_copy(PREVALL[:, g * 1024:(g + 1) * 1024], prevf[:, :]), DV)
        sc['xtm_free'] = tst
        sc['xdump_free'] = sc['last_xdump']
        sc['bfm_free'] = [now(PE), tbd]
        sc['cfm_free'] = cfm_wait + [sc['last_xdump']]
        return tsv

    def s2_group(p, g, b, sc):
        (Bfm, Cfm, sgtm, sgf, E_, scm, btm, xz, xdt, xdtd,
         prevf, prevbf, hiT8, loT8, rhs1, ea, t1b, gzb,
         varcol, rstd, dt_tm, dtd_tm, hi_tm, lo_tm, lam_all) = b
        EW('q', sc['bc_free'])
        tlb = EI('q', lambda q, g=g, p=p: q.dma_start(Bfm[:, :], B_dram[g, :, TH * p: TH * (p + 1)]), SQ, 16)
        tlc = EI('q', lambda q, g=g, p=p: q.dma_start(Cfm[:, :], C_dram[g, :, TH * p: TH * (p + 1)]), SQ, 16)
        sgf_free = None
        for q_ in range(8):
            slot = sc['wslot']
            sc['wslot'] = 1 - slot
            tload = load_wchunk(CH_GATE(g, q_), slot)
            pr, pps = pair_psums(sc)
            EW('t', tload)
            for j in range(2):
                pp = pps[j]
                EW('t', p512_free[2 * pr + j])
                for k in range(DC):
                    lhsT = WBUF[:, slot * 4096 + k * 128: slot * 4096 + (k + 1) * 128]
                    rhs = ARB[:, k * THP + 3 + 512 * j: k * THP + 3 + 512 * (j + 1)]
                    EI('t', lambda t, o=pp, l_=lhsT, r=rhs, s=(k == 0), e=(k == DC - 1):
                       t.matmul(o, l_, r, start=s, stop=e), PE)
            wslot_free[slot] = now(PE)
            EW('s', now(PE))
            EW('s', sgf_free)
            EI('s', lambda s, pp=pps[0]: s.activation(sgf[:, 0:512], pp, AF.Silu), AC)
            tsg = EI('s', lambda s, pp=pps[1]: s.activation(sgf[:, 512:1024], pp, AF.Silu), AC)
            p512_free[2 * pr] = tsg
            p512_free[2 * pr + 1] = tsg
            EW('t', tsg)
            EW('t', pT_free[0])
            ttr = None
            for z in range(NZ):
                ttr = EI('t', lambda t, z=z: t.transpose(pT[:, z * 128:(z + 1) * 128],
                                                         sgf[:, z * 128:(z + 1) * 128], IBF[:, :]), PE)
            sgf_free = ttr
            EW('v', ttr)
            EW('v', sc['sgtm_free'])
            sc['sgtm_free'] = None
            dstv = sgtm.rearrange("l (z c) -> l z c", c=1024)[:, :, q_ * 128:(q_ + 1) * 128]
            srcv = pT.rearrange("l (z c) -> l z c", c=128)
            tev = EI('v', lambda v, d=dstv, s_=srcv: v.tensor_copy(d, s_), DV)
            pT_free[0] = tev
        if p == 0:
            tpi = EI('v', lambda v, g=g: v.tensor_copy(prevf[:, :], SINALL[0][:, g * 1024:(g + 1) * 1024]), DV)
        else:
            tpi = EI('v', lambda v, g=g: v.tensor_copy(prevf[:, :], PREVALL[:, g * 1024:(g + 1) * 1024]), DV)
        EW('t', tlb)
        EW('t', tlc)
        xz_free = [None, None]
        z_pe = None
        for z in range(NZ):
            gz = NZ * p + z
            xsl = z % 2
            EW('q', xz_free[xsl])
            tlx = EI('q', lambda q, g=g, gz=gz, xsl=xsl: q.dma_start(
                xz[:, xsl * 1024:(xsl + 1) * 1024], X_dram[g, gz, :, :]), SQ, 16)
            xzv = xz[:, xsl * 1024:(xsl + 1) * 1024]
            eav_ = ea[:, 16 * xsl: 16 * (xsl + 1)]
            t1v = t1b[:, 1024 * xsl: 1024 * (xsl + 1)]
            EW('v', z_pe)
            EW('t', pT_free[0])
            ttr = EI('t', lambda t, z=z: t.transpose(pT[:, 0:128], Bfm[:, z * 128:(z + 1) * 128],
                                                     IBF[:, :]), PE)
            EW('v', ttr)
            tbm = EI('v', lambda v: v.tensor_copy(btm[:, :], pT[:, 0:128]), DV)
            pT_free[0] = tbm
            EW('t', pSc_free[0])
            tsc = EI('t', lambda t, z=z: t.matmul(pSc[:, :], Bfm[:, z * 128:(z + 1) * 128],
                                                  Cfm[:, z * 128:(z + 1) * 128], start=True, stop=True), PE)
            EW('v', tsc)
            tscm = EI('v', lambda v: v.tensor_tensor(scm[:, :], pSc[:, :], TRIL[:, :], OP.mult), DV)
            pSc_free[0] = tscm
            EW('v', tlx)
            dtv = dt_tm[:, z * 128 + HPG * g: z * 128 + HPG * (g + 1)].unsqueeze(2).broadcast_to([128, HPG, 64])
            ddv = dtd_tm[:, z * 128 + HPG * g: z * 128 + HPG * (g + 1)].unsqueeze(2).broadcast_to([128, HPG, 64])
            xvv = xzv.rearrange("l (hh pp) -> l hh pp", pp=64)
            txdt = EI('v', lambda v, xvv=xvv, dtv=dtv: v.tensor_tensor(
                xdt.rearrange("l (hh pp) -> l hh pp", pp=64), xvv, dtv, OP.mult), DV)
            txdd = EI('v', lambda v, xvv=xvv, ddv=ddv: v.tensor_tensor(
                xdtd.rearrange("l (hh pp) -> l hh pp", pp=64), xvv, ddv, OP.mult), DV)
            dbv = DBC[:, HPG * g: HPG * (g + 1)].unsqueeze(2).broadcast_to([128, HPG, 64])
            EW('v', sc['sq_free'])
            sc['sq_free'] = None
            txD = EI('v', lambda v, xvv=xvv, dbv=dbv: v.tensor_tensor(
                gzb.rearrange("l (hh pp) -> l hh pp", pp=64), xvv, dbv, OP.mult), DV)
            if z == 0:
                EW('t', p512_free[2])
                EW('t', p512_free[3])
            first_y = True
            for q2 in range(2):
                h0 = z * 128 + HPG * g + 8 * q2
                EW('v', sc['ms_free'])
                sc['ms_free'] = None
                EW('t', pT_free[0])
                EI('t', lambda t, h0=h0: t.transpose(pT[0:8, 0:128], hi_tm[:, h0:h0 + 8], IBF[:, :]), PE)
                t2_ = EI('t', lambda t, h0=h0: t.transpose(pT[0:8, 128:256], lo_tm[:, h0:h0 + 8], IBF[:, :]), PE)
                EW('v', t2_)
                EI('v', lambda v: v.tensor_copy(hiT8[:, :], pT[0:8, 0:128]), DV)
                tcl = EI('v', lambda v: v.tensor_copy(loT8[:, :], pT[0:8, 128:256]), DV)
                pT_free[0] = tcl
                ex = IBF[:, :].unsqueeze(1).broadcast_to([128, 8, 128])
                hb = hi_tm[:, h0:h0 + 8].unsqueeze(2).broadcast_to([128, 8, 128])
                lb = lo_tm[:, h0:h0 + 8].unsqueeze(2).broadcast_to([128, 8, 128])
                tr1 = EI('v', lambda v, ex=ex, hb=hb: v.tensor_tensor(
                    rhs1.rearrange("l (hh c) -> l hh c", c=128), ex, hb, OP.mult), DV)
                EW('t', sc['pBig2_free'])
                sc['pBig2_free'] = None
                EW('t', tcl)
                EW('t', tr1)
                for j in range(2):
                    EI('t', lambda t, j=j: t.matmul(pBig2[:, 512 * j:512 * (j + 1)], hiT8[:, :],
                                                    NEGEX8[:, 512 * j: 512 * (j + 1)], start=True, stop=False), PE)
                    EI('t', lambda t, j=j: t.matmul(pBig2[:, 512 * j:512 * (j + 1)], loT8[:, :],
                                                    NEGEX8[:, 512 * j: 512 * (j + 1)], start=False, stop=False), PE)
                    EI('t', lambda t, j=j: t.matmul(pBig2[:, 512 * j:512 * (j + 1)], ONES[:, :],
                                                    rhs1[:, 512 * j: 512 * (j + 1)], start=False, stop=False), PE)
                th1 = now(PE)
                EW('v', th1)
                tr2 = EI('v', lambda v, ex=ex, lb=lb: v.tensor_tensor(
                    rhs1.rearrange("l (hh c) -> l hh c", c=128), ex, lb, OP.mult), DV)
                EW('t', tr2)
                tmd = None
                for j in range(2):
                    tmd = EI('t', lambda t, j=j: t.matmul(pBig2[:, 512 * j:512 * (j + 1)], ONES[:, :],
                                                          rhs1[:, 512 * j: 512 * (j + 1)], start=False, stop=True), PE)
                EW('v', tmd)
                EW('v', sc['sq_free'])
                sc['sq_free'] = None
                tmn = EI('v', lambda v: v.tensor_scalar_min(E_[:, :], pBig2[:, :], 0.0), DV)
                sc['pBig2_free'] = tmn
                EW('s', tmn)
                EW('s', sc['ms_free'])
                tex = EI('s', lambda s: s.activation(E_[:, :], E_[:, :], AF.Exp), AC)
                EW('v', tex)
                EW('v', tscm)
                scb = scm[:, :].unsqueeze(1).broadcast_to([128, 8, 128])
                tms = EI('v', lambda v, scb=scb: v.tensor_tensor(
                    E_.rearrange("s (hh c) -> s hh c", c=128),
                    E_.rearrange("s (hh c) -> s hh c", c=128), scb, OP.mult), DV)
                if first_y:
                    EW('t', sc['pBig1_free'])
                    sc['pBig1_free'] = None
                    first_y = False
                EW('t', tms)
                EW('t', txdt)
                EW('t', txD)
                for hh8 in range(8):
                    hh = 8 * q2 + hh8
                    EI('t', lambda t, hh=hh, hh8=hh8: t.matmul(pBig1[:, hh * 64:(hh + 1) * 64],
                                                               E_[:, hh8 * 128:(hh8 + 1) * 128],
                                                               xdt[:, hh * 64:(hh + 1) * 64], start=True, stop=False), PE)
                    EI('t', lambda t, hh=hh: t.matmul(pBig1[:, hh * 64:(hh + 1) * 64],
                                                      IBF[:, :],
                                                      gzb[:, hh * 64:(hh + 1) * 64], start=False, stop=True), PE)
                sc['ms_free'] = now(PE)
            tyd = now(PE)
            EW('v', tpi)
            tpb = EI('v', lambda v: v.tensor_copy(prevbf[:, :], prevf[:, :]), DV)
            EW('t', sc['pBig2_free'])
            sc['pBig2_free'] = None
            EW('t', tpb)
            for j in range(2):
                EI('t', lambda t, z=z, j=j: t.matmul(pBig2[:, 512 * j:512 * (j + 1)],
                                                     Cfm[:, z * 128:(z + 1) * 128],
                                                     prevbf[:, 512 * j:512 * (j + 1)], start=True, stop=True), PE)
            tya = now(PE)
            EW('t', txdd)
            for j in range(2):
                pp = [p512a, p512b][j]
                EW('t', p512_free[j])
                EI('t', lambda t, o=pp[:, :], j=j: t.matmul(
                    o, btm[:, :], xdtd[:, 512 * j: 512 * (j + 1)], start=True, stop=True), PE)
            tst = now(PE)
            lamv = lam_all[:, z * 128 + HPG * g: z * 128 + HPG * (g + 1)].unsqueeze(2).broadcast_to([128, HPG, 64])
            EW('v', tya)
            EW('v', now(DV))
            EI('v', lambda v, lamv=lamv: v.tensor_tensor(
                prevf.rearrange("n (hh pp) -> n hh pp", pp=64),
                prevf.rearrange("n (hh pp) -> n hh pp", pp=64), lamv, OP.mult), DV)
            EW('v', now(DV))
            EW('v', tst)
            EI('v', lambda v: v.tensor_add(prevf[:, 0:512], prevf[:, 0:512], p512a[:, :]), DV)
            tpu = EI('v', lambda v: v.tensor_add(prevf[:, 512:1024], prevf[:, 512:1024], p512b[:, :]), DV)
            p512_free[0] = tpu
            p512_free[1] = tpu
            tpi = tpu
            tea = EI('s', lambda s, z=z, g=g, eav_=eav_: s.activation(
                eav_, hi_tm[:, z * 128 + HPG * g: z * 128 + HPG * (g + 1)], AF.Exp), AC)
            EW('v', tea)
            EW('v', sc['gn_free'])
            sc['gn_free'] = None
            eav = eav_.unsqueeze(2).broadcast_to([128, HPG, 64])
            tt1 = EI('v', lambda v, eav=eav, t1v=t1v: v.tensor_tensor(
                t1v.rearrange("l (hh pp) -> l hh pp", pp=64),
                pBig2.rearrange("l (hh pp) -> l hh pp", pp=64), eav, OP.mult), DV)
            sc['pBig2_free'] = tt1
            EW('v', tyd)
            EW('v', now(DV))
            tt2 = EI('v', lambda v, t1v=t1v: v.tensor_add(gzb[:, :], pBig1[:, :], t1v), DV)
            sc['pBig1_free'] = tt2
            EW('v', now(DV))
            tgt = EI('v', lambda v, z=z: v.tensor_mul(gzb[:, :], gzb[:, :],
                                                      sgtm[:, z * 1024:(z + 1) * 1024]), DV)
            if "tap_g" in tap_aps and p == 0 and g == 0 and z == 0:
                EW('q', tgt)
                EI('q', lambda q: q.dma_start(tap_aps["tap_g"][:, :], gzb[:, :]), SQ, 16)
            EW('s', tgt)
            tsqg = EI('s', lambda s: s.activation(E_[:, :], gzb[:, :], AF.Square,
                                                  accum_out=varcol[:, :]), AC)
            sc['sq_free'] = tsqg
            EI('s', lambda s: s.activation(rstd[:, :], varcol[:, :], AF.Ln,
                                           bias=EPS, scale=1.0 / 1024), AC)
            tsr = EI('s', lambda s: s.activation(rstd[:, :], rstd[:, :], AF.Exp, scale=-0.5), AC)
            EW('v', tsr)
            tgn = EI('v', lambda v: v.tensor_scalar(gzb[:, :], gzb[:, :], rstd[:, :], None, OP.mult), DV)
            EW('t', tgn)
            EW('t', pT_free[0])
            ttr = None
            for cc in range(8):
                ttr = EI('t', lambda t, cc=cc: t.transpose(pT[:, cc * 128:(cc + 1) * 128],
                                                           gzb[:, cc * 128:(cc + 1) * 128], IBF[:, :]), PE)
            EW('v', ttr)
            tev = None
            for cc in range(8):
                tev = EI('v', lambda v, cc=cc, g=g, t1v=t1v: v.tensor_scalar(
                    t1v[:, cc * 128:(cc + 1) * 128], pT[:, cc * 128:(cc + 1) * 128],
                    NORMW[:, 8 * g + cc: 8 * g + cc + 1], None, OP.mult), DV)
            pT_free[0] = tev
            EW('q', tev)
            tdq = None
            for cc in range(8):
                tdq = EI('q', lambda q, cc=cc, g=g, gz=gz, t1v=t1v: q.dma_start(
                    gn_dram[8 * g + cc, :, gz, :], t1v[:, cc * 128:(cc + 1) * 128]), SQ, 16)
            sc['gn_free'] = tdq
            if "tap_gn" in tap_aps and p == 0 and g == 0 and z == 0:
                EI('q', lambda q, t1v=t1v: q.dma_start(tap_aps["tap_gn"][:, :], t1v[:, 0:1024]), SQ, 16)
            xz_free[xsl] = txD
            z_pe = now(PE)
        EW('v', now(DV))
        tsv = EI('v', lambda v, g=g: v.tensor_copy(PREVALL[:, g * 1024:(g + 1) * 1024], prevf[:, :]), DV)
        if sc['pBig1_free'] is not None:
            p512_free[2] = sc['pBig1_free']
            p512_free[3] = sc['pBig1_free']
        sc['bc_free'] = z_pe
        sc['ddiag_free'] = z_pe
        return tsv

    def s2_ffn(p, axb, ft):
        for j in range(2):
            EW('t', p512_free[0])
            tmm = None
            for dc in range(DC):
                lhsT = AEFF[0][:, dc * 64:(dc + 1) * 64]
                rhs = ARB[:, dc * THP + 3 + 512 * j: dc * THP + 3 + 512 * (j + 1)]
                tmm = EI('t', lambda t, l_=lhsT, r=rhs, s=(dc == 0), e=(dc == DC - 1):
                         t.matmul(p512a[0:64, :], l_, r, start=s, stop=e), PE)
            EW('v', tmm)
            EW('v', p == 1 and ffn_axfree[0] or None)
            tev = EI('v', lambda v, j=j: v.tensor_copy(axb[0:64, 512 * j: 512 * (j + 1)], p512a[0:64, :]), DV)
            p512_free[0] = tev
        axtok = now(DV)
        ft_free = [None, None]
        for oc in range(32):
            slot = oc % 2
            EW('g', wslot_free[slot])
            tload = EI('g', lambda g, oc=oc, sl=slot: g.dma_start(
                WBUF[:, sl * 4096:(sl + 1) * 4096],
                kern[oc, :, :, :].rearrange("d k c -> d (k c)")), DQ, 16)
            pr, pps = pair_psums(sc2)
            EW('t', tload)
            EW('t', axtok)
            for j in range(2):
                pp = pps[j]
                EW('t', p512_free[2 * pr + j])
                for k in range(DC):
                    lhsT = WBUF[:, slot * 4096 + k * 128: slot * 4096 + (k + 1) * 128]
                    rhs = ARB[:, k * THP + 3 + 512 * j: k * THP + 3 + 512 * (j + 1)]
                    EI('t', lambda t, o=pp, l_=lhsT, r=rhs, s=(k == 0):
                       t.matmul(o, l_, r, start=s, stop=False), PE)
                EI('t', lambda t, o=pp, oc=oc, j=j: t.matmul(
                    o, BST[0][:, oc * 128:(oc + 1) * 128], axb[0:64, 512 * j: 512 * (j + 1)],
                    start=False, stop=True), PE)
            tmm = now(PE)
            wslot_free[slot] = tmm
            EW('s', tmm)
            EW('s', ft_free[0])
            tev = None
            for j in range(2):
                tev = EI('s', lambda s, pp=pps[j], oc=oc, j=j: s.activation(
                    ft[:, 512 * j: 512 * (j + 1)], pp, AF.Identity,
                    bias=FFNB[:, oc:oc + 1]), AC)
            p512_free[2 * pr] = tev
            p512_free[2 * pr + 1] = tev
            EW('q', tev)
            tq = EI('q', lambda q, oc=oc, p=p: q.dma_start(
                ffn_dram[oc, :, 2 * p: 2 * p + 2, :].rearrange("c tg t -> c (tg t)"),
                ft[:, 0:1024]), SQ, 16)
            ft_free[0] = tq
        ffn_axfree[0] = now(PE)
        return ft_free[0]

    ffn_axfree = [None]

    p0()

    s1_scope = contextlib.ExitStack()

    def asb(name, shape, dt=BF16):
        return s1_scope.enter_context(nc.sbuf_tensor(name, shape, dt))

    xt = asb("xt1", [128, D], F32)
    xnb = asb("xnb1", [128, D])
    mcol = asb("mcol", [128, 1], F32)
    sqcol = asb("sqcol", [128, 1], F32)
    vcol = asb("vcol", [128, 1], F32)
    rstd0 = asb("rstd0", [128, 1], F32)
    nmcol = asb("nmcol", [128, 1], F32)
    sq2col = asb("sq2col", [128, 1], F32)
    ln_bufs1 = (xt, xnb, mcol, sqcol, sq2col, vcol, rstd0, nmcol)
    dtf = asb("dtf", [128, TH], F32)
    scr = asb("scr", [128, TH], F32)
    acum = asb("acum", [128, TH], F32)
    dt_tm = asb("dt_tm1", [128, TH])
    dtd_tm = asb("dtd_tm1", [128, TH])
    hi_tm = asb("hi_tm1", [128, TH])
    lo_tm = asb("lo_tm1", [128, TH])
    hi_fm = asb("hi_fm", [128, TH])
    lo_fm = asb("lo_fm", [128, TH])
    dt_fm = asb("dt_fm", [128, TH])
    dd_fm = asb("dd_fm", [128, TH])
    alt8 = asb("alt8", [8, 128])
    lam_all = asb("lam_all1", [128, TH])
    convin = asb("convin1", [128, THP])
    cdslot = asb("cdslot", [128, 512])
    Bfm = asb("Bfm1", [128, TH])
    Cfm = asb("Cfm1", [128, TH])
    Xtmg = asb("Xtmg", [128, NZ * 1024])
    btm = asb("btm1", [128, 128])
    xdtd = asb("xdtd1", [128, 1024])
    prevf = asb("prevf1", [128, 1024], F32)

    arb_free = [None]
    s1dt_dump = [None]
    sc1 = {'wslot': 0, 'pair': 0, 'conv_free': None, 'cd_free': None, 'mini_free': None, 'convin_free': None,
           'xtm_free': None, 'xdump_free': None, 'last_xdump': None,
           'bfm_free': None, 'cfm_free': None, 'prevall_ok': None}

    for p in range(2):
        ln_half(p, ln_bufs1, arb_free)
        s1_dt(p, (dtf, scr, acum, dt_tm, dtd_tm, hi_tm, lo_tm, alt8, lam_all,
                  hi_fm, lo_fm, dt_fm, dd_fm))
        if "tap_xn" in tap_aps and p == 0:
            EW('q', now(DV))
            EI('q', lambda q: q.dma_start(tap_aps["tap_xn"][:, :], ARB[:, :]), SQ, 16)
        for g in range(G):
            tsv = s1_group(p, g, (convin, cdslot, Bfm, Cfm, Xtmg, btm, xdtd,
                                  prevf, dt_tm, dtd_tm, lam_all), sc1)
        sc1['prevall_ok'] = tsv
        arb_free[0] = now(PE)
    EW('q', now(DV))
    if "tap_B" in tap_aps:
        EI('q', lambda q: q.dma_start(tap_aps["tap_B"][:, :], B_dram[0, :, :]), SQ, 16)
    if "tap_X" in tap_aps:
        EI('q', lambda q: q.dma_start(tap_aps["tap_X"][:, :], X_dram[0, 0, :, :]), SQ, 16)
    tex = EI('q', lambda q: q.dma_start(ex_dram[:, :], PREVALL[:, :]), SQ, 16)
    if "tap_prevall" in tap_aps:
        EI('q', lambda q: q.dma_start(tap_aps["tap_prevall"][:, :], PREVALL[:, :]), SQ, 16)
    EW('g', tex)
    tcc = EI('g', lambda g: g.collective_compute(
        "AllGather", OP.bypass, replica_groups=[list(range(NCORE))],
        ins=[ex_dram.tensor.ap().opt()], outs=[gath_dram.tensor.ap().opt()]), CC, 1)

    s1_scope.close()

    s2_scope = contextlib.ExitStack()

    def bsb(name, shape, dt=BF16):
        return s2_scope.enter_context(nc.sbuf_tensor(name, shape, dt))

    Gbuf = bsb("Gbuf", [128, 8192])
    AEFF[0] = bsb("AEFF", [128, 2048])
    BST[0] = bsb("BST", [64, 4096])
    sgtm = Gbuf
    xt2 = bsb("xt2", [128, D], F32)
    xnb2 = bsb("xnb2", [128, D])
    mcol2 = bsb("mcol2", [128, 1], F32)
    sqcol2 = bsb("sqcol2", [128, 1], F32)
    vcol2 = bsb("vcol2", [128, 1], F32)
    rstd2 = bsb("rstd2", [128, 1], F32)
    nmcol2 = bsb("nmcol2", [128, 1], F32)
    sq2col2 = bsb("sq2col2", [128, 1], F32)
    ln_bufs2 = (xt2, xnb2, mcol2, sqcol2, sq2col2, vcol2, rstd2, nmcol2)
    dt_tm2 = bsb("dt_tm2", [128, TH])
    dtd_tm2 = bsb("dtd_tm2", [128, TH])
    hi_tm2 = bsb("hi_tm2", [128, TH])
    lo_tm2 = bsb("lo_tm2", [128, TH])
    alt82 = bsb("alt82", [8, 128])
    lam2 = bsb("lam2", [128, TH])
    Bfm2 = bsb("Bfm2", [128, TH])
    Cfm2 = bsb("Cfm2", [128, TH])
    sgf = bsb("sgf", [128, TH])
    E_ = bsb("E_", [128, 1024])
    scm2 = bsb("scm2", [128, 128])
    btm2 = bsb("btm2", [128, 128])
    xz2 = bsb("xz2", [128, 2048])
    xdt2 = bsb("xdt2", [128, 1024])
    xdtd2 = bsb("xdtd2", [128, 1024])
    prevf2 = bsb("prevf2", [128, 1024], F32)
    prevbf = bsb("prevbf", [128, 1024])
    hiT8 = bsb("hiT8", [8, 128])
    loT8 = bsb("loT8", [8, 128])
    rhs1 = bsb("rhs1", [128, 1024])
    ea2 = bsb("ea2", [128, 32])
    t1b = bsb("t1b", [128, 2048])
    gzb = bsb("gzb", [128, 1024])
    varcol = bsb("varcol", [128, 1], F32)
    rstd3 = bsb("rstd3", [128, 1], F32)
    axb = bsb("axb", [64, 1024])
    ft2 = bsb("ft2", [128, 1024])
    SINALL[0] = PREVALL

    p0_lora()
    EW('v', tex)
    tmz = EI('v', lambda v: v.memset(SINALL[0][:, :], 0.0), DV)
    EW('g', tcc)
    for r in range(NCORE):
        tlg = EI('g', lambda g, r=r: g.dma_start(Gbuf[:, :], gath_dram[128 * r:128 * (r + 1), :]), DQ, 16)
        EW('v', tlg)
        tmz = EI('v', lambda v, r=r: v.scalar_tensor_tensor(
            SINALL[0][:, :], Gbuf[:, :], SELB[:, r:r + 1], SINALL[0][:, :], OP.mult, OP.add), DV)
        EW('g', tmz)
    if "tap_sin" in tap_aps:
        EW('q', tmz)
        EI('q', lambda q: q.dma_start(tap_aps["tap_sin"][:, :], SINALL[0][:, :]), SQ, 16)

    arb_free2 = [now(PE)]
    sc2 = {'wslot': 0, 'pair': 0, 'conv_free': None, 'pBig1_free': None, 'pBig2_free': now(AC), 'ms_free': None, 'sq_free': None,
           'sgtm_free': now(DV), 'gn_free': None, 'bc_free': None, 'ddiag_free': None}
    for p in range(2):
        ln_half(p, ln_bufs2, arb_free2)
        EW('q', now(PE))
        EW('q', now(DV))
        EW('q', now(AC))
        trl = None
        for idx, dst in [(0, dt_tm2), (1, dtd_tm2), (2, hi_tm2), (3, lo_tm2)]:
            trl = EI('q', lambda q, i=idx, d=dst, p=p: q.dma_start(d[:, :], dt_dram[p, i, :, :]), SQ, 16)
        trl = EI('q', lambda q, p=p: q.dma_start(alt82[:, :], dt_dram[p, 4, 0:8, 0:128]), SQ, 16)
        EW('v', trl)
        EW('t', trl)
        EW('s', now(DV))
        for z in range(NZ):
            EW('t', pSc_free[0])
            tm = EI('t', lambda t, z=z: t.matmul(pSc[:, :], SELZ8[:, z * 128:(z + 1) * 128],
                                                 alt82[:, :], start=True, stop=True), PE)
            EW('s', tm)
            tse = EI('s', lambda s, z=z: s.activation(lam2[:, z * 128:(z + 1) * 128],
                                                      pSc[:, :], AF.Exp), AC)
            pSc_free[0] = tse
        for g in range(G):
            s2_group(p, g, (Bfm2, Cfm2, sgtm, sgf, E_, scm2, btm2,
                            xz2, xdt2, xdtd2, prevf2, prevbf, hiT8, loT8,
                            rhs1, ea2, t1b, gzb, varcol, rstd3,
                            dt_tm2, dtd_tm2, hi_tm2, lo_tm2, lam2), sc2)
        s2_ffn(p, axb, ft2)
        arb_free2[0] = now(PE)

    s2_scope.close()

    p4_scope = contextlib.ExitStack()
    wo2 = p4_scope.enter_context(nc.sbuf_tensor("wo2", [128, 8192], BF16))
    xf4 = p4_scope.enter_context(nc.sbuf_tensor("xf4", [128, 2 * 512], F32))
    ft4 = p4_scope.enter_context(nc.sbuf_tensor("ft4", [128, 2 * 512], BF16))
    ostg = p4_scope.enter_context(nc.sbuf_tensor("ostg", [128, 2 * 512], F32))
    af4 = p4_scope.enter_context(nc.sbuf_tensor("af4", [128, 2 * 512], BF16))

    wo_slot_ap = [WBUF[:, 0:8192], wo2[:, :]]
    wo_free = [wslot_free[0], wslot_free[1]]
    EW('g', wslot_free[0])
    EW('g', wslot_free[1])
    for oc in range(32):
        slot = oc % 2
        EW('g', wo_free[slot])
        tl = EI('g', lambda g, oc=oc, d=wo_slot_ap[slot]: g.dma_start(
            d, w_outT[oc, :, :, :].rearrange("i ic o -> i (ic o)")), DQ, 16)
        EW('q', tl)
        tcache = EI('q', lambda q, oc=oc, d=wo_slot_ap[slot]: q.dma_start(woutbf[oc, :, :], d), SQ, 16)
        wo_free[slot] = tcache

    xf_free = [None, None]
    ft_free4 = [None, None]
    ostg_free = [None, None]
    last_out = None
    for tg in range(4):
        EW('q', now(PE))
        tgn = None
        for cc in range(64):
            tgn = EI('q', lambda q, cc=cc, tg=tg: q.dma_start(
                ARB[:, cc * 512:(cc + 1) * 512],
                gn_dram[cc, :, 4 * tg: 4 * tg + 4, :].rearrange("c z l -> c (z l)")), SQ, 16)
        for oc in range(32):
            slot = oc % 2
            wos = wo_slot_ap[slot]
            EW('q', wo_free[slot])
            twl = EI('q', lambda q, oc=oc, d=wos: q.dma_start(d, woutbf[oc, :, :]), SQ, 16)
            EW('q', ft_free4[slot])
            tfl = EI('q', lambda q, oc=oc, tg=tg, sl=slot: q.dma_start(
                ft4[:, sl * 512:(sl + 1) * 512], ffn_dram[oc, :, tg, :]), SQ, 16)
            EW('q', xf_free[slot])
            txl = EI('q', lambda q, oc=oc, tg=tg, sl=slot: q.dma_start(
                xf4[:, sl * 512:(sl + 1) * 512], x_fm[oc * 128:(oc + 1) * 128, tg * 512:(tg + 1) * 512]), SQ, 16)
            pp = [p512a, p512b][slot]
            EW('t', p512_free[slot])
            EW('t', twl)
            EW('t', tgn)
            for ic in range(64):
                EI('t', lambda t, o=pp[:, :], w=wos, ic=ic: t.matmul(
                    o, w[:, ic * 128:(ic + 1) * 128], ARB[:, ic * 512:(ic + 1) * 512],
                    start=(ic == 0), stop=(ic == 63)), PE)
            tmm = now(PE)
            wo_free[slot] = tmm
            EW('v', tmm)
            EW('v', tfl)
            EW('v', txl)
            tv1 = EI('v', lambda v, pp=pp, sl=slot: v.tensor_add(
                af4[:, sl * 512:(sl + 1) * 512], pp[:, :], ft4[:, sl * 512:(sl + 1) * 512]), DV)
            p512_free[slot] = tv1
            ft_free4[slot] = tv1
            EW('v', ostg_free[slot])
            tv2 = EI('v', lambda v, sl=slot: v.tensor_add(
                ostg[:, sl * 512:(sl + 1) * 512], af4[:, sl * 512:(sl + 1) * 512],
                xf4[:, sl * 512:(sl + 1) * 512]), DV)
            xf_free[slot] = tv2
            EW('q', tv2)
            tq = EI('q', lambda q, oc=oc, tg=tg, sl=slot: q.dma_start(
                out_fm[oc * 128:(oc + 1) * 128, tg * 512:(tg + 1) * 512],
                ostg[:, sl * 512:(sl + 1) * 512]), SQ, 16)
            ostg_free[slot] = tq
            last_out = tq
    EW('q', last_out)
    p4_scope.close()

    with nc.Block() as block:
        @block.tensor
        def _(t):
            for ek, fn in PROG:
                if ek == 't':
                    fn(t)

        @block.vector
        def _(v):
            for ek, fn in PROG:
                if ek == 'v':
                    fn(v)

        @block.scalar
        def _(s):
            for ek, fn in PROG:
                if ek == 's':
                    fn(s)

        @block.gpsimd
        def _(g):
            for ek, fn in PROG:
                if ek == 'g':
                    fn(g)

        @block.sync
        def _(q):
            for ek, fn in PROG:
                if ek == 'q':
                    fn(q)

    st.close()
    return nc


def _host_prep(inputs):
    x = np.asarray(inputs["L0_attn_x"], np.float32)
    w_in = np.asarray(inputs["L0_attn_in_proj_w"], np.float32)
    conv_w = np.asarray(inputs["L0_attn_conv_w"], np.float32)
    conv_b = np.asarray(inputs["L0_attn_conv_b"], np.float32)
    dt_bias = np.asarray(inputs["L0_attn_dt_bias"], np.float32)
    A_log = np.asarray(inputs["L0_attn_A_log"], np.float32)
    D_ = np.asarray(inputs["L0_attn_D"], np.float32)
    norm_w = np.asarray(inputs["L0_attn_norm_w"], np.float32)
    w_out = np.asarray(inputs["L0_attn_out_proj_w"], np.float32)
    gate = np.asarray(inputs["L0_ffn_gate"], np.float32)
    kernel_ = np.asarray(inputs["L0_ffn_kernel"], np.float32)
    bias = np.asarray(inputs["L0_ffn_bias"], np.float32)
    lora_A = np.asarray(inputs["L0_ffn_lora_A"], np.float32)
    lora_B = np.asarray(inputs["L0_ffn_lora_B"], np.float32)
    lora_As = np.asarray(inputs["L0_ffn_lora_A_shared"], np.float32)
    lora_Bs = np.asarray(inputs["L0_ffn_lora_B_shared"], np.float32)

    shared = {}
    wt = np.empty((NCHUNK, 128, DC, 128), np.float32)
    for ci in range(NCHUNK):
        r0 = _chunk_rows(ci)
        blk = w_in[r0:r0 + 128, :]
        wt[ci] = blk.reshape(128, DC, 128).transpose(2, 1, 0)
    shared["w_inT"] = np.ascontiguousarray(wt)
    cd = np.zeros((80, 128, 4, 128), np.float32)
    cb = np.empty((128, 80), np.float32)
    for j in range(80):
        chs = _conv_ch_of_chunk(j)
        for k in range(4):
            cd[j, np.arange(128), k, np.arange(128)] = conv_w[chs, 0, k]
        cb[:, j] = conv_b[chs]
    shared["convdiag"] = cd
    shared["convb"] = np.ascontiguousarray(cb)
    shared["w_outT"] = np.ascontiguousarray(
        w_out.reshape(32, 128, 64, 128).transpose(0, 3, 2, 1))
    shared["kern"] = np.ascontiguousarray(
        kernel_.reshape(32, 128, 32, 128).transpose(2, 1, 0, 3))
    la = np.ascontiguousarray(
        lora_A.transpose(2, 0, 1).reshape(32, 128, 16, 32).transpose(2, 1, 0, 3).reshape(16, 128, 1024))
    shared["loraA"] = la
    shared["loraAs"] = np.ascontiguousarray(
        lora_As.transpose(1, 0).reshape(32, 128, 32).transpose(1, 0, 2).reshape(128, 1024))
    shared["loraB"] = np.ascontiguousarray(lora_B.transpose(0, 2, 1))
    shared["loraBs"] = np.ascontiguousarray(lora_Bs.T)
    shared["drow"] = np.ascontiguousarray(D_.reshape(1, 128))
    shared["alog"] = np.ascontiguousarray(A_log.reshape(128, 1))
    shared["dtb"] = np.ascontiguousarray(dt_bias.reshape(128, 1))
    shared["normw"] = np.ascontiguousarray(norm_w.reshape(64, 128).T)
    shared["ffnb"] = np.ascontiguousarray(bias.reshape(32, 128).T)
    shared["c_ibf"] = np.eye(128, dtype=BF)
    sidx = np.arange(128)
    tril = (sidx[:, None] <= sidx[None, :]).astype(BF)
    shared["c_tril"] = tril
    shared["c_ones"] = np.ones((128, 128), BF)
    nex8 = np.zeros((8, 1024), np.float32)
    for hh in range(8):
        nex8[hh, hh * 128:(hh + 1) * 128] = -1.0
    shared["c_negex8"] = nex8.astype(BF)
    selz8 = np.zeros((8, 1024), np.float32)
    for z in range(8):
        selz8[z, z * 128:(z + 1) * 128] = 1.0
    shared["c_selz8"] = selz8.astype(BF)

    in_maps = []
    for c in range(NCORE):
        k, j = c // 2, c % 2
        m = dict(shared)
        xtm = np.zeros((T + 3, D), np.float32)
        if j == 1:
            xtm[0:3] = x[k, 2045:2048]
        xtm[3:] = x[k, 2048 * j: 2048 * (j + 1)]
        m["x_tm"] = xtm
        m["x_fm"] = np.ascontiguousarray(x[k, 2048 * j: 2048 * (j + 1)].T)
        m["gaterow"] = np.ascontiguousarray(gate[c].reshape(1, 16))
        sb_ = np.zeros((128, 8), np.float32)
        if j == 1:
            sb_[:, c - 1] = 1.0
        m["selb"] = sb_
        cv = np.zeros((128, 8), np.float32)
        for _i, _v in enumerate([1e-5, 1.0, -1.0, -0.5, 1.0 / 1024]):
            cv[:, _i] = _v
        m["cvals"] = cv
        in_maps.append(m)
    return in_maps


def kernel(**inputs):
    in_maps = _host_prep(inputs)
    nc = build_nc()
    res = run_bass_kernel_spmd(nc, in_maps, core_ids=list(range(NCORE)))
    out = np.empty((4, 4096, 4096), np.float32)
    for c in range(NCORE):
        k, j = c // 2, c % 2
        out[k, 2048 * j: 2048 * (j + 1)] = res.results[c]["out_fm"].T
    return out
